# version 02
# baseline (speedup 1.0000x reference)
import contextlib
import numpy as np
import ml_dtypes
import concourse.bass as bass
import concourse.mybir as mybir
from concourse.bass_utils import run_bass_kernel_spmd

F32 = mybir.dt.float32
BF16 = mybir.dt.bfloat16
U8 = mybir.dt.uint8
AF = mybir.ActivationFunctionType
ALU = mybir.AluOpType
BF16NP = ml_dtypes.bfloat16

T = 2048
D = 4096
NT = T // 128
NDT = D // 128
DI = 11008
NOI = DI // 128
NKV = 8
NH = 32
EPS = 1e-5
GROUPS2 = [[0, 1], [2, 3], [4, 5], [6, 7]]
ISCALE = float(1.0 / np.sqrt(128.0))


class Buf:
    __slots__ = ("w", "r")
    def __init__(self):
        self.w = None
        self.r = []

class TK:
    def __init__(self, nc):
        self.nc = nc
        self.ctx = contextlib.ExitStack()
        self.sems = {}
        self.counts = {}
        self.satisfied = {}
        self.eng = {"pe": nc.tensor, "act": nc.scalar, "dve": nc.vector,
                    "pool": nc.gpsimd, "sync": nc.sync}
        for n in ("pe", "act", "dve", "pool"):
            self._new_sem(n)
        self.NSLOT = 8
        self.qrr = {}
        for q in ("q_sync", "q_pool", "q_act"):
            self.qrr[q] = 0
            for i in range(self.NSLOT):
                self._new_sem(f"{q}_{i}")
        self.ccn = 0

    def _new_sem(self, name):
        self.sems[name] = self.ctx.enter_context(self.nc.semaphore(name))
        self.counts[name] = 0


    def cc_sem(self):
        name = f"cc_{self.ccn}"
        self.ccn += 1
        self._new_sem(name)
        return name

    def wait(self, engine, toks):
        eng = self.eng[engine]
        need = {}
        for (s, v) in toks:
            if v <= 0 or self.satisfied.get((engine, s), 0) >= v:
                continue
            need[s] = max(need.get(s, 0), v)
        for s, v in need.items():
            eng.wait_ge(self.sems[s], v)
            self.satisfied[(engine, s)] = v

    def inc(self, inst, sem, by):
        inst.then_inc(self.sems[sem], by)
        self.counts[sem] += by
        return (sem, self.counts[sem])

    def barrier(self):
        toks = [(s, c) for s, c in self.counts.items()
                if c > 0 and not s.startswith("cc_")]
        for e in ("pe", "act", "dve", "pool", "sync"):
            self.wait(e, toks)

    def _pre(self, engine, reads, writes):
        toks = []
        for b in reads:
            if b.w: toks.append(b.w)
        for b in writes:
            if b.w: toks.append(b.w)
            toks.extend(b.r)
        self.wait(engine, toks)

    def _post(self, t, reads, writes):
        for b in writes:
            b.w = t; b.r = []
        for b in reads:
            b.r.append(t)
        return t

    def op(self, engine, fn, reads=(), writes=()):
        self._pre(engine, reads, writes)
        inst = fn(self.eng[engine])
        return self._post(self.inc(inst, engine, 1), reads, writes)

    def dma(self, q, out, in_, reads=(), writes=(), **kw):
        engine = {"q_sync": "sync", "q_pool": "pool", "q_act": "act"}[q]
        self._pre(engine, reads, writes)
        sem = f"{q}_{self.qrr[q]}"
        self.qrr[q] = (self.qrr[q] + 1) % self.NSLOT
        self.wait(engine, [(sem, self.counts[sem])])
        inst = self.eng[engine].dma_start(out=out, in_=in_, **kw)
        return self._post(self.inc(inst, sem, 16), reads, writes)

    def mmgroup(self, mms, reads=(), writes=()):
        self._pre("pe", reads, writes)
        inst = None
        for f in mms:
            inst = f(self.eng["pe"])
        return self._post(self.inc(inst, "pe", 1), reads, writes)

    def collective(self, kind, op, in_ap, out_ap, write_tok_buf):
        sem = self.cc_sem()
        inst = self.nc.gpsimd.collective_compute(
            kind, op, replica_groups=GROUPS2,
            ins=[in_ap.opt()], outs=[out_ap.opt()])
        inst.then_inc(self.sems[sem], 1)
        self.counts[sem] += 1
        write_tok_buf.w = (sem, 1)
        write_tok_buf.r = []
        return sem


class Ctx:
    def __init__(self, nc, A):
        self.nc, self.A = nc, A
        self.tr = TK(nc)
        self.stack = contextlib.ExitStack()
        sb = lambda n, s, d: self.stack.enter_context(nc.sbuf_tensor(n, s, d))
        self.s_ident = sb("c_ident", [128, 128], BF16)
        self.s_ones_col = sb("c_ones_col", [128, 1], BF16)
        self.s_ones_row = sb("c_ones_row", [1, 128], BF16)
        self.b_ident, self.b_ones_col, self.b_ones_row = Buf(), Buf(), Buf()
        self.tr.dma("q_sync", self.s_ident[:], A["ident"], writes=[self.b_ident])
        self.tr.op("pool", lambda e: e.memset(self.s_ones_col[:], 1.0), writes=[self.b_ones_col])
        self.tr.op("pool", lambda e: e.memset(self.s_ones_row[:], 1.0), writes=[self.b_ones_row])

    def alloc(self, name, shape, dtype, stack):
        return stack.enter_context(self.nc.sbuf_tensor(name, shape, dtype)), Buf()

    def palloc(self, name, shape, stack, dtype=F32):
        return stack.enter_context(self.nc.psum_tensor(name, shape, dtype)), Buf()



def emit_lnT(C, x_ap, s_xnT, b_xnT, tag):
    tr = C.tr
    with contextlib.ExitStack() as st:
        s_x, b_x, s_xnb, b_xnb = [], [], [], []
        for i in range(2):
            t, b = C.alloc(f"{tag}_x{i}", [128, D], F32, st); s_x.append(t); b_x.append(b)
            t, b = C.alloc(f"{tag}_n{i}", [128, D], BF16, st); s_xnb.append(t); b_xnb.append(b)
        s_junk, b_junk = C.alloc(f"{tag}_junk", [128, D], F32, st)
        s_st, b_st = C.alloc(f"{tag}_st", [128, 8], F32, st)
        p_tr, b_ptr = zip(*[C.palloc(f"{tag}_p{i}", [128, 128], st, BF16) for i in range(2)])
        for tt in range(NT):
            xb = tt % 2
            tr.dma("q_sync", s_x[xb][:], x_ap[tt*128:(tt+1)*128, :], writes=[b_x[xb]])
            tr.op("dve", lambda e: e.reduce_sum(s_st[:, 0:1], s_x[xb][:], axis=mybir.AxisListType.X),
                  reads=[b_x[xb]], writes=[b_st])
            tr.op("act", lambda e: e.activation(s_junk[:], s_x[xb][:], AF.Square, accum_out=s_st[:, 1:2]),
                  reads=[b_x[xb]], writes=[b_junk, b_st])
            tr.op("dve", lambda e: e.tensor_scalar(s_st[:, 2:3], s_st[:, 0:1], 1.0 / D, None, ALU.mult),
                  reads=[b_st], writes=[b_st])
            tr.op("dve", lambda e: e.tensor_scalar(s_st[:, 3:4], s_st[:, 1:2], 1.0 / D, float(EPS), ALU.mult, ALU.add),
                  reads=[b_st], writes=[b_st])
            tr.op("dve", lambda e: e.scalar_tensor_tensor(s_st[:, 4:5], s_st[:, 2:3], 1.0, s_st[:, 2:3], ALU.mult, ALU.mult),
                  reads=[b_st], writes=[b_st])
            tr.op("dve", lambda e: e.tensor_sub(s_st[:, 3:4], s_st[:, 3:4], s_st[:, 4:5]),
                  reads=[b_st], writes=[b_st])
            tr.op("act", lambda e: e.activation(s_st[:, 5:6], s_st[:, 3:4], AF.Ln),
                  reads=[b_st], writes=[b_st])
            tr.op("act", lambda e: e.activation(s_st[:, 6:7], s_st[:, 5:6], AF.Exp, scale=-0.5),
                  reads=[b_st], writes=[b_st])
            tr.op("dve", lambda e: e.scalar_tensor_tensor(s_st[:, 7:8], s_st[:, 2:3], -1.0, s_st[:, 6:7], ALU.mult, ALU.mult),
                  reads=[b_st], writes=[b_st])
            tr.op("dve", lambda e: e.tensor_scalar(s_xnb[xb][:], s_x[xb][:], s_st[:, 6:7], s_st[:, 7:8], ALU.mult, ALU.add),
                  reads=[b_x[xb], b_st], writes=[b_xnb[xb]])
            for dt in range(NDT):
                pb = dt % 2
                tr.op("pe", lambda e: e.transpose(p_tr[pb][:], s_xnb[xb][:, dt*128:(dt+1)*128], C.s_ident[:]),
                      reads=[b_xnb[xb], C.b_ident], writes=[b_ptr[pb]])
                tr.op("dve", lambda e: e.tensor_copy(s_xnT[:, dt, tt*128:(tt+1)*128], p_tr[pb][:]),
                      reads=[b_ptr[pb]], writes=[b_xnT])
    tr.barrier()


def emit_layer0(C, x_ap, x1_ap):
    nc, tr, A = C.nc, C.tr, C.A
    lay = contextlib.ExitStack()
    s_sg86, b_sg86 = C.alloc("l0_sg86", [86, T], BF16, lay)
    b_cc0 = Buf()

    era = contextlib.ExitStack()
    s_xnT, b_xnT = C.alloc("l0_xnT", [128, NDT, T], BF16, era)
    s_bias, b_bias = C.alloc("l0_bias", [128, 48], F32, era)
    tr.dma("q_sync", s_bias[:, 0:32], A["bq_c"], writes=[b_bias])
    tr.dma("q_sync", s_bias[:, 32:40], A["bk_c"], writes=[b_bias])
    tr.dma("q_sync", s_bias[:, 40:48], A["bv_c"], writes=[b_bias])
    s_sg, b_sg = C.alloc("l0_sg", [32, T], BF16, era)

    emit_lnT(C, x_ap, s_xnT, b_xnT, "l0ln")
    if DBG:
        A["d_xnT"] = _scratch(nc, "d_xnT", (128, NDT, T), BF16)
        tr.dma("q_act", A["d_xnT"], s_xnT[:], reads=[b_xnT])
        tr.barrier()

    with contextlib.ExitStack() as st:
        s_w, b_w = zip(*[C.alloc(f"l0kv_w{i}", [128, NDT, 128], BF16, st) for i in range(2)])
        s_t1, b_t1 = C.alloc("l0kv_t1", [128, 512], F32, st)
        s_t2, b_t2 = C.alloc("l0kv_t2", [128, 512], F32, st)
        s_t3, b_t3 = C.alloc("l0kv_t3", [128, 512], F32, st)
        s_kT, b_kT = C.alloc("l0_kT", [128, T], BF16, st)
        s_vT, b_vT = C.alloc("l0_vT", [128, T], BF16, st)
        s_kx, b_kx = C.alloc("l0_kx", [128, NT, 128], BF16, st)
        s_vx, b_vx = C.alloc("l0_vx", [128, NT, 128], BF16, st)
        s_zero, b_zero = C.alloc("l0kv_zero", [128, 512], F32, st)
        tr.op("pool", lambda e: e.memset(s_zero[:], 0.0), writes=[b_zero])
        p_y, b_py = zip(*[C.palloc(f"l0kv_p{i}", [128, 512], st) for i in range(2)])
        p_t2, b_pt2 = zip(*[C.palloc(f"l0kv_pt{i}", [128, 128], st, BF16) for i in range(2)])
        p_kv, b_pkv = C.palloc("l0_pkv", [128, 1024], st)
        p_ks, b_pks = C.palloc("l0_pks", [1, 1024], st)

        for kh in range(NKV):
            for m, wname, bcol, s_dst, b_dst, do_phi in (
                    (0, "wkT", 32 + kh, s_kT, b_kT, True),
                    (1, "wvT", 40 + kh, s_vT, b_vT, False)):
                wb = m
                tr.dma("q_pool", s_w[wb][:],
                       A[wname].rearrange("(dt p) o -> p dt o", p=128)[:, :, kh*128:(kh+1)*128],
                       writes=[b_w[wb]])
                for tc in range(4):
                    pb = tc % 2
                    sl = slice(tc*512, (tc+1)*512)
                    tr.mmgroup([(lambda e, dt=dt, pb=pb, wb=wb, sl=sl:
                                 e.matmul(p_y[pb][:], s_w[wb][:, dt, :], s_xnT[:, dt, sl],
                                          start=(dt == 0), stop=(dt == NDT-1)))
                                for dt in range(NDT)],
                               reads=[b_w[wb], b_xnT], writes=[b_py[pb]])
                    if do_phi:
                        tr.op("dve", lambda e: e.tensor_scalar(s_t2[:], p_y[pb][:], s_bias[:, bcol:bcol+1], 0.0, ALU.add, ALU.min),
                              reads=[b_py[pb], b_bias], writes=[b_t2])
                        tr.op("act", lambda e: e.activation(s_t3[:], s_t2[:], AF.Exp),
                              reads=[b_t2], writes=[b_t3])
                        tr.op("dve", lambda e: e.scalar_tensor_tensor(s_t1[:], p_y[pb][:], s_bias[:, bcol:bcol+1], s_zero[:], ALU.add, ALU.max),
                              reads=[b_py[pb], b_bias, b_zero], writes=[b_t1])
                        tr.op("dve", lambda e: e.tensor_add(s_dst[:, sl], s_t1[:], s_t3[:]),
                              reads=[b_t1, b_t3], writes=[b_dst])
                    else:
                        tr.op("dve", lambda e: e.tensor_scalar(s_dst[:, sl], p_y[pb][:], s_bias[:, bcol:bcol+1], None, ALU.add),
                              reads=[b_py[pb], b_bias], writes=[b_dst])
            for s_src, b_src, s_d, b_d in ((s_kT, b_kT, s_kx, b_kx), (s_vT, b_vT, s_vx, b_vx)):
                for tt in range(NT):
                    pb = tt % 2
                    tr.op("pe", lambda e: e.transpose(p_t2[pb][:], s_src[:, tt*128:(tt+1)*128], C.s_ident[:]),
                          reads=[b_src, C.b_ident], writes=[b_pt2[pb]])
                    tr.op("dve", lambda e: e.tensor_copy(s_d[:, tt, :], p_t2[pb][:]),
                          reads=[b_pt2[pb]], writes=[b_d])
            sl_kv = slice(kh*128, (kh+1)*128)
            tr.mmgroup([(lambda e, tt=tt, sl_kv=sl_kv:
                         e.matmul(p_kv[:, sl_kv], s_kx[:, tt, :], s_vx[:, tt, :],
                                  start=(tt == 0), stop=(tt == NT-1)))
                        for tt in range(NT)],
                       reads=[b_kx, b_vx], writes=[b_pkv])
            tr.mmgroup([(lambda e, tt=tt, sl_kv=sl_kv:
                         e.matmul(p_ks[:, sl_kv], C.s_ones_col[:], s_kx[:, tt, :],
                                  start=(tt == 0), stop=(tt == NT-1)))
                        for tt in range(NT)],
                       reads=[b_kx, C.b_ones_col], writes=[b_pks])
        s_kv, b_kv = C.alloc("l0_skv", [128, 1024], F32, st)
        s_ks, b_ks = C.alloc("l0_sks", [1, 1024], F32, st)
        tr.op("act", lambda e: e.copy(s_kv[:], p_kv[:]), reads=[b_pkv], writes=[b_kv])
        tr.op("act", lambda e: e.copy(s_ks[:], p_ks[:]), reads=[b_pks], writes=[b_ks])
        b_ccin = Buf()
        tr.dma("q_sync", A["d_cc0in"][0:128, :], s_kv[:], reads=[b_kv], writes=[b_ccin])
        tr.dma("q_sync", A["d_cc0in"][128:129, :], s_ks[:], reads=[b_ks], writes=[b_ccin])
        tr.wait("pool", [b_ccin.w])
        tr.collective("AllReduce", ALU.add, A["d_cc0in"], A["d_cc0out"], b_cc0)
    tr.barrier()

    with contextlib.ExitStack() as st:
        s_stair32, b_st32 = C.alloc("l0_st32", [128, 63], BF16, st)
        tr.dma("q_sync", s_stair32[:], A["stair32"], writes=[b_st32])
        p_sg, b_psg = C.palloc("l0_psg", [32, T], st)
        for tc in range(4):
            sl = slice(tc*512, (tc+1)*512)
            tr.mmgroup([(lambda e, dt=dt, sl=sl:
                         e.matmul(p_sg[:, sl], s_stair32[:, 31-dt:63-dt], s_xnT[:, dt, sl],
                                  start=(dt == 0), stop=(dt == NDT-1)))
                        for dt in range(NDT)],
                       reads=[b_st32, b_xnT], writes=[b_psg])
        tr.op("act", lambda e: e.copy(s_sg[:], p_sg[:]), reads=[b_psg], writes=[b_sg])
        s_z8, b_z8 = C.alloc("l0_z8", [32, DI // 2], U8, st)
        s_zn, b_zn = C.alloc("l0_zn", [32, DI], U8, st)
        s_sf, b_sf = C.alloc("l0_sf", [32, DI // 2], F32, st)
        s_zs, b_zs = C.alloc("l0_zs", [32, DI], BF16, st)
        for nm in ("g", "u"):
            tr.dma("q_sync", s_z8[:], A[f"z{nm}8"], writes=[b_z8])
            tr.op("dve", lambda e: e.tensor_single_scalar(s_zn[:, 0:DI//2], s_z8[:], 15, ALU.bitwise_and),
                  reads=[b_z8], writes=[b_zn])
            tr.op("dve", lambda e: e.tensor_single_scalar(s_zn[:, DI//2:DI], s_z8[:], 4, ALU.logical_shift_right),
                  reads=[b_z8], writes=[b_zn])
            zv = s_zs[:].rearrange("p (n two) -> p n two", two=2)
            for par in (0, 1):
                snm = f"s{nm}ef" if par == 0 else f"s{nm}of"
                off = 0 if par == 0 else DI // 2
                tr.dma("q_sync", s_sf[:], A[snm], writes=[b_sf])
                tr.op("dve", lambda e, zv=zv, off=off, par=par: e.scalar_tensor_tensor(
                        zv[:, :, par:par+1],
                        s_zn[:, off:off+DI//2].rearrange("p (n o) -> p n o", o=1), -1.0,
                        s_sf[:].rearrange("p (n o) -> p n o", o=1), ALU.mult, ALU.mult),
                      reads=[b_zn, b_sf], writes=[b_zs])
            tr.dma("q_act", A[f"d_zs{nm}"], s_zs[:], reads=[b_zs])
    tr.barrier()

    with contextlib.ExitStack() as st:
        s_stair86, b_st86 = C.alloc("l0_st86", [128, 171], BF16, st)
        tr.dma("q_sync", s_stair86[:], A["stair86"], writes=[b_st86])
        bufs = {}
        for nm in ("g", "u"):
            e = {}
            t_, b_ = C.alloc(f"l0q8{nm}", [128, NDT, 64], U8, st)
            e["q8"], e["b_q8"] = (t_, t_), (b_, b_)
            e["nib"], e["b_nib"] = C.alloc(f"l0nib{nm}", [128, NDT, 128], U8, st)
            if nm == "g":
                e["wt"], e["b_wt"] = zip(*[C.alloc(f"l0wt{nm}{i}", [128, NDT, 128], BF16, st) for i in range(2)])
            else:
                t_, b_ = C.alloc(f"l0wt{nm}", [128, NDT, 128], BF16, st)
                e["wt"], e["b_wt"] = (t_, t_), (b_, b_)
            e["sc"], e["b_sc"] = C.alloc(f"l0sc{nm}", [128, 2, NDT, 64], BF16, st)
            e["zss"], e["b_zss"] = zip(*[C.alloc(f"l0zss{nm}{i}", [32, 128], BF16, st) for i in range(2)])
            e["p"], e["b_p"] = zip(*[C.palloc(f"l0p{nm}{i}", [128, 512], st) for i in range(2)])
            bufs[nm] = e
        p_s86, b_ps86 = C.palloc("l0_ps86", [86, T], st)
        s_silu, b_silu = C.alloc("l0_silu", [128, 512], F32, st)
        s_h, b_h = zip(*[C.alloc(f"l0_h{i}", [128, T], BF16, st) for i in range(2)])

        qsrc = {"g": "qg", "u": "qu"}
        ssrc = {"g": ("sge", "sgo"), "u": ("sue", "suo")}
        for ot in range(NOI):
            ob = ot % 2
            for nm in ("g", "u"):
                e = bufs[nm]
                tr.dma("q_sync", e["q8"][ob][:],
                       A[qsrc[nm]][ot].rearrange("(dt p) b -> p dt b", p=128),
                       writes=[e["b_q8"][ob]])
                se, so = ssrc[nm]
                tr.dma("q_pool", e["sc"][:, 0], A[se][ot].partition_broadcast(128), writes=[e["b_sc"]])
                tr.dma("q_pool", e["sc"][:, 1], A[so][ot].partition_broadcast(128), writes=[e["b_sc"]])
                tr.dma("q_sync", e["zss"][ob][:], A[f"d_zs{nm}"][:, ot*128:(ot+1)*128], writes=[e["b_zss"][ob]])
                tr.op("dve", lambda e2, e=e, ob=ob: e2.tensor_single_scalar(e["nib"][:, :, 0:64], e["q8"][ob][:], 15, ALU.bitwise_and),
                      reads=[e["b_q8"][ob]], writes=[e["b_nib"]])
                tr.op("dve", lambda e2, e=e, ob=ob: e2.tensor_single_scalar(e["nib"][:, :, 64:128], e["q8"][ob][:], 4, ALU.logical_shift_right),
                      reads=[e["b_q8"][ob]], writes=[e["b_nib"]])
                wv = e["wt"][ob][:].rearrange("p dt (n two) -> p dt n two", two=2)
                for par, bo in ((0, 0), (1, 64)):
                    tr.op("dve", lambda e2, e=e, wv=wv, par=par, bo=bo: e2.tensor_tensor(
                            wv[:, :, :, par:par+1],
                            e["nib"][:, :, bo:bo+64].rearrange("p dt (n o) -> p dt n o", o=1),
                            e["sc"][:, par].rearrange("p dt (n o) -> p dt n o", o=1), ALU.mult),
                          reads=[e["b_nib"], e["b_sc"]], writes=[e["b_wt"][ob]])
            for tc in range(4):
                sl = slice(tc*512, (tc+1)*512)
                for nm in ("g", "u"):
                    e = bufs[nm]
                    mms = [(lambda e2, e=e, dt=dt, ob=ob, sl=sl:
                            e2.matmul(e["p"][ob][:], e["wt"][ob][:, dt, :], s_xnT[:, dt, sl],
                                      start=(dt == 0), stop=False))
                           for dt in range(NDT)]
                    mms.append(lambda e2, e=e, ob=ob, sl=sl:
                               e2.matmul(e["p"][ob][:], e["zss"][ob][:], s_sg[:, sl],
                                         start=False, stop=True))
                    tr.mmgroup(mms, reads=[e["b_wt"][ob], b_xnT, e["b_zss"][ob], b_sg],
                               writes=[e["b_p"][ob]])
                tr.op("act", lambda e2: e2.activation(s_silu[:], bufs["g"]["p"][ob][:], AF.Silu),
                      reads=[bufs["g"]["b_p"][ob]], writes=[b_silu])
                tr.op("dve", lambda e2: e2.tensor_tensor(s_h[ob][:, sl], s_silu[:], bufs["u"]["p"][ob][:], ALU.mult),
                      reads=[b_silu, bufs["u"]["b_p"][ob]], writes=[b_h[ob]])
                tr.mmgroup([(lambda e2, ot=ot, sl=sl, ob=ob:
                             e2.matmul(p_s86[:, sl], s_stair86[:, 85-ot:171-ot], s_h[ob][:, sl],
                                       start=(ot == 0), stop=(ot == NOI-1)))],
                           reads=[b_st86, b_h[ob]], writes=[b_ps86])
            tr.dma("q_act", A["d_h"][ot], s_h[ob][:], reads=[b_h[ob]])
        tr.op("act", lambda e: e.copy(s_sg86[:], p_s86[:]), reads=[b_ps86], writes=[b_sg86])
    tr.barrier()

    with contextlib.ExitStack() as st:
        s_w, b_w = zip(*[C.alloc(f"l0q_w{i}", [128, NDT, 128], BF16, st) for i in range(2)])
        s_t1, b_t1 = C.alloc("l0q_t1", [128, 512], F32, st)
        s_t2, b_t2 = C.alloc("l0q_t2", [128, 512], F32, st)
        s_t3, b_t3 = C.alloc("l0q_t3", [128, 512], F32, st)
        s_phi, b_phi = zip(*[C.alloc(f"l0q_f{i}", [128, T], BF16, st) for i in range(2)])
        s_zero, b_zero = C.alloc("l0q_zero", [128, 512], F32, st)
        tr.op("pool", lambda e: e.memset(s_zero[:], 0.0), writes=[b_zero])
        p_y, b_py = zip(*[C.palloc(f"l0q_p{i}", [128, 512], st) for i in range(2)])
        for ot in range(NDT):
            wb = ot % 2
            tr.dma("q_pool", s_w[wb][:],
                   A["wqT"].rearrange("(dt p) o -> p dt o", p=128)[:, :, ot*128:(ot+1)*128],
                   writes=[b_w[wb]])
            for tc in range(4):
                pb = tc % 2
                sl = slice(tc*512, (tc+1)*512)
                tr.mmgroup([(lambda e, dt=dt, pb=pb, wb=wb, sl=sl:
                             e.matmul(p_y[pb][:], s_w[wb][:, dt, :], s_xnT[:, dt, sl],
                                      start=(dt == 0), stop=(dt == NDT-1)))
                            for dt in range(NDT)],
                           reads=[b_w[wb], b_xnT], writes=[b_py[pb]])
                tr.op("dve", lambda e: e.tensor_scalar(s_t2[:], p_y[pb][:], s_bias[:, ot:ot+1], 0.0, ALU.add, ALU.min),
                      reads=[b_py[pb], b_bias], writes=[b_t2])
                tr.op("act", lambda e: e.activation(s_t3[:], s_t2[:], AF.Exp),
                      reads=[b_t2], writes=[b_t3])
                tr.op("dve", lambda e: e.scalar_tensor_tensor(s_t1[:], p_y[pb][:], s_bias[:, ot:ot+1], s_zero[:], ALU.add, ALU.max),
                      reads=[b_py[pb], b_bias, b_zero], writes=[b_t1])
                tr.op("dve", lambda e: e.tensor_add(s_phi[wb][:, sl], s_t1[:], s_t3[:]),
                      reads=[b_t1, b_t3], writes=[b_phi[wb]])
            tr.dma("q_act", A["d_phiq"][ot], s_phi[wb][:], reads=[b_phi[wb]])
    tr.barrier()
    era.close()

    with contextlib.ExitStack() as st:
        s_zd8, b_zd8 = C.alloc("l0_zd8", [86, 2048], U8, st)
        s_zdn, b_zdn = C.alloc("l0_zdn", [86, 4096], U8, st)
        s_sdf, b_sdf = C.alloc("l0_sdf", [86, 2048], F32, st)
        s_zsd, b_zsd = C.alloc("l0_zsd", [86, 4096], BF16, st)
        tr.dma("q_sync", s_zd8[:], A["zd8"], writes=[b_zd8])
        tr.op("dve", lambda e: e.tensor_single_scalar(s_zdn[:, 0:2048], s_zd8[:], 15, ALU.bitwise_and),
              reads=[b_zd8], writes=[b_zdn])
        tr.op("dve", lambda e: e.tensor_single_scalar(s_zdn[:, 2048:4096], s_zd8[:], 4, ALU.logical_shift_right),
              reads=[b_zd8], writes=[b_zdn])
        zv = s_zsd[:].rearrange("p (n two) -> p n two", two=2)
        for par in (0, 1):
            snm = "sdef" if par == 0 else "sdof"
            off = 0 if par == 0 else 2048
            tr.dma("q_sync", s_sdf[:], A[snm], writes=[b_sdf])
            tr.op("dve", lambda e, par=par, off=off: e.scalar_tensor_tensor(
                    zv[:, :, par:par+1],
                    s_zdn[:, off:off+2048].rearrange("p (n o) -> p n o", o=1), -1.0,
                    s_sdf[:].rearrange("p (n o) -> p n o", o=1), ALU.mult, ALU.mult),
                  reads=[b_zdn, b_sdf], writes=[b_zsd])
        s_wd, b_wd = C.alloc("l0_wd", [128, NOI, 512], BF16, st)
        s_qd8, b_qd8 = zip(*[C.alloc(f"l0_qd8{i}", [128, 256], U8, st) for i in range(2)])
        s_dn, b_dn = C.alloc("l0_dn", [128, 512], U8, st)
        s_scd, b_scd = C.alloc("l0_scd", [128, 512], BF16, st)
        s_ht, b_ht = zip(*[C.alloc(f"l0_ht{i}", [128, NOI, 128], BF16, st) for i in range(2)])
        s_gs, b_gs = zip(*[C.alloc(f"l0_gs{i}", [128, 512], BF16, st) for i in range(2)])
        p_d, b_pd = zip(*[C.palloc(f"l0_pd{i}", [128, 512], st) for i in range(2)])
        for cc in range(8):
            csl = slice(cc*512, (cc+1)*512)
            for ot in range(NOI):
                qb = ot % 2
                tr.dma("q_sync", s_qd8[qb][:], A["qd"][cc, ot*128:(ot+1)*128, :], writes=[b_qd8[qb]])
                tr.dma("q_pool", s_scd[:].rearrange("p (o n) -> p o n", o=1),
                       A["sd_b"][cc, ot:ot+1, :].partition_broadcast(128), writes=[b_scd])
                tr.op("dve", lambda e: e.tensor_single_scalar(s_dn[:, 0:256], s_qd8[qb][:], 15, ALU.bitwise_and),
                      reads=[b_qd8[qb]], writes=[b_dn])
                tr.op("dve", lambda e: e.tensor_single_scalar(s_dn[:, 256:512], s_qd8[qb][:], 4, ALU.logical_shift_right),
                      reads=[b_qd8[qb]], writes=[b_dn])
                wv = s_wd[:, ot, :].rearrange("p (n two) -> p n two", two=2)
                sv = s_scd[:].rearrange("p (n two) -> p n two", two=2)
                for par, bo in ((0, 0), (1, 256)):
                    tr.op("dve", lambda e, wv=wv, sv=sv, par=par, bo=bo: e.tensor_tensor(
                            wv[:, :, par:par+1],
                            s_dn[:, bo:bo+256].rearrange("p (n o) -> p n o", o=1),
                            sv[:, :, par:par+1], ALU.mult),
                          reads=[b_dn, b_scd], writes=[b_wd])
                if DBG and cc == 0 and ot == 0:
                    dbg_dn = nc.dram_tensor("d_dndump", [128, 512], U8, kind="ExternalOutput").ap()
                    tr.dma("q_act", dbg_dn, s_dn[:], reads=[b_dn])
                    dbg_scd = nc.dram_tensor("d_scddump", [128, 512], BF16, kind="ExternalOutput").ap()
                    tr.dma("q_act", dbg_scd, s_scd[:], reads=[b_scd])
            if DBG and cc == 0:
                dbg_wd = nc.dram_tensor("d_wddump", [128, NOI, 512], BF16, kind="ExternalOutput").ap()
                tr.dma("q_act", dbg_wd, s_wd[:], reads=[b_wd])
                dbg_sg86 = nc.dram_tensor("d_sg86dump", [86, T], BF16, kind="ExternalOutput").ap()
                tr.dma("q_act", dbg_sg86, s_sg86[:], reads=[b_sg86])
                dbg_zsd = nc.dram_tensor("d_zsddump", [86, 4096], BF16, kind="ExternalOutput").ap()
                tr.dma("q_act", dbg_zsd, s_zsd[:], reads=[b_zsd])
            for tt in range(NT):
                hb = tt % 2
                tr.dma("q_sync", s_ht[hb][:],
                       A["d_h"][:, :, tt*128:(tt+1)*128].rearrange("ot p t -> p ot t"),
                       writes=[b_ht[hb]])
                pb = tt % 2
                mms = [(lambda e, ot=ot, hb=hb, pb=pb:
                        e.matmul(p_d[pb][:], s_ht[hb][:, ot, :], s_wd[:, ot, :],
                                 start=(ot == 0), stop=False))
                       for ot in range(NOI)]
                mms.append(lambda e, pb=pb, tt=tt, csl=csl:
                           e.matmul(p_d[pb][:], s_sg86[:, tt*128:(tt+1)*128], s_zsd[:, csl],
                                    start=False, stop=True))
                tr.mmgroup(mms, reads=[b_ht[hb], b_wd, b_sg86, b_zsd], writes=[b_pd[pb]])
                tr.op("act", lambda e: e.activation(s_gs[pb][:], p_d[pb][:], AF.Silu),
                      reads=[b_pd[pb]], writes=[b_gs[pb]])
                tr.dma("q_act", A["d_gs0"][tt, :, csl], s_gs[pb][:], reads=[b_gs[pb]])
    tr.barrier()

    s_aT, b_aT = C.alloc("l0_aT", [128, NH, T], BF16, lay)
    with contextlib.ExitStack() as st:
        s_kvb, b_kvb = C.alloc("l0_kvb", [128, 1024], BF16, st)
        s_rk, b_rk = C.alloc("l0_rk", [128, 8], F32, st)
        tr.wait("pool", [b_cc0.w])
        if DBG:
            dbg_cc = nc.dram_tensor("d_ccdump", [129, 1024], F32, kind="ExternalOutput").ap()
            tr.dma("q_sync", dbg_cc, A["d_cc0out"], reads=[b_cc0])
        tr.dma("q_pool", s_kvb[:], A["d_cc0out"][0:128, :], reads=[b_cc0], writes=[b_kvb])
        with C.nc.allow_non_contiguous_dma("tiny ksum transpose load"):
            tr.dma("q_sync", s_rk[:],
                   A["d_cc0out"][128:129, :].rearrange("o (h d) -> d (o h)", h=8),
                   reads=[b_cc0], writes=[b_rk])
        tr.op("dve", lambda e: e.tensor_scalar(s_rk[:], s_rk[:], 1e-10, None, ALU.add),
              reads=[b_rk], writes=[b_rk])
        tr.op("dve", lambda e: e.reciprocal(s_rk[:], s_rk[:]), reads=[b_rk], writes=[b_rk])
        s_ph, b_ph = zip(*[C.alloc(f"l0_ph{i}", [128, T], BF16, st) for i in range(2)])
        p_a, b_pa = zip(*[C.palloc(f"l0_pa{i}", [128, 512], st) for i in range(2)])
        for h in range(NH):
            hb = h % 2
            kvh = h // 4
            tr.dma("q_sync", s_ph[hb][:], A["d_phiq"][h], writes=[b_ph[hb]])
            for tc in range(4):
                pb = tc % 2
                sl = slice(tc*512, (tc+1)*512)
                tr.mmgroup([(lambda e, pb=pb, hb=hb, kvh=kvh, sl=sl:
                             e.matmul(p_a[pb][:], s_kvb[:, kvh*128:(kvh+1)*128], s_ph[hb][:, sl],
                                      start=True, stop=True))],
                           reads=[b_kvb, b_ph[hb]], writes=[b_pa[pb]])
                tr.op("dve", lambda e, pb=pb, h=h, sl=sl, kvh=kvh: e.tensor_scalar(
                        s_aT[:, h, sl], p_a[pb][:], s_rk[:, kvh:kvh+1], None, ALU.mult),
                      reads=[b_pa[pb], b_rk], writes=[b_aT])
    tr.barrier()

    with contextlib.ExitStack() as st:
        s_wo, b_wo = C.alloc("l0_wo", [128, NH, 512], BF16, st)
        s_bo, b_bo = C.alloc("l0_bo", [128, 512], F32, st)
        s_xc, b_xc = C.alloc("l0_xc", [128, 512], F32, st)
        s_g0, b_g0 = C.alloc("l0_g0", [128, 512], BF16, st)
        s_c1, b_c1 = C.alloc("l0_c1", [128, 512], F32, st)
        s_c2, b_c2 = C.alloc("l0_c2", [128, 512], F32, st)
        s_x1, b_x1 = zip(*[C.alloc(f"l0_x1{i}", [128, 512], F32, st) for i in range(2)])
        p_s, b_ps = zip(*[C.palloc(f"l0_ps{i}", [128, 512], st) for i in range(2)])
        for cc in range(8):
            csl = slice(cc*512, (cc+1)*512)
            tr.dma("q_pool", s_wo[:],
                   A["woT"].rearrange("(ot p) c -> p ot c", p=128)[:, :, csl],
                   writes=[b_wo])
            tr.dma("q_sync", s_bo[:].rearrange("p (o n) -> p o n", o=1),
                   A["bo"][0:1, csl].partition_broadcast(128), writes=[b_bo])
            for tt in range(NT):
                pb = tt % 2
                tsl = slice(tt*128, (tt+1)*128)
                tr.mmgroup([(lambda e, ot=ot, pb=pb, tsl=tsl:
                             e.matmul(p_s[pb][:], s_aT[:, ot, tsl], s_wo[:, ot, :],
                                      start=(ot == 0), stop=(ot == NH-1)))
                            for ot in range(NH)],
                           reads=[b_aT, b_wo], writes=[b_ps[pb]])
                tr.dma("q_sync", s_xc[:], x_ap[tsl, csl], writes=[b_xc])
                tr.dma("q_sync", s_g0[:], A["d_gs0"][tt, :, csl], writes=[b_g0])
                tr.op("dve", lambda e, pb=pb: e.tensor_add(s_c1[:], p_s[pb][:], s_bo[:]),
                      reads=[b_ps[pb], b_bo], writes=[b_c1])
                tr.op("dve", lambda e: e.tensor_mul(s_c2[:], s_c1[:], s_g0[:]),
                      reads=[b_c1, b_g0], writes=[b_c2])
                xb = tt % 2
                tr.op("dve", lambda e, xb=xb: e.tensor_add(s_x1[xb][:], s_c2[:], s_xc[:]),
                      reads=[b_c2, b_xc], writes=[b_x1[xb]])
                tr.dma("q_act", x1_ap[tsl, csl], s_x1[xb][:], reads=[b_x1[xb]])
    tr.barrier()
    lay.close()




def emit_layer1(C, x1_ap, out_ap):
    nc, tr, A = C.nc, C.tr, C.A
    lay = contextlib.ExitStack()
    b_agk, b_agv, b_agq, b_aga = Buf(), Buf(), Buf(), Buf()
    s_rk, b_rk = C.alloc("l1_rk", [1, 1], mybir.dt.uint32, lay)
    tr.dma("q_sync", s_rk[:], A["rk"], writes=[b_rk])
    rkreg = nc.gpsimd.alloc_register("rkreg")
    tr.op("pool", lambda e: e.reg_load(rkreg, s_rk[0:1, 0:1]), reads=[b_rk])
    rv = nc.gpsimd.snap(rkreg, min_val=0, max_val=1)
    s_s, b_s = C.alloc("l1_s", [128, 4], F32, lay)
    s_cv, b_cv = C.alloc("l1_cv", [128, NDT], F32, lay)
    tr.dma("q_sync", s_s[:].rearrange("p (o n) -> p o n", o=1),
           A["s012"][0:1, :].partition_broadcast(128), writes=[b_s])
    tr.dma("q_sync", s_cv[:], A["cvec_c"], writes=[b_cv])

    era = contextlib.ExitStack()
    s_xnT, b_xnT = C.alloc("l1_xnT", [128, NDT, T], BF16, era)
    emit_lnT(C, x1_ap, s_xnT, b_xnT, "l1ln")
    if DBG:
        dbg_xn1 = nc.dram_tensor("d_xn1T", [128, NDT, T], BF16, kind="ExternalOutput").ap()
        tr.dma("q_act", dbg_xn1, s_xnT[:], reads=[b_xnT])

    s_cos, b_cos = C.alloc("l1_cos", [128, T], F32, era)
    s_sin, b_sin = C.alloc("l1_sin", [128, T], F32, era)
    tr.dma("q_sync", s_cos[:], A["cos_c"], writes=[b_cos])
    tr.dma("q_sync", s_sin[:], A["sin_c"], writes=[b_sin])

    def rope_evac(st, p_y, b_py, sl, dst, b_dst, bufs):
        s_r1, b_r1, s_r2, b_r2 = bufs
        tr.op("dve", lambda e: e.tensor_tensor(s_r1[:], p_y[:], s_cos[:, sl], ALU.mult),
              reads=[b_py, b_cos], writes=[b_r1])
        tr.op("dve", lambda e: e.tensor_tensor(s_r2[0:64, :], p_y[64:128, :], s_sin[0:64, sl], ALU.mult),
              reads=[b_py, b_sin], writes=[b_r2])
        tr.op("dve", lambda e: e.tensor_tensor(s_r2[64:128, :], p_y[0:64, :], s_sin[64:128, sl], ALU.mult),
              reads=[b_py, b_sin], writes=[b_r2])
        tr.op("dve", lambda e: e.tensor_add(dst, s_r1[:], s_r2[:]),
              reads=[b_r1, b_r2], writes=[b_dst])

    with contextlib.ExitStack() as st:
        s_w, b_w = zip(*[C.alloc(f"l1kv_w{i}", [128, NDT, 128], BF16, st) for i in range(2)])
        s_r1, b_r1 = C.alloc("l1kv_r1", [128, 512], F32, st)
        s_r2, b_r2 = C.alloc("l1kv_r2", [128, 512], F32, st)
        rbufs = (s_r1, b_r1, s_r2, b_r2)
        s_kT, b_kT = C.alloc("l1_kT", [128, T], BF16, st)
        s_vT, b_vT = C.alloc("l1_vT", [128, T], BF16, st)
        s_vx, b_vx = C.alloc("l1_vx", [128, NT, 128], BF16, st)
        p_y, b_py = zip(*[C.palloc(f"l1kv_p{i}", [128, 512], st) for i in range(2)])
        p_t2, b_pt2 = zip(*[C.palloc(f"l1kv_pt{i}", [128, 128], st, BF16) for i in range(2)])
        for kh in range(NKV):
            for m, wname in ((0, "wk1"), (1, "wv1")):
                wb = m
                tr.dma("q_pool", s_w[wb][:],
                       A[wname].rearrange("(dt p) o -> p dt o", p=128)[:, :, kh*128:(kh+1)*128],
                       writes=[b_w[wb]])
                for tc in range(4):
                    pb = tc % 2
                    sl = slice(tc*512, (tc+1)*512)
                    tr.mmgroup([(lambda e, dt=dt, pb=pb, wb=wb, sl=sl:
                                 e.matmul(p_y[pb][:], s_w[wb][:, dt, :], s_xnT[:, dt, sl],
                                          start=(dt == 0), stop=(dt == NDT-1)))
                                for dt in range(NDT)],
                               reads=[b_w[wb], b_xnT], writes=[b_py[pb]])
                    if m == 0:
                        rope_evac(st, p_y[pb], b_py[pb], sl, s_kT[:, sl], b_kT, rbufs)
                    else:
                        tr.op("dve", lambda e, pb=pb, sl=sl: e.tensor_copy(s_vT[:, sl], p_y[pb][:]),
                              reads=[b_py[pb]], writes=[b_vT])
            tr.dma("q_act", A["d_agk_in"].rearrange("(h p) t -> p h t", p=128)[:, kh, :],
                   s_kT[:], reads=[b_kT])
            for tt in range(NT):
                pb = tt % 2
                tr.op("pe", lambda e, tt=tt, pb=pb: e.transpose(p_t2[pb][:], s_vT[:, tt*128:(tt+1)*128], C.s_ident[:]),
                      reads=[b_vT, C.b_ident], writes=[b_pt2[pb]])
                tr.op("dve", lambda e, tt=tt, pb=pb: e.tensor_copy(s_vx[:, tt, :], p_t2[pb][:]),
                      reads=[b_pt2[pb]], writes=[b_vx])
            tr.dma("q_act", A["d_agv_in"].rearrange("(tt p) v -> p tt v", p=128)[:, :, kh*128:(kh+1)*128],
                   s_vx[:], reads=[b_vx])
        toks = [(sname, c) for sname, c in tr.counts.items() if sname.startswith("q_act")]
        tr.wait("pool", toks)
        tr.collective("AllGather", ALU.bypass, A["d_agk_in"], A["d_agk_out"], b_agk)
        tr.collective("AllGather", ALU.bypass, A["d_agv_in"], A["d_agv_out"], b_agv)
    tr.barrier()

    with contextlib.ExitStack() as st:
        s_w, b_w = zip(*[C.alloc(f"l1q_w{i}", [128, NDT, 128], BF16, st) for i in range(2)])
        s_r1, b_r1 = C.alloc("l1q_r1", [128, 512], F32, st)
        s_r2, b_r2 = C.alloc("l1q_r2", [128, 512], F32, st)
        rbufs = (s_r1, b_r1, s_r2, b_r2)
        s_qt, b_qt = zip(*[C.alloc(f"l1q_t{i}", [128, T], BF16, st) for i in range(2)])
        p_y, b_py = zip(*[C.palloc(f"l1q_p{i}", [128, 512], st) for i in range(2)])
        for ot in range(NDT):
            wb = ot % 2
            tr.dma("q_pool", s_w[wb][:],
                   A["wq1"].rearrange("(dt p) o -> p dt o", p=128)[:, :, ot*128:(ot+1)*128],
                   writes=[b_w[wb]])
            for tc in range(4):
                pb = tc % 2
                sl = slice(tc*512, (tc+1)*512)
                tr.mmgroup([(lambda e, dt=dt, pb=pb, wb=wb, sl=sl:
                             e.matmul(p_y[pb][:], s_w[wb][:, dt, :], s_xnT[:, dt, sl],
                                      start=(dt == 0), stop=(dt == NDT-1)))
                            for dt in range(NDT)],
                           reads=[b_w[wb], b_xnT], writes=[b_py[pb]])
                rope_evac(st, p_y[pb], b_py[pb], sl, s_qt[wb][:, sl], b_qt[wb], rbufs)
            tr.dma("q_act", A["d_agq_in"].rearrange("(h p) t -> p h t", p=128)[:, ot, :],
                   s_qt[wb][:], reads=[b_qt[wb]])
        toks = [(sname, c) for sname, c in tr.counts.items() if sname.startswith("q_act")]
        tr.wait("pool", toks)
        tr.collective("AllGather", ALU.bypass, A["d_agq_in"], A["d_agq_out"], b_agq)
    tr.barrier()

    with contextlib.ExitStack() as st:
        s_w8, b_w8 = zip(*[C.alloc(f"l1w8_{i}", [128, NDT, 128], U8, st) for i in range(2)])
        s_wb_, b_wb_ = zip(*[C.alloc(f"l1wb_{i}", [128, NDT, 128], BF16, st) for i in range(2)])
        for o2t in range(NDT):
            wb = o2t % 2
            tr.dma("q_sync", s_w8[wb][:], A["k1r"][o2t].rearrange("dt p o -> p dt o"),
                   writes=[b_w8[wb]])
            tr.op("dve", lambda e, wb=wb: e.tensor_scalar(s_wb_[wb][:], s_w8[wb][:], 2.0, -1.0, ALU.mult, ALU.add),
                  reads=[b_w8[wb]], writes=[b_wb_[wb]])
            tr.dma("q_act", A["d_w1bf"][o2t].rearrange("dt p o -> p dt o"), s_wb_[wb][:],
                   reads=[b_wb_[wb]])
        for d2t in range(NDT):
            wb = d2t % 2
            tr.dma("q_sync", s_w8[wb][:].rearrange("p dt o -> p (dt o)"),
                   A["k2r"][d2t], writes=[b_w8[wb]])
            tr.op("dve", lambda e, wb=wb: e.tensor_scalar(
                    s_wb_[wb][:].rearrange("p dt o -> p (dt o)"),
                    s_w8[wb][:].rearrange("p dt o -> p (dt o)"), 2.0, -1.0, ALU.mult, ALU.add),
                  reads=[b_w8[wb]], writes=[b_wb_[wb]])
            tr.dma("q_act", A["d_w2bf"][d2t], s_wb_[wb][:].rearrange("p dt o -> p (dt o)"),
                  reads=[b_wb_[wb]])
        s_t1, b_t1 = C.alloc("l1m1_t1", [128, 512], F32, st)
        s_h1, b_h1 = zip(*[C.alloc(f"l1_h1{i}", [128, T], BF16, st) for i in range(2)])
        p_y, b_py = zip(*[C.palloc(f"l1m1_p{i}", [128, 512], st) for i in range(2)])
        for ot in range(NDT):
            wb = ot % 2
            tr.dma("q_sync", s_w8[wb][:], A["k0r"][ot].rearrange("dt p o -> p dt o"),
                   writes=[b_w8[wb]])
            tr.op("dve", lambda e, wb=wb: e.tensor_scalar(s_wb_[wb][:], s_w8[wb][:], 2.0, -1.0, ALU.mult, ALU.add),
                  reads=[b_w8[wb]], writes=[b_wb_[wb]])
            for tc in range(4):
                pb = tc % 2
                sl = slice(tc*512, (tc+1)*512)
                tr.mmgroup([(lambda e, dt=dt, pb=pb, wb=wb, sl=sl:
                             e.matmul(p_y[pb][:], s_wb_[wb][:, dt, :], s_xnT[:, dt, sl],
                                      start=(dt == 0), stop=(dt == NDT-1)))
                            for dt in range(NDT)],
                           reads=[b_wb_[wb], b_xnT], writes=[b_py[pb]])
                tr.op("act", lambda e, pb=pb: e.activation(s_t1[:], p_y[pb][:], AF.Relu, scale=s_s[:, 0:1]),
                      reads=[b_py[pb], b_s], writes=[b_t1])
                tr.op("dve", lambda e, wb=wb, sl=sl, ot=ot: e.tensor_scalar(s_h1[wb][:, sl], s_t1[:], s_cv[:, ot:ot+1], None, ALU.mult),
                      reads=[b_t1, b_cv], writes=[b_h1[wb]])
            tr.dma("q_act", A["d_h1"][ot], s_h1[wb][:], reads=[b_h1[wb]])
    tr.barrier()
    era.close()

    with contextlib.ExitStack() as st:
        s_h1c, b_h1c = C.alloc("l1_h1c", [128, NDT, 512], BF16, st)
        s_h2c, b_h2c = C.alloc("l1_h2c", [128, NDT, 512], BF16, st)
        s_w1b, b_w1b = zip(*[C.alloc(f"l1w1b{i}", [128, NDT, 128], BF16, st) for i in range(2)])
        s_w2b, b_w2b = C.alloc("l1w2b", [128, NDT, 512], BF16, st)
        s_t1, b_t1 = C.alloc("l1m2_t1", [128, 512], F32, st)
        s_gs1, b_gs1 = zip(*[C.alloc(f"l1_gs1{i}", [128, 512], BF16, st) for i in range(2)])
        p_y, b_py = zip(*[C.palloc(f"l1m2_p{i}", [128, 512], st) for i in range(2)])
        for tc in range(4):
            sl = slice(tc*512, (tc+1)*512)
            tr.dma("q_sync", s_h1c[:], A["d_h1"][:, :, sl].rearrange("ot p t -> p ot t"),
                   writes=[b_h1c])
            for o2t in range(NDT):
                wb = o2t % 2
                tr.dma("q_pool", s_w1b[wb][:], A["d_w1bf"][o2t].rearrange("dt p o -> p dt o"),
                       writes=[b_w1b[wb]])
                pb = o2t % 2
                tr.mmgroup([(lambda e, dt=dt, pb=pb, wb=wb:
                             e.matmul(p_y[pb][:], s_w1b[wb][:, dt, :], s_h1c[:, dt, :],
                                      start=(dt == 0), stop=(dt == NDT-1)))
                            for dt in range(NDT)],
                           reads=[b_w1b[wb], b_h1c], writes=[b_py[pb]])
                tr.op("act", lambda e, pb=pb: e.activation(s_t1[:], p_y[pb][:], AF.Relu, scale=s_s[:, 1:2]),
                      reads=[b_py[pb], b_s], writes=[b_t1])
                tr.op("dve", lambda e, o2t=o2t: e.tensor_scalar(s_h2c[:, o2t, :], s_t1[:], s_cv[:, o2t:o2t+1], None, ALU.mult),
                      reads=[b_t1, b_cv], writes=[b_h2c])
            for cc in range(8):
                csl = slice(cc*512, (cc+1)*512)
                tr.dma("q_pool", s_w2b[:], A["d_w2bf"][:, :, csl].rearrange("dt p c -> p dt c"),
                       writes=[b_w2b])
                for t4 in range(4):
                    tt = tc*4 + t4
                    pb = t4 % 2
                    tsl = slice(t4*128, (t4+1)*128)
                    tr.mmgroup([(lambda e, dt=dt, pb=pb, tsl=tsl:
                                 e.matmul(p_y[pb][:], s_h2c[:, dt, tsl], s_w2b[:, dt, :],
                                          start=(dt == 0), stop=(dt == NDT-1)))
                                for dt in range(NDT)],
                               reads=[b_h2c, b_w2b], writes=[b_py[pb]])
                    gb = t4 % 2
                    tr.op("act", lambda e, pb=pb, gb=gb: e.activation(s_gs1[gb][:], p_y[pb][:], AF.Silu, scale=s_s[:, 2:3]),
                          reads=[b_py[pb], b_s], writes=[b_gs1[gb]])
                    tr.dma("q_act", A["d_gs1"][tt, :, csl], s_gs1[gb][:], reads=[b_gs1[gb]])
    tr.barrier()

    def gtile(t):
        return t if t < 8 else (t + 8 if t < 24 else t - 16)
    def gchunk(c):
        return c if c < 2 else (c + 2 if c < 6 else c - 4)
    at_stack = contextlib.ExitStack()
    s_aT, b_aT = C.alloc("l1_aT", [128, 16, 4096], BF16, at_stack)
    b_qmy, b_kmy, b_vmy = Buf(), Buf(), Buf()
    tr.wait("pool", [b_agk.w, b_agv.w, b_agq.w])
    rv = nc.gpsimd.snap(rkreg, min_val=0, max_val=1)
    tr.dma("q_pool", A["d_qmy"][:, 0:T], A["d_agq_out"][bass.ds(rv*2048, 2048), :],
           reads=[b_agq], writes=[b_qmy])
    tr.dma("q_pool", A["d_qmy"][:, T:2*T], A["d_agq_out"][bass.ds(rv*2048 + 4096, 2048), :],
           reads=[b_agq], writes=[b_qmy])
    tr.dma("q_pool", A["d_kmy"][:, 0:T], A["d_agk_out"][bass.ds(rv*512, 512), :],
           reads=[b_agk], writes=[b_kmy])
    tr.dma("q_pool", A["d_kmy"][:, T:2*T], A["d_agk_out"][bass.ds(rv*512 + 1024, 512), :],
           reads=[b_agk], writes=[b_kmy])
    tr.dma("q_pool", A["d_vmy"], A["d_agv_out"][:, bass.ds(rv*512, 512)],
           reads=[b_agv], writes=[b_vmy])
    with contextlib.ExitStack() as st:
        s_kth, b_kth = C.alloc("l1_kth", [128, 4096], BF16, st)
        s_vh, b_vh = C.alloc("l1_vh", [128, 32, 128], BF16, st)
        s_qh, b_qh = zip(*[C.alloc(f"l1_qh{i}", [128, 4096], BF16, st) for i in range(2)])
        s_msk, b_msk = C.alloc("l1_msk", [128, 4, 512], BF16, st)
        tr.dma("q_sync", s_msk[:], A["masks"].rearrange("m p f -> p m f"), writes=[b_msk])
        s_pT, b_pT = zip(*[C.alloc(f"l1_pT{i}", [128, 512], BF16, st) for i in range(2)])
        s_pTm, b_pTm = zip(*[C.alloc(f"l1_pTm{i}", [128, 512], BF16, st) for i in range(2)])
        s_rl, b_rl = C.alloc("l1_rl", [1, 512], F32, st)
        s_rlb, b_rlb = C.alloc("l1_rlb", [1, 512], BF16, st)
        s_rb, b_rb = C.alloc("l1_rb", [128, 512], F32, st)
        p_st, b_pst = zip(*[C.palloc(f"l1_pst{i}", [128, 512], st) for i in range(2)])
        p_pv, b_ppv = C.palloc("l1_ppv", [128, 512], st)
        p_l, b_pl = C.palloc("l1_pl", [1, 512], st)
        p_bc, b_pbc = C.palloc("l1_pbc", [128, 512], st)
        for kvl in range(4):
            tr.dma("q_sync", s_kth[:], A["d_kmy"][kvl*128:(kvl+1)*128, :],
                   reads=[b_kmy], writes=[b_kth])
            tr.dma("q_sync", s_vh[:],
                   A["d_vmy"].rearrange("(kt p) v -> p kt v", p=128)[:, :, kvl*128:(kvl+1)*128],
                   reads=[b_vmy], writes=[b_vh])
            for r4 in range(4):
                hh = kvl * 4 + r4
                hb = r4 % 2
                tr.dma("q_sync", s_qh[hb][:], A["d_qmy"][hh*128:(hh+1)*128, :],
                       reads=[b_qmy], writes=[b_qh[hb]])
                for qc in range(8):
                    q0 = qc * 512
                    gsl = slice(gchunk(qc)*512, gchunk(qc)*512 + 512)
                    nkt = (q0 + 512) // 128
                    for kt in range(nkt):
                        ktg = gtile(kt)
                        sb = kt % 2
                        tr.mmgroup([(lambda e, ktg=ktg, sb=sb, hb=hb, gsl=gsl:
                                     e.matmul(p_st[sb][:], s_kth[:, ktg*128:(ktg+1)*128], s_qh[hb][:, gsl],
                                              start=True, stop=True))],
                                   reads=[b_kth, b_qh[hb]], writes=[b_pst[sb]])
                        tr.op("act", lambda e, sb=sb: e.activation(s_pT[sb][:], p_st[sb][:], AF.Exp, scale=ISCALE),
                              reads=[b_pst[sb]], writes=[b_pT[sb]])
                        src, b_src = s_pT[sb], b_pT[sb]
                        di = kt - q0 // 128
                        if di >= 0:
                            tr.op("dve", lambda e, sb=sb, di=di: e.tensor_tensor(s_pTm[sb][:], s_pT[sb][:], s_msk[:, di, :], ALU.mult),
                                  reads=[b_pT[sb], b_msk], writes=[b_pTm[sb]])
                            src, b_src = s_pTm[sb], b_pTm[sb]
                        tr.mmgroup([(lambda e, ktg=ktg, src=src, kt=kt, nkt=nkt:
                                     e.matmul(p_pv[:], s_vh[:, ktg, :], src[:],
                                              start=(kt == 0), stop=(kt == nkt-1)))],
                                   reads=[b_vh, b_src], writes=[b_ppv])
                        tr.mmgroup([(lambda e, src=src, kt=kt, nkt=nkt:
                                     e.matmul(p_l[:], C.s_ones_col[:], src[:],
                                              start=(kt == 0), stop=(kt == nkt-1)))],
                                   reads=[C.b_ones_col, b_src], writes=[b_pl])
                    tr.op("dve", lambda e: e.reciprocal(s_rl[:], p_l[:]),
                          reads=[b_pl], writes=[b_rl])
                    tr.op("dve", lambda e: e.tensor_copy(s_rlb[:], s_rl[:]),
                          reads=[b_rl], writes=[b_rlb])
                    tr.mmgroup([(lambda e: e.matmul(p_bc[:], C.s_ones_row[:], s_rlb[:],
                                                    start=True, stop=True))],
                               reads=[C.b_ones_row, b_rlb], writes=[b_pbc])
                    tr.op("act", lambda e: e.copy(s_rb[:], p_bc[:]), reads=[b_pbc], writes=[b_rb])
                    tr.op("dve", lambda e, hh=hh, gsl=gsl: e.tensor_tensor(s_aT[:, hh, gsl], p_pv[:], s_rb[:], ALU.mult),
                          reads=[b_ppv, b_rb], writes=[b_aT])
        if DBG:
            dbg_aT = nc.dram_tensor("d_aTdump", [128, 16, 4096], BF16, kind="ExternalOutput").ap()
            tr.dma("q_act", dbg_aT, s_aT[:], reads=[b_aT])
        tr.dma("q_act", A["d_aga_in"].rearrange("(h p) t -> p h t", p=128),
               s_aT[:], reads=[b_aT])
        toks = [(sname, c) for sname, c in tr.counts.items() if sname.startswith("q_act")]
        tr.wait("pool", toks)
        tr.collective("AllGather", ALU.bypass, A["d_aga_in"], A["d_aga_out"], b_aga)
    tr.barrier()
    at_stack.close()

    with contextlib.ExitStack() as st:
        s_aT2, b_aT2 = C.alloc("l1_aT2", [128, NH, T], BF16, st)
        tr.wait("pool", [b_aga.w])
        rv = nc.gpsimd.snap(rkreg, min_val=0, max_val=1)
        tr.dma("q_pool", s_aT2[:],
               A["d_aga_out"].rearrange("(h p) t -> p h t", p=128)[:, :, bass.ds(rv*T, T)],
               reads=[b_aga], writes=[b_aT2])
        s_wo, b_wo = C.alloc("l1_wo", [128, NH, 512], BF16, st)
        s_xc, b_xc = C.alloc("l1_xc", [128, 512], F32, st)
        s_g1, b_g1 = C.alloc("l1_g1", [128, 512], BF16, st)
        s_c2, b_c2 = C.alloc("l1_c2", [128, 512], F32, st)
        s_x2, b_x2 = zip(*[C.alloc(f"l1_x2{i}", [128, 512], F32, st) for i in range(2)])
        p_s, b_ps = zip(*[C.palloc(f"l1_ps{i}", [128, 512], st) for i in range(2)])
        for cc in range(8):
            csl = slice(cc*512, (cc+1)*512)
            tr.dma("q_pool", s_wo[:],
                   A["wo1"].rearrange("(ot p) c -> p ot c", p=128)[:, :, csl],
                   writes=[b_wo])
            for tt in range(NT):
                pb = tt % 2
                tsl = slice(tt*128, (tt+1)*128)
                tr.mmgroup([(lambda e, ot=ot, pb=pb, tsl=tsl:
                             e.matmul(p_s[pb][:], s_aT2[:, ot, tsl], s_wo[:, ot, :],
                                      start=(ot == 0), stop=(ot == NH-1)))
                            for ot in range(NH)],
                           reads=[b_aT2, b_wo], writes=[b_ps[pb]])
                tr.dma("q_sync", s_xc[:], x1_ap[tsl, csl], writes=[b_xc])
                tr.dma("q_sync", s_g1[:], A["d_gs1"][tt, :, csl], writes=[b_g1])
                tr.op("dve", lambda e, pb=pb: e.tensor_mul(s_c2[:], p_s[pb][:], s_g1[:]),
                      reads=[b_ps[pb], b_g1], writes=[b_c2])
                xb = tt % 2
                tr.op("dve", lambda e, xb=xb: e.tensor_add(s_x2[xb][:], s_c2[:], s_xc[:]),
                      reads=[b_c2, b_xc], writes=[b_x2[xb]])
                tr.dma("q_act", out_ap[tsl, csl], s_x2[xb][:], reads=[b_x2[xb]])
    tr.barrier()
    lay.close()


DBG = False

def _scratch(nc, name, shape, dtype):
    kind = "ExternalOutput" if (DBG and name in (
        "d_h", "d_phiq", "d_gs0", "d_xnT", "d_zsg",
        "d_h1", "d_gs1", "d_w1bf", "d_qmy", "d_kmy", "d_vmy")) else "Internal"
    return nc.dram_tensor(name, list(shape), dtype, kind=kind).ap()


def _common_dram(nc, A, need_l0, need_l1):
    ext = []
    def inp(name, shape, dtype):
        A[name] = nc.dram_tensor(name, list(shape), dtype, kind="ExternalInput").ap()
        ext.append(name)
    inp("ident", (128, 128), BF16)
    if need_l0:
        inp("x", (T, D), F32)
        inp("wqT", (D, D), F32); inp("wkT", (D, 1024), F32)
        inp("wvT", (D, 1024), F32); inp("woT", (D, D), F32)
        inp("bq_c", (128, 32), F32); inp("bk_c", (128, 8), F32)
        inp("bv_c", (128, 8), F32); inp("bo", (1, D), F32)
        inp("qg", (NOI, D, 64), U8); inp("qu", (NOI, D, 64), U8)
        inp("sge", (NOI, 32, 64), F32); inp("sgo", (NOI, 32, 64), F32)
        inp("sue", (NOI, 32, 64), F32); inp("suo", (NOI, 32, 64), F32)
        inp("sgef", (32, DI // 2), F32); inp("sgof", (32, DI // 2), F32)
        inp("suef", (32, DI // 2), F32); inp("suof", (32, DI // 2), F32)
        inp("zg8", (32, DI // 2), U8); inp("zu8", (32, DI // 2), U8)
        inp("qd", (8, DI, 256), U8)
        inp("sd_b", (8, NOI, 512), F32)
        inp("sdef", (NOI, 2048), F32); inp("sdof", (NOI, 2048), F32)
        inp("zd8", (NOI, 2048), U8)
        inp("stair32", (128, 63), BF16); inp("stair86", (128, 171), BF16)
        A["d_h"] = _scratch(nc, "d_h", (NOI, 128, T), BF16)
        A["d_phiq"] = _scratch(nc, "d_phiq", (NDT, 128, T), BF16)
        A["d_gs0"] = _scratch(nc, "d_gs0", (NT, 128, D), BF16)
        A["d_zsg"] = _scratch(nc, "d_zsg", (32, DI), BF16)
        A["d_zsu"] = _scratch(nc, "d_zsu", (32, DI), BF16)
        A["d_cc0in"] = _scratch(nc, "d_cc0in", (129, 1024), F32)
        A["d_cc0out"] = _scratch(nc, "d_cc0out", (129, 1024), F32)
    if need_l1:
        inp("wq1", (D, D), F32); inp("wk1", (D, 1024), F32)
        inp("wv1", (D, 1024), F32); inp("wo1", (D, D), F32)
        inp("k0r", (NDT, NDT, 128, 128), U8)
        inp("k1r", (NDT, NDT, 128, 128), U8)
        inp("k2r", (NDT, 128, D), U8)
        inp("s012", (1, 4), F32)
        inp("cvec_c", (128, NDT), F32)
        inp("cos_c", (128, T), F32)
        inp("sin_c", (128, T), F32)
        inp("masks", (4, 128, 512), BF16)
        inp("rk", (1, 1), mybir.dt.uint32)
        A["d_agk_in"] = _scratch(nc, "d_agk_in", (1024, T), BF16)
        A["d_agk_out"] = _scratch(nc, "d_agk_out", (2048, T), BF16)
        A["d_agv_in"] = _scratch(nc, "d_agv_in", (T, 1024), BF16)
        A["d_agv_out"] = _scratch(nc, "d_agv_out", (2 * T, 1024), BF16)
        A["d_agq_in"] = _scratch(nc, "d_agq_in", (4096, T), BF16)
        A["d_agq_out"] = _scratch(nc, "d_agq_out", (8192, T), BF16)
        A["d_aga_in"] = _scratch(nc, "d_aga_in", (2048, 2 * T), BF16)
        A["d_aga_out"] = _scratch(nc, "d_aga_out", (4096, 2 * T), BF16)
        A["d_qmy"] = _scratch(nc, "d_qmy", (2048, 2 * T), BF16)
        A["d_kmy"] = _scratch(nc, "d_kmy", (512, 2 * T), BF16)
        A["d_vmy"] = _scratch(nc, "d_vmy", (2 * T, 512), BF16)
        A["d_h1"] = _scratch(nc, "d_h1", (NDT, 128, T), BF16)
        A["d_w1bf"] = _scratch(nc, "d_w1bf", (NDT, NDT, 128, 128), BF16)
        A["d_w2bf"] = _scratch(nc, "d_w2bf", (NDT, 128, D), BF16)
        A["d_gs1"] = _scratch(nc, "d_gs1", (NT, 128, D), BF16)
    return ext


def build_l0():
    nc = bass.Bass("TRN2", target_bir_lowering=False)
    A = {}
    ext = _common_dram(nc, A, True, False)
    x1 = nc.dram_tensor("x1", [T, D], F32, kind="ExternalOutput").ap()
    C = Ctx(nc, A)
    emit_layer0(C, A["x"], x1)
    C.tr.barrier()
    C.stack.close()
    C.tr.ctx.close()
    return nc, ext


def build_fused():
    nc = bass.Bass("TRN2", target_bir_lowering=False)
    A = {}
    ext = _common_dram(nc, A, True, True)
    A["d_x1"] = _scratch(nc, "d_x1", (T, D), F32)
    out = nc.dram_tensor("xout", [T, D], F32, kind="ExternalOutput").ap()
    C = Ctx(nc, A)
    emit_layer0(C, A["x"], A["d_x1"])
    emit_layer1(C, A["d_x1"], out)
    C.tr.barrier()
    C.stack.close()
    C.tr.ctx.close()
    return nc, ext


def build_l1():
    nc = bass.Bass("TRN2", target_bir_lowering=False)
    A = {}
    ext = _common_dram(nc, A, False, True)
    x1 = nc.dram_tensor("x1in", [T, D], F32, kind="ExternalInput").ap()
    ext.append("x1in")
    out = nc.dram_tensor("xout", [T, D], F32, kind="ExternalOutput").ap()
    C = Ctx(nc, A)
    emit_layer1(C, x1, out)
    C.tr.barrier()
    C.stack.close()
    C.tr.ctx.close()
    return nc, ext



def _host_shared(inputs):
    S = {}
    S["ident"] = np.eye(128, dtype=BF16NP)
    f32 = lambda a: np.ascontiguousarray(np.asarray(a), dtype=np.float32)
    S["wqT"] = f32(np.asarray(inputs["L0_attn_Wq"]).T)
    S["wkT"] = f32(np.asarray(inputs["L0_attn_Wk"]).T)
    S["wvT"] = f32(np.asarray(inputs["L0_attn_Wv"]).T)
    S["woT"] = f32(np.asarray(inputs["L0_attn_Wo"]).T)
    S["bq_c"] = f32(np.asarray(inputs["L0_attn_bq"]).reshape(32, 128).T)
    S["bk_c"] = f32(np.asarray(inputs["L0_attn_bk"]).reshape(8, 128).T)
    S["bv_c"] = f32(np.asarray(inputs["L0_attn_bv"]).reshape(8, 128).T)
    S["bo"] = f32(np.asarray(inputs["L0_attn_bo"]).reshape(1, D))
    for nm, qk, sk, zk in (("g", "L0_ffn_gate_qweight", "L0_ffn_gate_scales", "L0_ffn_gate_qzeros"),
                           ("u", "L0_ffn_up_qweight", "L0_ffn_up_scales", "L0_ffn_up_qzeros")):
        qb = np.ascontiguousarray(np.asarray(inputs[qk])).view(np.uint8)
        S["q" + nm] = np.ascontiguousarray(qb.reshape(D, NOI, 64).transpose(1, 0, 2))
        sc = f32(np.asarray(inputs[sk]))
        s4 = sc.reshape(32, NOI, 64, 2)
        S["s" + nm + "e"] = np.ascontiguousarray(s4[:, :, :, 0].transpose(1, 0, 2))
        S["s" + nm + "o"] = np.ascontiguousarray(s4[:, :, :, 1].transpose(1, 0, 2))
        S["s" + nm + "ef"] = np.ascontiguousarray(sc[:, 0::2])
        S["s" + nm + "of"] = np.ascontiguousarray(sc[:, 1::2])
        S["z" + nm + "8"] = np.ascontiguousarray(np.asarray(inputs[zk])).view(np.uint8)
    db = np.ascontiguousarray(np.asarray(inputs["L0_ffn_down_qweight"])).view(np.uint8)
    S["qd"] = np.ascontiguousarray(db.reshape(DI, 8, 256).transpose(1, 0, 2))
    ds = f32(np.asarray(inputs["L0_ffn_down_scales"]))
    S["sd_b"] = np.ascontiguousarray(ds.reshape(NOI, 8, 512).transpose(1, 0, 2))
    S["sdef"] = np.ascontiguousarray(ds[:, 0::2])
    S["sdof"] = np.ascontiguousarray(ds[:, 1::2])
    S["zd8"] = np.ascontiguousarray(np.asarray(inputs["L0_ffn_down_qzeros"])).view(np.uint8)
    st32 = np.zeros((128, 63), dtype=BF16NP); st32[:, 31] = 1
    S["stair32"] = st32
    st86 = np.zeros((128, 171), dtype=BF16NP); st86[:, 85] = 1
    S["stair86"] = st86
    S["wq1"] = f32(inputs["L1_attn_Wq"])
    S["wk1"] = f32(inputs["L1_attn_Wk"])
    S["wv1"] = f32(inputs["L1_attn_Wv"])
    S["wo1"] = f32(inputs["L1_attn_Wo"])
    for nm, key in (("k0r", "L1_ffn_k0"), ("k1r", "L1_ffn_k1")):
        ku = np.ascontiguousarray(np.asarray(inputs[key])).view(np.uint8)
        S[nm] = np.ascontiguousarray(ku.reshape(NDT, 128, NDT, 128).transpose(2, 0, 1, 3))
    k2u = np.ascontiguousarray(np.asarray(inputs["L1_ffn_k2"])).view(np.uint8)
    S["k2r"] = np.ascontiguousarray(k2u.reshape(NDT, 128, D))
    S["s012"] = np.array([[float(np.asarray(inputs["L1_ffn_s0"])),
                           float(np.asarray(inputs["L1_ffn_s1"])),
                           float(np.asarray(inputs["L1_ffn_s2"])), 0.0]], dtype=np.float32)
    S["cvec_c"] = np.ascontiguousarray(
        np.where(np.arange(D) < 3277, 2.0, -8.0).astype(np.float32).reshape(NDT, 128).T)
    masks = np.zeros((4, 128, 512), dtype=BF16NP)
    for i in range(4):
        pp = np.arange(128)[:, None]
        ff = np.arange(512)[None, :]
        masks[i] = (ff >= pp + i * 128).astype(BF16NP)
    S["masks"] = masks
    inv = (10000.0 ** (-np.arange(64) / 64.0)).astype(np.float64)
    for h in (0, 1):
        glob = np.concatenate([np.arange(s0, s1) for (s0, s1) in _tok_slices(h)])
        ang = glob[None, :] * inv[:, None]
        cos = np.cos(ang).astype(np.float32)
        sin = np.sin(ang).astype(np.float32)
        S[f"cos_c{h}"] = np.ascontiguousarray(np.concatenate([cos, cos], axis=0))
        S[f"sin_c{h}"] = np.ascontiguousarray(np.concatenate([-sin, sin], axis=0))
    return S


def _tok_slices(h):
    if h == 0:
        return [(0, 1024), (3072, 4096)]
    return [(1024, 3072)]


def _host_percore(inputs, core):
    b, h = core // 2, core % 2
    x = np.asarray(inputs["L0_attn_hidden_states"])
    parts = [np.asarray(x[b, s0:s1, :], dtype=np.float32) for (s0, s1) in _tok_slices(h)]
    return {"x": np.ascontiguousarray(np.concatenate(parts, axis=0))}


def kernel(**inputs):
    S = _host_shared(inputs)
    nc, ext = build_fused()
    in_maps = []
    for c in range(8):
        hh = c % 2
        m = {k: S[k] for k in ext if k in S}
        m.update(_host_percore(inputs, c))
        m["cos_c"] = S[f"cos_c{hh}"]
        m["sin_c"] = S[f"sin_c{hh}"]
        m["rk"] = np.array([[hh]], dtype=np.uint32)
        in_maps.append(m)
    res = run_bass_kernel_spmd(nc, in_maps, core_ids=list(range(8)))
    out = np.zeros((4, 4096, 4096), dtype=np.float32)
    for c in range(8):
        b, hh = c // 2, c % 2
        xo = np.asarray(res.results[c]["xout"])
        ofs = 0
        for (s0, s1) in _tok_slices(hh):
            n = s1 - s0
            out[b, s0:s1, :] = xo[ofs:ofs+n]
            ofs += n
    return out


# version 04
# speedup vs baseline: 1.2253x; 1.2253x over previous
import contextlib
import numpy as np
import ml_dtypes
import concourse.bass as bass
import concourse.mybir as mybir
from concourse.bass_utils import run_bass_kernel_spmd

F32 = mybir.dt.float32
BF16 = mybir.dt.bfloat16
U8 = mybir.dt.uint8
AF = mybir.ActivationFunctionType
ALU = mybir.AluOpType
BF16NP = ml_dtypes.bfloat16

T = 2048
D = 4096
NT = T // 128
NDT = D // 128
DI = 11008
NOI = DI // 128
NKV = 8
NH = 32
EPS = 1e-5
GROUPS2 = [[0, 1], [2, 3], [4, 5], [6, 7]]
ISCALE = float(1.0 / np.sqrt(128.0))


class Buf:
    __slots__ = ("w", "r")
    def __init__(self):
        self.w = None
        self.r = []

class TK:
    def __init__(self, nc):
        self.nc = nc
        self.ctx = contextlib.ExitStack()
        self.sems = {}
        self.counts = {}
        self.satisfied = {}
        self.eng = {"pe": nc.tensor, "act": nc.scalar, "dve": nc.vector,
                    "pool": nc.gpsimd, "sync": nc.sync}
        for n in ("pe", "act", "dve", "pool"):
            self._new_sem(n)
        self.NSLOT = 8
        self.qrr = {}
        for q in ("q_sync", "q_pool", "q_act"):
            self.qrr[q] = 0
            for i in range(self.NSLOT):
                self._new_sem(f"{q}_{i}")
        self.ccn = 0

    def _new_sem(self, name):
        self.sems[name] = self.ctx.enter_context(self.nc.semaphore(name))
        self.counts[name] = 0


    def cc_sem(self):
        name = f"cc_{self.ccn}"
        self.ccn += 1
        self._new_sem(name)
        return name

    def wait(self, engine, toks):
        eng = self.eng[engine]
        need = {}
        for (s, v) in toks:
            if v <= 0 or self.satisfied.get((engine, s), 0) >= v:
                continue
            need[s] = max(need.get(s, 0), v)
        for s, v in need.items():
            eng.wait_ge(self.sems[s], v)
            self.satisfied[(engine, s)] = v

    def inc(self, inst, sem, by):
        inst.then_inc(self.sems[sem], by)
        self.counts[sem] += by
        return (sem, self.counts[sem])

    def barrier(self):
        toks = [(s, c) for s, c in self.counts.items()
                if c > 0 and not s.startswith("cc_")]
        for e in ("pe", "act", "dve", "pool", "sync"):
            self.wait(e, toks)

    def _pre(self, engine, reads, writes):
        toks = []
        for b in reads:
            if b.w: toks.append(b.w)
        for b in writes:
            if b.w: toks.append(b.w)
            toks.extend(b.r)
        self.wait(engine, toks)

    def _post(self, t, reads, writes):
        for b in writes:
            b.w = t; b.r = []
        for b in reads:
            b.r.append(t)
        return t

    def op(self, engine, fn, reads=(), writes=()):
        self._pre(engine, reads, writes)
        inst = fn(self.eng[engine])
        return self._post(self.inc(inst, engine, 1), reads, writes)

    def dma(self, q, out, in_, reads=(), writes=(), **kw):
        engine = {"q_sync": "sync", "q_pool": "pool", "q_act": "act"}[q]
        self._pre(engine, reads, writes)
        sem = f"{q}_{self.qrr[q]}"
        self.qrr[q] = (self.qrr[q] + 1) % self.NSLOT
        self.wait(engine, [(sem, self.counts[sem])])
        inst = self.eng[engine].dma_start(out=out, in_=in_, **kw)
        return self._post(self.inc(inst, sem, 16), reads, writes)

    def mmgroup(self, mms, reads=(), writes=()):
        self._pre("pe", reads, writes)
        inst = None
        for f in mms:
            inst = f(self.eng["pe"])
        return self._post(self.inc(inst, "pe", 1), reads, writes)

    def collective(self, kind, op, in_ap, out_ap, write_tok_buf):
        sem = self.cc_sem()
        inst = self.nc.gpsimd.collective_compute(
            kind, op, replica_groups=GROUPS2,
            ins=[in_ap.opt()], outs=[out_ap.opt()])
        inst.then_inc(self.sems[sem], 1)
        self.counts[sem] += 1
        write_tok_buf.w = (sem, 1)
        write_tok_buf.r = []
        return sem


class Ctx:
    def __init__(self, nc, A):
        self.nc, self.A = nc, A
        self.tr = TK(nc)
        self.stack = contextlib.ExitStack()
        sb = lambda n, s, d: self.stack.enter_context(nc.sbuf_tensor(n, s, d))
        self.s_ident = sb("c_ident", [128, 128], BF16)
        self.s_ones_col = sb("c_ones_col", [128, 1], BF16)
        self.s_ones_row = sb("c_ones_row", [1, 128], BF16)
        self.b_ident, self.b_ones_col, self.b_ones_row = Buf(), Buf(), Buf()
        self.tr.dma("q_sync", self.s_ident[:], A["ident"], writes=[self.b_ident])
        self.tr.op("pool", lambda e: e.memset(self.s_ones_col[:], 1.0), writes=[self.b_ones_col])
        self.tr.op("pool", lambda e: e.memset(self.s_ones_row[:], 1.0), writes=[self.b_ones_row])

    def alloc(self, name, shape, dtype, stack):
        return stack.enter_context(self.nc.sbuf_tensor(name, shape, dtype)), Buf()

    def palloc(self, name, shape, stack, dtype=F32):
        return stack.enter_context(self.nc.psum_tensor(name, shape, dtype)), Buf()



def emit_lnT(C, x_ap, s_xnT, b_xnT, tag):
    tr = C.tr
    with contextlib.ExitStack() as st:
        s_x, b_x, s_xnb, b_xnb = [], [], [], []
        for i in range(2):
            t, b = C.alloc(f"{tag}_x{i}", [128, D], F32, st); s_x.append(t); b_x.append(b)
            t, b = C.alloc(f"{tag}_n{i}", [128, D], BF16, st); s_xnb.append(t); b_xnb.append(b)
        s_junk1, b_junk1 = C.alloc(f"{tag}_junk", [128, D], F32, st)
        s_junk, b_junk = (s_junk1, s_junk1), (b_junk1, b_junk1)
        s_stt, b_stt = zip(*[C.alloc(f"{tag}_st{i}", [128, 8], F32, st) for i in range(2)])
        p_tr, b_ptr = zip(*[C.palloc(f"{tag}_p{i}", [128, 128], st, BF16) for i in range(2)])
        for tt in range(NT):
            xb = tt % 2
            s_st, b_st = s_stt[xb], b_stt[xb]
            s_junk2, b_junk2 = s_junk[xb], b_junk[xb]
            tr.dma("q_sync", s_x[xb][:], x_ap[tt*128:(tt+1)*128, :], writes=[b_x[xb]])
            tr.op("dve", lambda e: e.reduce_sum(s_st[:, 0:1], s_x[xb][:], axis=mybir.AxisListType.X),
                  reads=[b_x[xb]], writes=[b_st])
            tr.op("act", lambda e: e.activation(s_junk2[:], s_x[xb][:], AF.Square, accum_out=s_st[:, 1:2]),
                  reads=[b_x[xb]], writes=[b_junk2, b_st])
            tr.op("dve", lambda e: e.tensor_scalar(s_st[:, 2:3], s_st[:, 0:1], 1.0 / D, None, ALU.mult),
                  reads=[b_st], writes=[b_st])
            tr.op("dve", lambda e: e.tensor_scalar(s_st[:, 3:4], s_st[:, 1:2], 1.0 / D, float(EPS), ALU.mult, ALU.add),
                  reads=[b_st], writes=[b_st])
            tr.op("dve", lambda e: e.scalar_tensor_tensor(s_st[:, 4:5], s_st[:, 2:3], 1.0, s_st[:, 2:3], ALU.mult, ALU.mult),
                  reads=[b_st], writes=[b_st])
            tr.op("dve", lambda e: e.tensor_sub(s_st[:, 3:4], s_st[:, 3:4], s_st[:, 4:5]),
                  reads=[b_st], writes=[b_st])
            tr.op("act", lambda e: e.activation(s_st[:, 5:6], s_st[:, 3:4], AF.Ln),
                  reads=[b_st], writes=[b_st])
            tr.op("act", lambda e: e.activation(s_st[:, 6:7], s_st[:, 5:6], AF.Exp, scale=-0.5),
                  reads=[b_st], writes=[b_st])
            tr.op("dve", lambda e: e.scalar_tensor_tensor(s_st[:, 7:8], s_st[:, 2:3], -1.0, s_st[:, 6:7], ALU.mult, ALU.mult),
                  reads=[b_st], writes=[b_st])
            tr.op("dve", lambda e: e.tensor_scalar(s_xnb[xb][:], s_x[xb][:], s_st[:, 6:7], s_st[:, 7:8], ALU.mult, ALU.add),
                  reads=[b_x[xb], b_st], writes=[b_xnb[xb]])
            for dt in range(NDT):
                pb = dt % 2
                tr.op("pe", lambda e: e.transpose(p_tr[pb][:], s_xnb[xb][:, dt*128:(dt+1)*128], C.s_ident[:]),
                      reads=[b_xnb[xb], C.b_ident], writes=[b_ptr[pb]])
                tr.op("dve", lambda e: e.tensor_copy(s_xnT[:, dt, tt*128:(tt+1)*128], p_tr[pb][:]),
                      reads=[b_ptr[pb]], writes=[b_xnT])
    tr.barrier()


def emit_layer0(C, x_ap, x1_ap, preconv_l1=False):
    nc, tr, A = C.nc, C.tr, C.A
    lay = contextlib.ExitStack()
    s_sg86, b_sg86 = C.alloc("l0_sg86", [86, T], BF16, lay)
    b_cc0 = Buf()

    era = contextlib.ExitStack()
    s_xnT, b_xnT = C.alloc("l0_xnT", [128, NDT, T], BF16, era)
    s_bias, b_bias = C.alloc("l0_bias", [128, 48], F32, era)
    tr.dma("q_sync", s_bias[:, 0:32], A["bq_c"], writes=[b_bias])
    tr.dma("q_sync", s_bias[:, 32:40], A["bk_c"], writes=[b_bias])
    tr.dma("q_sync", s_bias[:, 40:48], A["bv_c"], writes=[b_bias])
    s_sg, b_sg = C.alloc("l0_sg", [32, T], BF16, era)

    emit_lnT(C, x_ap, s_xnT, b_xnT, "l0ln")
    if DBG:
        A["d_xnT"] = _scratch(nc, "d_xnT", (128, NDT, T), BF16)
        tr.dma("q_act", A["d_xnT"], s_xnT[:], reads=[b_xnT])
        tr.barrier()

    with contextlib.ExitStack() as st:
        s_w, b_w = zip(*[C.alloc(f"l0kv_w{i}", [128, NDT, 128], BF16, st) for i in range(2)])
        s_t1, b_t1 = C.alloc("l0kv_t1", [128, 512], F32, st)
        s_t2, b_t2 = C.alloc("l0kv_t2", [128, 512], F32, st)
        s_t3, b_t3 = C.alloc("l0kv_t3", [128, 512], F32, st)
        s_kT, b_kT = C.alloc("l0_kT", [128, T], BF16, st)
        s_vT, b_vT = C.alloc("l0_vT", [128, T], BF16, st)
        s_kx, b_kx = C.alloc("l0_kx", [128, NT, 128], BF16, st)
        s_vx, b_vx = C.alloc("l0_vx", [128, NT, 128], BF16, st)
        s_zero, b_zero = C.alloc("l0kv_zero", [128, 512], F32, st)
        tr.op("pool", lambda e: e.memset(s_zero[:], 0.0), writes=[b_zero])
        p_y, b_py = zip(*[C.palloc(f"l0kv_p{i}", [128, 512], st) for i in range(2)])
        p_t2, b_pt2 = zip(*[C.palloc(f"l0kv_pt{i}", [128, 128], st, BF16) for i in range(2)])
        p_kv, b_pkv = C.palloc("l0_pkv", [128, 1024], st)
        p_ks, b_pks = C.palloc("l0_pks", [1, 1024], st)

        for kh in range(NKV):
            for m, wname, bcol, s_dst, b_dst, do_phi in (
                    (0, "wkT", 32 + kh, s_kT, b_kT, True),
                    (1, "wvT", 40 + kh, s_vT, b_vT, False)):
                wb = m
                tr.dma("q_pool", s_w[wb][:],
                       A[wname].rearrange("(dt p) o -> p dt o", p=128)[:, :, kh*128:(kh+1)*128],
                       writes=[b_w[wb]])
                for tc in range(4):
                    pb = tc % 2
                    sl = slice(tc*512, (tc+1)*512)
                    tr.mmgroup([(lambda e, dt=dt, pb=pb, wb=wb, sl=sl:
                                 e.matmul(p_y[pb][:], s_w[wb][:, dt, :], s_xnT[:, dt, sl],
                                          start=(dt == 0), stop=(dt == NDT-1)))
                                for dt in range(NDT)],
                               reads=[b_w[wb], b_xnT], writes=[b_py[pb]])
                    if do_phi:
                        tr.op("dve", lambda e: e.tensor_scalar(s_t2[:], p_y[pb][:], s_bias[:, bcol:bcol+1], 0.0, ALU.add, ALU.min),
                              reads=[b_py[pb], b_bias], writes=[b_t2])
                        tr.op("act", lambda e: e.activation(s_t3[:], s_t2[:], AF.Exp),
                              reads=[b_t2], writes=[b_t3])
                        tr.op("dve", lambda e: e.scalar_tensor_tensor(s_t1[:], p_y[pb][:], s_bias[:, bcol:bcol+1], s_zero[:], ALU.add, ALU.max),
                              reads=[b_py[pb], b_bias, b_zero], writes=[b_t1])
                        tr.op("dve", lambda e: e.tensor_add(s_dst[:, sl], s_t1[:], s_t3[:]),
                              reads=[b_t1, b_t3], writes=[b_dst])
                    else:
                        tr.op("dve", lambda e: e.tensor_scalar(s_dst[:, sl], p_y[pb][:], s_bias[:, bcol:bcol+1], None, ALU.add),
                              reads=[b_py[pb], b_bias], writes=[b_dst])
            for s_src, b_src, s_d, b_d in ((s_kT, b_kT, s_kx, b_kx), (s_vT, b_vT, s_vx, b_vx)):
                for tt in range(NT):
                    pb = tt % 2
                    tr.op("pe", lambda e: e.transpose(p_t2[pb][:], s_src[:, tt*128:(tt+1)*128], C.s_ident[:]),
                          reads=[b_src, C.b_ident], writes=[b_pt2[pb]])
                    tr.op("dve", lambda e: e.tensor_copy(s_d[:, tt, :], p_t2[pb][:]),
                          reads=[b_pt2[pb]], writes=[b_d])
            sl_kv = slice(kh*128, (kh+1)*128)
            tr.mmgroup([(lambda e, tt=tt, sl_kv=sl_kv:
                         e.matmul(p_kv[:, sl_kv], s_kx[:, tt, :], s_vx[:, tt, :],
                                  start=(tt == 0), stop=(tt == NT-1)))
                        for tt in range(NT)],
                       reads=[b_kx, b_vx], writes=[b_pkv])
            tr.mmgroup([(lambda e, tt=tt, sl_kv=sl_kv:
                         e.matmul(p_ks[:, sl_kv], C.s_ones_col[:], s_kx[:, tt, :],
                                  start=(tt == 0), stop=(tt == NT-1)))
                        for tt in range(NT)],
                       reads=[b_kx, C.b_ones_col], writes=[b_pks])
        s_kv, b_kv = C.alloc("l0_skv", [128, 1024], F32, st)
        s_ks, b_ks = C.alloc("l0_sks", [1, 1024], F32, st)
        tr.op("act", lambda e: e.copy(s_kv[:], p_kv[:]), reads=[b_pkv], writes=[b_kv])
        tr.op("act", lambda e: e.copy(s_ks[:], p_ks[:]), reads=[b_pks], writes=[b_ks])
        b_ccin = Buf()
        tr.dma("q_sync", A["d_cc0in"][0:128, :], s_kv[:], reads=[b_kv], writes=[b_ccin])
        tr.dma("q_sync", A["d_cc0in"][128:129, :], s_ks[:], reads=[b_ks], writes=[b_ccin])
        tr.wait("pool", [b_ccin.w])
        tr.collective("AllReduce", ALU.add, A["d_cc0in"], A["d_cc0out"], b_cc0)
    tr.barrier()

    with contextlib.ExitStack() as st:
        s_stair32, b_st32 = C.alloc("l0_st32", [128, 63], BF16, st)
        tr.dma("q_sync", s_stair32[:], A["stair32"], writes=[b_st32])
        p_sg, b_psg = C.palloc("l0_psg", [32, T], st)
        for tc in range(4):
            sl = slice(tc*512, (tc+1)*512)
            tr.mmgroup([(lambda e, dt=dt, sl=sl:
                         e.matmul(p_sg[:, sl], s_stair32[:, 31-dt:63-dt], s_xnT[:, dt, sl],
                                  start=(dt == 0), stop=(dt == NDT-1)))
                        for dt in range(NDT)],
                       reads=[b_st32, b_xnT], writes=[b_psg])
        tr.op("act", lambda e: e.copy(s_sg[:], p_sg[:]), reads=[b_psg], writes=[b_sg])
        s_z8, b_z8 = C.alloc("l0_z8", [32, DI // 2], U8, st)
        s_zn, b_zn = C.alloc("l0_zn", [32, DI], U8, st)
        s_sf, b_sf = C.alloc("l0_sf", [32, DI // 2], F32, st)
        s_zs, b_zs = C.alloc("l0_zs", [32, DI], BF16, st)
        for nm in ("g", "u"):
            tr.dma("q_sync", s_z8[:], A[f"z{nm}8"], writes=[b_z8])
            tr.op("dve", lambda e: e.tensor_single_scalar(s_zn[:, 0:DI//2], s_z8[:], 15, ALU.bitwise_and),
                  reads=[b_z8], writes=[b_zn])
            tr.op("dve", lambda e: e.tensor_single_scalar(s_zn[:, DI//2:DI], s_z8[:], 4, ALU.logical_shift_right),
                  reads=[b_z8], writes=[b_zn])
            zv = s_zs[:].rearrange("p (n two) -> p n two", two=2)
            for par in (0, 1):
                snm = f"s{nm}ef" if par == 0 else f"s{nm}of"
                off = 0 if par == 0 else DI // 2
                tr.dma("q_sync", s_sf[:], A[snm], writes=[b_sf])
                tr.op("dve", lambda e, zv=zv, off=off, par=par: e.scalar_tensor_tensor(
                        zv[:, :, par:par+1],
                        s_zn[:, off:off+DI//2].rearrange("p (n o) -> p n o", o=1), -1.0,
                        s_sf[:].rearrange("p (n o) -> p n o", o=1), ALU.mult, ALU.mult),
                      reads=[b_zn, b_sf], writes=[b_zs])
            tr.dma("q_act", A[f"d_zs{nm}"], s_zs[:], reads=[b_zs])
    tr.barrier()

    with contextlib.ExitStack() as st:
        s_stair86, b_st86 = C.alloc("l0_st86", [128, 171], BF16, st)
        tr.dma("q_sync", s_stair86[:], A["stair86"], writes=[b_st86])
        bufs = {}
        for nm in ("g", "u"):
            e = {}
            t_, b_ = C.alloc(f"l0q8{nm}", [128, NDT, 64], U8, st)
            e["q8"], e["b_q8"] = (t_, t_), (b_, b_)
            e["nib"], e["b_nib"] = C.alloc(f"l0nib{nm}", [128, NDT, 128], U8, st)
            if nm == "g":
                e["wt"], e["b_wt"] = zip(*[C.alloc(f"l0wt{nm}{i}", [128, NDT, 128], BF16, st) for i in range(2)])
            else:
                t_, b_ = C.alloc(f"l0wt{nm}", [128, NDT, 128], BF16, st)
                e["wt"], e["b_wt"] = (t_, t_), (b_, b_)
            e["sc"], e["b_sc"] = C.alloc(f"l0sc{nm}", [128, 2, NDT, 64], BF16, st)
            e["zss"], e["b_zss"] = zip(*[C.alloc(f"l0zss{nm}{i}", [32, 128], BF16, st) for i in range(2)])
            e["p"], e["b_p"] = zip(*[C.palloc(f"l0p{nm}{i}", [128, 512], st) for i in range(2)])
            bufs[nm] = e
        p_s86, b_ps86 = C.palloc("l0_ps86", [86, T], st)
        s_silu, b_silu = C.alloc("l0_silu", [128, 512], F32, st)
        s_h, b_h = zip(*[C.alloc(f"l0_h{i}", [128, T], BF16, st) for i in range(2)])

        qsrc = {"g": "qg", "u": "qu"}
        ssrc = {"g": ("sge", "sgo"), "u": ("sue", "suo")}
        for ot in range(NOI):
            ob = ot % 2
            for nm in ("g", "u"):
                e = bufs[nm]
                tr.dma("q_sync", e["q8"][ob][:],
                       A[qsrc[nm]][ot].rearrange("(dt p) b -> p dt b", p=128),
                       writes=[e["b_q8"][ob]])
                se, so = ssrc[nm]
                tr.dma("q_pool", e["sc"][:, 0], A[se][ot].partition_broadcast(128), writes=[e["b_sc"]])
                tr.dma("q_pool", e["sc"][:, 1], A[so][ot].partition_broadcast(128), writes=[e["b_sc"]])
                tr.dma("q_sync", e["zss"][ob][:], A[f"d_zs{nm}"][:, ot*128:(ot+1)*128], writes=[e["b_zss"][ob]])
                tr.op("dve", lambda e2, e=e, ob=ob: e2.tensor_single_scalar(e["nib"][:, :, 0:64], e["q8"][ob][:], 15, ALU.bitwise_and),
                      reads=[e["b_q8"][ob]], writes=[e["b_nib"]])
                tr.op("dve", lambda e2, e=e, ob=ob: e2.tensor_single_scalar(e["nib"][:, :, 64:128], e["q8"][ob][:], 4, ALU.logical_shift_right),
                      reads=[e["b_q8"][ob]], writes=[e["b_nib"]])
                wv = e["wt"][ob][:].rearrange("p dt (n two) -> p dt n two", two=2)
                for par, bo in ((0, 0), (1, 64)):
                    tr.op("dve", lambda e2, e=e, wv=wv, par=par, bo=bo: e2.tensor_tensor(
                            wv[:, :, :, par:par+1],
                            e["nib"][:, :, bo:bo+64].rearrange("p dt (n o) -> p dt n o", o=1),
                            e["sc"][:, par].rearrange("p dt (n o) -> p dt n o", o=1), ALU.mult),
                          reads=[e["b_nib"], e["b_sc"]], writes=[e["b_wt"][ob]])
            for tc in range(4):
                sl = slice(tc*512, (tc+1)*512)
                for nm in ("g", "u"):
                    e = bufs[nm]
                    mms = [(lambda e2, e=e, dt=dt, ob=ob, sl=sl:
                            e2.matmul(e["p"][ob][:], e["wt"][ob][:, dt, :], s_xnT[:, dt, sl],
                                      start=(dt == 0), stop=False))
                           for dt in range(NDT)]
                    mms.append(lambda e2, e=e, ob=ob, sl=sl:
                               e2.matmul(e["p"][ob][:], e["zss"][ob][:], s_sg[:, sl],
                                         start=False, stop=True))
                    tr.mmgroup(mms, reads=[e["b_wt"][ob], b_xnT, e["b_zss"][ob], b_sg],
                               writes=[e["b_p"][ob]])
                tr.op("act", lambda e2: e2.activation(s_silu[:], bufs["g"]["p"][ob][:], AF.Silu),
                      reads=[bufs["g"]["b_p"][ob]], writes=[b_silu])
                tr.op("dve", lambda e2: e2.tensor_tensor(s_h[ob][:, sl], s_silu[:], bufs["u"]["p"][ob][:], ALU.mult),
                      reads=[b_silu, bufs["u"]["b_p"][ob]], writes=[b_h[ob]])
                tr.mmgroup([(lambda e2, ot=ot, sl=sl, ob=ob:
                             e2.matmul(p_s86[:, sl], s_stair86[:, 85-ot:171-ot], s_h[ob][:, sl],
                                       start=(ot == 0), stop=(ot == NOI-1)))],
                           reads=[b_st86, b_h[ob]], writes=[b_ps86])
            tr.dma("q_act", A["d_h"][ot], s_h[ob][:], reads=[b_h[ob]])
        tr.op("act", lambda e: e.copy(s_sg86[:], p_s86[:]), reads=[b_ps86], writes=[b_sg86])
    tr.barrier()

    with contextlib.ExitStack() as st:
        s_w, b_w = zip(*[C.alloc(f"l0q_w{i}", [128, NDT, 128], BF16, st) for i in range(2)])
        s_t1, b_t1 = C.alloc("l0q_t1", [128, 512], F32, st)
        s_t2, b_t2 = C.alloc("l0q_t2", [128, 512], F32, st)
        s_t3, b_t3 = C.alloc("l0q_t3", [128, 512], F32, st)
        s_phi, b_phi = zip(*[C.alloc(f"l0q_f{i}", [128, T], BF16, st) for i in range(2)])
        s_zero, b_zero = C.alloc("l0q_zero", [128, 512], F32, st)
        tr.op("pool", lambda e: e.memset(s_zero[:], 0.0), writes=[b_zero])
        p_y, b_py = zip(*[C.palloc(f"l0q_p{i}", [128, 512], st) for i in range(2)])
        for ot in range(NDT):
            wb = ot % 2
            tr.dma("q_pool", s_w[wb][:],
                   A["wqT"].rearrange("(dt p) o -> p dt o", p=128)[:, :, ot*128:(ot+1)*128],
                   writes=[b_w[wb]])
            for tc in range(4):
                pb = tc % 2
                sl = slice(tc*512, (tc+1)*512)
                tr.mmgroup([(lambda e, dt=dt, pb=pb, wb=wb, sl=sl:
                             e.matmul(p_y[pb][:], s_w[wb][:, dt, :], s_xnT[:, dt, sl],
                                      start=(dt == 0), stop=(dt == NDT-1)))
                            for dt in range(NDT)],
                           reads=[b_w[wb], b_xnT], writes=[b_py[pb]])
                tr.op("dve", lambda e: e.tensor_scalar(s_t2[:], p_y[pb][:], s_bias[:, ot:ot+1], 0.0, ALU.add, ALU.min),
                      reads=[b_py[pb], b_bias], writes=[b_t2])
                tr.op("act", lambda e: e.activation(s_t3[:], s_t2[:], AF.Exp),
                      reads=[b_t2], writes=[b_t3])
                tr.op("dve", lambda e: e.scalar_tensor_tensor(s_t1[:], p_y[pb][:], s_bias[:, ot:ot+1], s_zero[:], ALU.add, ALU.max),
                      reads=[b_py[pb], b_bias, b_zero], writes=[b_t1])
                tr.op("dve", lambda e: e.tensor_add(s_phi[wb][:, sl], s_t1[:], s_t3[:]),
                      reads=[b_t1, b_t3], writes=[b_phi[wb]])
            tr.dma("q_act", A["d_phiq"][ot], s_phi[wb][:], reads=[b_phi[wb]])
    tr.barrier()
    era.close()

    with contextlib.ExitStack() as st:
        with contextlib.ExitStack() as pst:
            s_zsd, b_zsd = C.alloc("l0_zsd", [86, 4096], BF16, st)
            s_zd8, b_zd8 = C.alloc("l0_zd8", [86, 2048], U8, pst)
            s_zdn, b_zdn = C.alloc("l0_zdn", [86, 4096], U8, pst)
            s_sdf, b_sdf = C.alloc("l0_sdf", [86, 2048], F32, pst)
            tr.dma("q_sync", s_zd8[:], A["zd8"], writes=[b_zd8])
            tr.op("dve", lambda e: e.tensor_single_scalar(s_zdn[:, 0:2048], s_zd8[:], 15, ALU.bitwise_and),
                  reads=[b_zd8], writes=[b_zdn])
            tr.op("dve", lambda e: e.tensor_single_scalar(s_zdn[:, 2048:4096], s_zd8[:], 4, ALU.logical_shift_right),
                  reads=[b_zd8], writes=[b_zdn])
            zv = s_zsd[:].rearrange("p (n two) -> p n two", two=2)
            for par in (0, 1):
                snm = "sdef" if par == 0 else "sdof"
                off = 0 if par == 0 else 2048
                tr.dma("q_sync", s_sdf[:], A[snm], writes=[b_sdf])
                tr.op("dve", lambda e, par=par, off=off: e.scalar_tensor_tensor(
                        zv[:, :, par:par+1],
                        s_zdn[:, off:off+2048].rearrange("p (n o) -> p n o", o=1),
                        -1.0,
                        s_sdf[:].rearrange("p (n o) -> p n o", o=1), ALU.mult, ALU.mult),
                      reads=[b_zdn, b_sdf], writes=[b_zsd])
            tr.barrier()
        QB = (22, 22, 22, 20)
        QO = (0, 22, 44, 66)
        s_qd8b1, b_qd8b1 = C.alloc("l0_qd8b", [128, 22, 256], U8, st)
        s_qd8b, b_qd8b = (s_qd8b1, s_qd8b1), (b_qd8b1, b_qd8b1)
        s_dnb, b_dnb = C.alloc("l0_dnb", [128, 22, 512], U8, st)
        s_scdb, b_scdb = C.alloc("l0_scdb", [128, 22, 512], BF16, st)
        s_wdt, b_wdt = C.alloc("l0_wdt", [128, 22, 512], BF16, st)
        b_wdbf = [Buf() for _ in range(8)]

        def deq_step(cc, qi):
            nq, oq = QB[qi], QO[qi]
            qb = qi % 2
            tr.dma("q_sync", s_qd8b[qb][:, 0:nq, :],
                   A["qd"][cc, oq*128:(oq+nq)*128, :].rearrange("(ot p) b -> p ot b", p=128),
                   writes=[b_qd8b[qb]])
            tr.dma("q_pool", s_scdb[:, 0:nq, :],
                   A["sd_b"][cc, oq:oq+nq, :].partition_broadcast(128), writes=[b_scdb])
            tr.op("dve", lambda e: e.tensor_single_scalar(s_dnb[:, 0:nq, 0:256], s_qd8b[qb][:, 0:nq, :], 15, ALU.bitwise_and),
                  reads=[b_qd8b[qb]], writes=[b_dnb])
            tr.op("dve", lambda e: e.tensor_single_scalar(s_dnb[:, 0:nq, 256:512], s_qd8b[qb][:, 0:nq, :], 4, ALU.logical_shift_right),
                  reads=[b_qd8b[qb]], writes=[b_dnb])
            wv = s_wdt[:, 0:nq, :].rearrange("p ot (n two) -> p ot n two", two=2)
            sv = s_scdb[:, 0:nq, :].rearrange("p ot (n two) -> p ot n two", two=2)
            for par, bo in ((0, 0), (1, 256)):
                tr.op("dve", lambda e, wv=wv, sv=sv, par=par, bo=bo: e.tensor_tensor(
                        wv[:, :, :, par:par+1],
                        s_dnb[:, 0:nq, bo:bo+256].rearrange("p ot (n o) -> p ot n o", o=1),
                        sv[:, :, :, par:par+1], ALU.mult),
                      reads=[b_dnb, b_scdb], writes=[b_wdt])
            tr.dma("q_act", A["d_wdbf"][cc].rearrange("(ot p) c2 -> p ot c2", p=128)[:, oq:oq+nq, :],
                   s_wdt[:, 0:nq, :], reads=[b_wdt], writes=[b_wdbf[cc]])

        s_wd, b_wd = C.alloc("l0_wd", [128, NOI, 512], BF16, st)
        s_ht, b_ht = zip(*[C.alloc(f"l0_ht{i}", [128, NOI, 128], BF16, st) for i in range(2)])
        s_gs, b_gs = zip(*[C.alloc(f"l0_gs{i}", [128, 512], BF16, st) for i in range(2)])
        p_d, b_pd = zip(*[C.palloc(f"l0_pd{i}", [128, 512], st) for i in range(2)])

        for qi in range(4):
            deq_step(0, qi)
        for cc in range(8):
            csl = slice(cc*512, (cc+1)*512)
            if cc + 1 < 8:
                for qi in range(4):
                    deq_step(cc + 1, qi)
            tr.dma("q_sync", s_wd[:], A["d_wdbf"][cc].rearrange("(ot p) c2 -> p ot c2", p=128),
                   reads=[b_wdbf[cc]], writes=[b_wd])
            if DBG and cc == 0:
                dbg_wd2 = nc.dram_tensor("d_wd2dump", [128, NOI, 512], BF16, kind="ExternalOutput").ap()
                tr.dma("q_act", dbg_wd2, s_wd[:], reads=[b_wd])
                dbg_zsd2 = nc.dram_tensor("d_zsd2dump", [86, 4096], BF16, kind="ExternalOutput").ap()
                tr.dma("q_act", dbg_zsd2, s_zsd[:], reads=[b_zsd])
                dbg_sg2 = nc.dram_tensor("d_sg2dump", [86, T], BF16, kind="ExternalOutput").ap()
                tr.dma("q_act", dbg_sg2, s_sg86[:], reads=[b_sg86])
            for tt in range(NT):
                hb = tt % 2
                tr.dma("q_sync", s_ht[hb][:],
                       A["d_h"][:, :, tt*128:(tt+1)*128].rearrange("ot p t -> p ot t"),
                       writes=[b_ht[hb]])
                pb = tt % 2
                mms = [(lambda e, ot=ot, hb=hb, pb=pb:
                        e.matmul(p_d[pb][:], s_ht[hb][:, ot, :], s_wd[:, ot, :],
                                 start=(ot == 0), stop=False))
                       for ot in range(NOI)]
                mms.append(lambda e, pb=pb, tt=tt, csl=csl:
                           e.matmul(p_d[pb][:], s_sg86[:, tt*128:(tt+1)*128], s_zsd[:, csl],
                                    start=False, stop=True))
                tr.mmgroup(mms, reads=[b_ht[hb], b_wd, b_sg86, b_zsd], writes=[b_pd[pb]])
                tr.op("act", lambda e, pb=pb: e.activation(s_gs[pb][:], p_d[pb][:], AF.Silu),
                      reads=[b_pd[pb]], writes=[b_gs[pb]])
                tr.dma("q_act", A["d_gs0"][tt, :, csl], s_gs[pb][:], reads=[b_gs[pb]])
    tr.barrier()

    s_aT, b_aT = C.alloc("l0_aT", [128, NH, T], BF16, lay)
    with contextlib.ExitStack() as st:
        s_kvb, b_kvb = C.alloc("l0_kvb", [128, 1024], BF16, st)
        s_rk, b_rk = C.alloc("l0_rk", [128, 8], F32, st)
        tr.wait("pool", [b_cc0.w])
        if DBG:
            dbg_cc = nc.dram_tensor("d_ccdump", [129, 1024], F32, kind="ExternalOutput").ap()
            tr.dma("q_sync", dbg_cc, A["d_cc0out"], reads=[b_cc0])
        tr.dma("q_pool", s_kvb[:], A["d_cc0out"][0:128, :], reads=[b_cc0], writes=[b_kvb])
        with C.nc.allow_non_contiguous_dma("tiny ksum transpose load"):
            tr.dma("q_sync", s_rk[:],
                   A["d_cc0out"][128:129, :].rearrange("o (h d) -> d (o h)", h=8),
                   reads=[b_cc0], writes=[b_rk])
        tr.op("dve", lambda e: e.tensor_scalar(s_rk[:], s_rk[:], 1e-10, None, ALU.add),
              reads=[b_rk], writes=[b_rk])
        tr.op("dve", lambda e: e.reciprocal(s_rk[:], s_rk[:]), reads=[b_rk], writes=[b_rk])
        s_ph, b_ph = zip(*[C.alloc(f"l0_ph{i}", [128, T], BF16, st) for i in range(2)])
        p_a, b_pa = zip(*[C.palloc(f"l0_pa{i}", [128, 512], st) for i in range(2)])
        for h in range(NH):
            hb = h % 2
            kvh = h // 4
            tr.dma("q_sync", s_ph[hb][:], A["d_phiq"][h], writes=[b_ph[hb]])
            for tc in range(4):
                pb = tc % 2
                sl = slice(tc*512, (tc+1)*512)
                tr.mmgroup([(lambda e, pb=pb, hb=hb, kvh=kvh, sl=sl:
                             e.matmul(p_a[pb][:], s_kvb[:, kvh*128:(kvh+1)*128], s_ph[hb][:, sl],
                                      start=True, stop=True))],
                           reads=[b_kvb, b_ph[hb]], writes=[b_pa[pb]])
                tr.op("dve", lambda e, pb=pb, h=h, sl=sl, kvh=kvh: e.tensor_scalar(
                        s_aT[:, h, sl], p_a[pb][:], s_rk[:, kvh:kvh+1], None, ALU.mult),
                      reads=[b_pa[pb], b_rk], writes=[b_aT])
    tr.barrier()

    with contextlib.ExitStack() as st:
        if preconv_l1:
            A2 = C.A
            s_w8c, b_w8c = zip(*[C.alloc(f"l0_w8c{i}", [128, NDT, 128], U8, st) for i in range(2)])
            s_wbc, b_wbc = C.alloc("l0_wbc", [128, NDT, 128], BF16, st)
            for o2t in range(NDT):
                wb8 = o2t % 2
                tr.dma("q_sync", s_w8c[wb8][:], A2["k1r"][o2t].rearrange("dt p o -> p dt o"),
                       writes=[b_w8c[wb8]])
                tr.op("dve", lambda e, wb8=wb8: e.tensor_scalar(s_wbc[:], s_w8c[wb8][:], 2.0, -1.0, ALU.mult, ALU.add),
                      reads=[b_w8c[wb8]], writes=[b_wbc])
                tr.dma("q_act", A2["d_w1bf"][o2t].rearrange("dt p o -> p dt o"), s_wbc[:],
                       reads=[b_wbc])
            for d2t in range(NDT):
                wb8 = d2t % 2
                tr.dma("q_sync", s_w8c[wb8][:].rearrange("p dt o -> p (dt o)"), A2["k2r"][d2t],
                       writes=[b_w8c[wb8]])
                tr.op("dve", lambda e, wb8=wb8: e.tensor_scalar(
                        s_wbc[:].rearrange("p dt o -> p (dt o)"),
                        s_w8c[wb8][:].rearrange("p dt o -> p (dt o)"), 2.0, -1.0, ALU.mult, ALU.add),
                      reads=[b_w8c[wb8]], writes=[b_wbc])
                tr.dma("q_act", A2["d_w2bf"][d2t], s_wbc[:].rearrange("p dt o -> p (dt o)"),
                       reads=[b_wbc])
            C.preconv_done = True
        s_wo, b_wo = C.alloc("l0_wo", [128, NH, 512], BF16, st)
        s_bo, b_bo = C.alloc("l0_bo", [128, 512], F32, st)
        s_xc, b_xc = C.alloc("l0_xc", [128, 512], F32, st)
        s_g0, b_g0 = C.alloc("l0_g0", [128, 512], BF16, st)
        s_c1, b_c1 = C.alloc("l0_c1", [128, 512], F32, st)
        s_c2, b_c2 = C.alloc("l0_c2", [128, 512], F32, st)
        s_x1, b_x1 = zip(*[C.alloc(f"l0_x1{i}", [128, 512], F32, st) for i in range(2)])
        p_s, b_ps = zip(*[C.palloc(f"l0_ps{i}", [128, 512], st) for i in range(2)])
        for cc in range(8):
            csl = slice(cc*512, (cc+1)*512)
            tr.dma("q_pool", s_wo[:],
                   A["woT"].rearrange("(ot p) c -> p ot c", p=128)[:, :, csl],
                   writes=[b_wo])
            tr.dma("q_sync", s_bo[:].rearrange("p (o n) -> p o n", o=1),
                   A["bo"][0:1, csl].partition_broadcast(128), writes=[b_bo])
            for tt in range(NT):
                pb = tt % 2
                tsl = slice(tt*128, (tt+1)*128)
                tr.mmgroup([(lambda e, ot=ot, pb=pb, tsl=tsl:
                             e.matmul(p_s[pb][:], s_aT[:, ot, tsl], s_wo[:, ot, :],
                                      start=(ot == 0), stop=(ot == NH-1)))
                            for ot in range(NH)],
                           reads=[b_aT, b_wo], writes=[b_ps[pb]])
                tr.dma("q_sync", s_xc[:], x_ap[tsl, csl], writes=[b_xc])
                tr.dma("q_sync", s_g0[:], A["d_gs0"][tt, :, csl], writes=[b_g0])
                tr.op("dve", lambda e, pb=pb: e.tensor_add(s_c1[:], p_s[pb][:], s_bo[:]),
                      reads=[b_ps[pb], b_bo], writes=[b_c1])
                tr.op("dve", lambda e: e.tensor_mul(s_c2[:], s_c1[:], s_g0[:]),
                      reads=[b_c1, b_g0], writes=[b_c2])
                xb = tt % 2
                tr.op("dve", lambda e, xb=xb: e.tensor_add(s_x1[xb][:], s_c2[:], s_xc[:]),
                      reads=[b_c2, b_xc], writes=[b_x1[xb]])
                tr.dma("q_act", x1_ap[tsl, csl], s_x1[xb][:], reads=[b_x1[xb]])
    tr.barrier()
    lay.close()




def emit_layer1(C, x1_ap, out_ap):
    nc, tr, A = C.nc, C.tr, C.A
    lay = contextlib.ExitStack()
    b_agk, b_agv, b_agq, b_aga = Buf(), Buf(), Buf(), Buf()
    s_rk, b_rk = C.alloc("l1_rk", [1, 1], mybir.dt.uint32, lay)
    tr.dma("q_sync", s_rk[:], A["rk"], writes=[b_rk])
    rkreg = nc.gpsimd.alloc_register("rkreg")
    tr.op("pool", lambda e: e.reg_load(rkreg, s_rk[0:1, 0:1]), reads=[b_rk])
    rv = nc.gpsimd.snap(rkreg, min_val=0, max_val=1)
    s_s, b_s = C.alloc("l1_s", [128, 4], F32, lay)
    s_cv, b_cv = C.alloc("l1_cv", [128, NDT], F32, lay)
    tr.dma("q_sync", s_s[:].rearrange("p (o n) -> p o n", o=1),
           A["s012"][0:1, :].partition_broadcast(128), writes=[b_s])
    tr.dma("q_sync", s_cv[:], A["cvec_c"], writes=[b_cv])

    era = contextlib.ExitStack()
    s_xnT, b_xnT = C.alloc("l1_xnT", [128, NDT, T], BF16, era)
    emit_lnT(C, x1_ap, s_xnT, b_xnT, "l1ln")
    if DBG:
        dbg_xn1 = nc.dram_tensor("d_xn1T", [128, NDT, T], BF16, kind="ExternalOutput").ap()
        tr.dma("q_act", dbg_xn1, s_xnT[:], reads=[b_xnT])

    s_cos, b_cos = C.alloc("l1_cos", [128, T], F32, era)
    s_sin, b_sin = C.alloc("l1_sin", [128, T], F32, era)
    tr.dma("q_sync", s_cos[:], A["cos_c"], writes=[b_cos])
    tr.dma("q_sync", s_sin[:], A["sin_c"], writes=[b_sin])

    def rope_evac(st, p_y, b_py, sl, dst, b_dst, bufs):
        s_r1, b_r1, s_r2, b_r2 = bufs
        tr.op("dve", lambda e: e.tensor_tensor(s_r1[:], p_y[:], s_cos[:, sl], ALU.mult),
              reads=[b_py, b_cos], writes=[b_r1])
        tr.op("dve", lambda e: e.tensor_tensor(s_r2[0:64, :], p_y[64:128, :], s_sin[0:64, sl], ALU.mult),
              reads=[b_py, b_sin], writes=[b_r2])
        tr.op("dve", lambda e: e.tensor_tensor(s_r2[64:128, :], p_y[0:64, :], s_sin[64:128, sl], ALU.mult),
              reads=[b_py, b_sin], writes=[b_r2])
        tr.op("dve", lambda e: e.tensor_add(dst, s_r1[:], s_r2[:]),
              reads=[b_r1, b_r2], writes=[b_dst])

    with contextlib.ExitStack() as st:
        s_w, b_w = zip(*[C.alloc(f"l1kv_w{i}", [128, NDT, 128], BF16, st) for i in range(2)])
        s_r1, b_r1 = C.alloc("l1kv_r1", [128, 512], F32, st)
        s_r2, b_r2 = C.alloc("l1kv_r2", [128, 512], F32, st)
        rbufs = (s_r1, b_r1, s_r2, b_r2)
        s_kT, b_kT = C.alloc("l1_kT", [128, T], BF16, st)
        s_vT, b_vT = C.alloc("l1_vT", [128, T], BF16, st)
        s_vx, b_vx = C.alloc("l1_vx", [128, NT, 128], BF16, st)
        p_y, b_py = zip(*[C.palloc(f"l1kv_p{i}", [128, 512], st) for i in range(2)])
        p_t2, b_pt2 = zip(*[C.palloc(f"l1kv_pt{i}", [128, 128], st, BF16) for i in range(2)])
        for kh in range(NKV):
            for m, wname in ((0, "wk1"), (1, "wv1")):
                wb = m
                tr.dma("q_pool", s_w[wb][:],
                       A[wname].rearrange("(dt p) o -> p dt o", p=128)[:, :, kh*128:(kh+1)*128],
                       writes=[b_w[wb]])
                for tc in range(4):
                    pb = tc % 2
                    sl = slice(tc*512, (tc+1)*512)
                    tr.mmgroup([(lambda e, dt=dt, pb=pb, wb=wb, sl=sl:
                                 e.matmul(p_y[pb][:], s_w[wb][:, dt, :], s_xnT[:, dt, sl],
                                          start=(dt == 0), stop=(dt == NDT-1)))
                                for dt in range(NDT)],
                               reads=[b_w[wb], b_xnT], writes=[b_py[pb]])
                    if m == 0:
                        rope_evac(st, p_y[pb], b_py[pb], sl, s_kT[:, sl], b_kT, rbufs)
                    else:
                        tr.op("dve", lambda e, pb=pb, sl=sl: e.tensor_copy(s_vT[:, sl], p_y[pb][:]),
                              reads=[b_py[pb]], writes=[b_vT])
            tr.dma("q_act", A["d_agk_in"].rearrange("(h p) t -> p h t", p=128)[:, kh, :],
                   s_kT[:], reads=[b_kT])
            for tt in range(NT):
                pb = tt % 2
                tr.op("pe", lambda e, tt=tt, pb=pb: e.transpose(p_t2[pb][:], s_vT[:, tt*128:(tt+1)*128], C.s_ident[:]),
                      reads=[b_vT, C.b_ident], writes=[b_pt2[pb]])
                tr.op("dve", lambda e, tt=tt, pb=pb: e.tensor_copy(s_vx[:, tt, :], p_t2[pb][:]),
                      reads=[b_pt2[pb]], writes=[b_vx])
            tr.dma("q_act", A["d_agv_in"].rearrange("(tt p) v -> p tt v", p=128)[:, :, kh*128:(kh+1)*128],
                   s_vx[:], reads=[b_vx])
        toks = [(sname, c) for sname, c in tr.counts.items() if sname.startswith("q_act")]
        tr.wait("pool", toks)
        tr.collective("AllGather", ALU.bypass, A["d_agk_in"], A["d_agk_out"], b_agk)
        tr.collective("AllGather", ALU.bypass, A["d_agv_in"], A["d_agv_out"], b_agv)
    tr.barrier()

    with contextlib.ExitStack() as st:
        s_w, b_w = zip(*[C.alloc(f"l1q_w{i}", [128, NDT, 128], BF16, st) for i in range(2)])
        s_r1, b_r1 = C.alloc("l1q_r1", [128, 512], F32, st)
        s_r2, b_r2 = C.alloc("l1q_r2", [128, 512], F32, st)
        rbufs = (s_r1, b_r1, s_r2, b_r2)
        s_qt, b_qt = zip(*[C.alloc(f"l1q_t{i}", [128, T], BF16, st) for i in range(2)])
        p_y, b_py = zip(*[C.palloc(f"l1q_p{i}", [128, 512], st) for i in range(2)])
        for ot in range(NDT):
            wb = ot % 2
            tr.dma("q_pool", s_w[wb][:],
                   A["wq1"].rearrange("(dt p) o -> p dt o", p=128)[:, :, ot*128:(ot+1)*128],
                   writes=[b_w[wb]])
            for tc in range(4):
                pb = tc % 2
                sl = slice(tc*512, (tc+1)*512)
                tr.mmgroup([(lambda e, dt=dt, pb=pb, wb=wb, sl=sl:
                             e.matmul(p_y[pb][:], s_w[wb][:, dt, :], s_xnT[:, dt, sl],
                                      start=(dt == 0), stop=(dt == NDT-1)))
                            for dt in range(NDT)],
                           reads=[b_w[wb], b_xnT], writes=[b_py[pb]])
                rope_evac(st, p_y[pb], b_py[pb], sl, s_qt[wb][:, sl], b_qt[wb], rbufs)
            tr.dma("q_act", A["d_agq_in"].rearrange("(h p) t -> p h t", p=128)[:, ot, :],
                   s_qt[wb][:], reads=[b_qt[wb]])
        toks = [(sname, c) for sname, c in tr.counts.items() if sname.startswith("q_act")]
        tr.wait("pool", toks)
        tr.collective("AllGather", ALU.bypass, A["d_agq_in"], A["d_agq_out"], b_agq)
    tr.barrier()

    with contextlib.ExitStack() as st:
        s_w8, b_w8 = zip(*[C.alloc(f"l1w8_{i}", [128, NDT, 128], U8, st) for i in range(2)])
        s_wb_, b_wb_ = zip(*[C.alloc(f"l1wb_{i}", [128, NDT, 128], BF16, st) for i in range(2)])
        for o2t in (() if getattr(C, "preconv_done", False) else range(NDT)):
            wb = o2t % 2
            tr.dma("q_sync", s_w8[wb][:], A["k1r"][o2t].rearrange("dt p o -> p dt o"),
                   writes=[b_w8[wb]])
            tr.op("dve", lambda e, wb=wb: e.tensor_scalar(s_wb_[wb][:], s_w8[wb][:], 2.0, -1.0, ALU.mult, ALU.add),
                  reads=[b_w8[wb]], writes=[b_wb_[wb]])
            tr.dma("q_act", A["d_w1bf"][o2t].rearrange("dt p o -> p dt o"), s_wb_[wb][:],
                   reads=[b_wb_[wb]])
        for d2t in (() if getattr(C, "preconv_done", False) else range(NDT)):
            wb = d2t % 2
            tr.dma("q_sync", s_w8[wb][:].rearrange("p dt o -> p (dt o)"),
                   A["k2r"][d2t], writes=[b_w8[wb]])
            tr.op("dve", lambda e, wb=wb: e.tensor_scalar(
                    s_wb_[wb][:].rearrange("p dt o -> p (dt o)"),
                    s_w8[wb][:].rearrange("p dt o -> p (dt o)"), 2.0, -1.0, ALU.mult, ALU.add),
                  reads=[b_w8[wb]], writes=[b_wb_[wb]])
            tr.dma("q_act", A["d_w2bf"][d2t], s_wb_[wb][:].rearrange("p dt o -> p (dt o)"),
                  reads=[b_wb_[wb]])
        s_t1, b_t1 = C.alloc("l1m1_t1", [128, 512], F32, st)
        s_h1, b_h1 = zip(*[C.alloc(f"l1_h1{i}", [128, T], BF16, st) for i in range(2)])
        p_y, b_py = zip(*[C.palloc(f"l1m1_p{i}", [128, 512], st) for i in range(2)])
        for ot in range(NDT):
            wb = ot % 2
            tr.dma("q_sync", s_w8[wb][:], A["k0r"][ot].rearrange("dt p o -> p dt o"),
                   writes=[b_w8[wb]])
            tr.op("dve", lambda e, wb=wb: e.tensor_scalar(s_wb_[wb][:], s_w8[wb][:], 2.0, -1.0, ALU.mult, ALU.add),
                  reads=[b_w8[wb]], writes=[b_wb_[wb]])
            for tc in range(4):
                pb = tc % 2
                sl = slice(tc*512, (tc+1)*512)
                tr.mmgroup([(lambda e, dt=dt, pb=pb, wb=wb, sl=sl:
                             e.matmul(p_y[pb][:], s_wb_[wb][:, dt, :], s_xnT[:, dt, sl],
                                      start=(dt == 0), stop=(dt == NDT-1)))
                            for dt in range(NDT)],
                           reads=[b_wb_[wb], b_xnT], writes=[b_py[pb]])
                tr.op("act", lambda e, pb=pb: e.activation(s_t1[:], p_y[pb][:], AF.Relu, scale=s_s[:, 0:1]),
                      reads=[b_py[pb], b_s], writes=[b_t1])
                tr.op("dve", lambda e, wb=wb, sl=sl, ot=ot: e.tensor_scalar(s_h1[wb][:, sl], s_t1[:], s_cv[:, ot:ot+1], None, ALU.mult),
                      reads=[b_t1, b_cv], writes=[b_h1[wb]])
            tr.dma("q_act", A["d_h1"][ot], s_h1[wb][:], reads=[b_h1[wb]])
    tr.barrier()
    era.close()

    with contextlib.ExitStack() as st:
        s_h1c, b_h1c = C.alloc("l1_h1c", [128, NDT, 512], BF16, st)
        s_h2c, b_h2c = C.alloc("l1_h2c", [128, NDT, 512], BF16, st)
        s_w1b, b_w1b = zip(*[C.alloc(f"l1w1b{i}", [128, NDT, 128], BF16, st) for i in range(2)])
        s_w2b, b_w2b = C.alloc("l1w2b", [128, NDT, 512], BF16, st)
        s_t1, b_t1 = C.alloc("l1m2_t1", [128, 512], F32, st)
        s_gs1, b_gs1 = zip(*[C.alloc(f"l1_gs1{i}", [128, 512], BF16, st) for i in range(2)])
        p_y, b_py = zip(*[C.palloc(f"l1m2_p{i}", [128, 512], st) for i in range(2)])
        for tc in range(4):
            sl = slice(tc*512, (tc+1)*512)
            tr.dma("q_sync", s_h1c[:], A["d_h1"][:, :, sl].rearrange("ot p t -> p ot t"),
                   writes=[b_h1c])
            for o2t in range(NDT):
                wb = o2t % 2
                tr.dma("q_pool", s_w1b[wb][:], A["d_w1bf"][o2t].rearrange("dt p o -> p dt o"),
                       writes=[b_w1b[wb]])
                pb = o2t % 2
                tr.mmgroup([(lambda e, dt=dt, pb=pb, wb=wb:
                             e.matmul(p_y[pb][:], s_w1b[wb][:, dt, :], s_h1c[:, dt, :],
                                      start=(dt == 0), stop=(dt == NDT-1)))
                            for dt in range(NDT)],
                           reads=[b_w1b[wb], b_h1c], writes=[b_py[pb]])
                tr.op("act", lambda e, pb=pb: e.activation(s_t1[:], p_y[pb][:], AF.Relu, scale=s_s[:, 1:2]),
                      reads=[b_py[pb], b_s], writes=[b_t1])
                tr.op("dve", lambda e, o2t=o2t: e.tensor_scalar(s_h2c[:, o2t, :], s_t1[:], s_cv[:, o2t:o2t+1], None, ALU.mult),
                      reads=[b_t1, b_cv], writes=[b_h2c])
            for cc in range(8):
                csl = slice(cc*512, (cc+1)*512)
                tr.dma("q_pool", s_w2b[:], A["d_w2bf"][:, :, csl].rearrange("dt p c -> p dt c"),
                       writes=[b_w2b])
                for t4 in range(4):
                    tt = tc*4 + t4
                    pb = t4 % 2
                    tsl = slice(t4*128, (t4+1)*128)
                    tr.mmgroup([(lambda e, dt=dt, pb=pb, tsl=tsl:
                                 e.matmul(p_y[pb][:], s_h2c[:, dt, tsl], s_w2b[:, dt, :],
                                          start=(dt == 0), stop=(dt == NDT-1)))
                                for dt in range(NDT)],
                               reads=[b_h2c, b_w2b], writes=[b_py[pb]])
                    gb = t4 % 2
                    tr.op("act", lambda e, pb=pb, gb=gb: e.activation(s_gs1[gb][:], p_y[pb][:], AF.Silu, scale=s_s[:, 2:3]),
                          reads=[b_py[pb], b_s], writes=[b_gs1[gb]])
                    tr.dma("q_act", A["d_gs1"][tt, :, csl], s_gs1[gb][:], reads=[b_gs1[gb]])
    tr.barrier()

    def gtile(t):
        return t if t < 8 else (t + 8 if t < 24 else t - 16)
    def gchunk(c):
        return c if c < 2 else (c + 2 if c < 6 else c - 4)
    at_stack = contextlib.ExitStack()
    s_aT, b_aT = C.alloc("l1_aT", [128, 16, 4096], BF16, at_stack)
    b_qmy, b_kmy, b_vmy = Buf(), Buf(), Buf()
    tr.wait("pool", [b_agk.w, b_agv.w, b_agq.w])
    rv = nc.gpsimd.snap(rkreg, min_val=0, max_val=1)
    tr.dma("q_pool", A["d_qmy"][:, 0:T], A["d_agq_out"][bass.ds(rv*2048, 2048), :],
           reads=[b_agq], writes=[b_qmy])
    tr.dma("q_pool", A["d_qmy"][:, T:2*T], A["d_agq_out"][bass.ds(rv*2048 + 4096, 2048), :],
           reads=[b_agq], writes=[b_qmy])
    tr.dma("q_pool", A["d_kmy"][:, 0:T], A["d_agk_out"][bass.ds(rv*512, 512), :],
           reads=[b_agk], writes=[b_kmy])
    tr.dma("q_pool", A["d_kmy"][:, T:2*T], A["d_agk_out"][bass.ds(rv*512 + 1024, 512), :],
           reads=[b_agk], writes=[b_kmy])
    tr.dma("q_pool", A["d_vmy"], A["d_agv_out"][:, bass.ds(rv*512, 512)],
           reads=[b_agv], writes=[b_vmy])
    with contextlib.ExitStack() as st:
        s_kth, b_kth = C.alloc("l1_kth", [128, 4096], BF16, st)
        s_vh, b_vh = C.alloc("l1_vh", [128, 32, 128], BF16, st)
        s_qh, b_qh = zip(*[C.alloc(f"l1_qh{i}", [128, 4096], BF16, st) for i in range(2)])
        s_msk, b_msk = C.alloc("l1_msk", [128, 4, 512], BF16, st)
        tr.dma("q_sync", s_msk[:], A["masks"].rearrange("m p f -> p m f"), writes=[b_msk])
        s_pT, b_pT = zip(*[C.alloc(f"l1_pT{i}", [128, 512], BF16, st) for i in range(3)])
        s_pTm, b_pTm = zip(*[C.alloc(f"l1_pTm{i}", [128, 512], BF16, st) for i in range(3)])
        s_rl, b_rl = C.alloc("l1_rl", [1, 512], F32, st)
        s_rlb, b_rlb = C.alloc("l1_rlb", [1, 512], BF16, st)
        s_rb, b_rb = C.alloc("l1_rb", [128, 512], F32, st)
        p_st, b_pst = zip(*[C.palloc(f"l1_pst{i}", [128, 512], st) for i in range(3)])
        p_pv, b_ppv = zip(*[C.palloc(f"l1_ppv{i}", [128, 512], st) for i in range(2)])
        p_l, b_pl = zip(*[C.palloc(f"l1_pl{i}", [1, 512], st) for i in range(2)])
        p_bc, b_pbc = C.palloc("l1_pbc", [128, 512], st)
        ucnt = [0]
        pend = [None]

        def emit_norm():
            if pend[0] is None:
                return
            hh, gsl, pvb = pend[0]
            pend[0] = None
            tr.op("dve", lambda e: e.reciprocal(s_rl[:], p_l[pvb][:]),
                  reads=[b_pl[pvb]], writes=[b_rl])
            tr.op("dve", lambda e: e.tensor_copy(s_rlb[:], s_rl[:]),
                  reads=[b_rl], writes=[b_rlb])
            tr.mmgroup([(lambda e: e.matmul(p_bc[:], C.s_ones_row[:], s_rlb[:],
                                            start=True, stop=True))],
                       reads=[C.b_ones_row, b_rlb], writes=[b_pbc])
            tr.op("act", lambda e: e.copy(s_rb[:], p_bc[:]), reads=[b_pbc], writes=[b_rb])
            tr.op("dve", lambda e: e.tensor_tensor(s_aT[:, hh, gsl], p_pv[pvb][:], s_rb[:], ALU.mult),
                  reads=[b_ppv[pvb], b_rb], writes=[b_aT])

        for kvl in range(4):
            tr.dma("q_sync", s_kth[:], A["d_kmy"][kvl*128:(kvl+1)*128, :],
                   reads=[b_kmy], writes=[b_kth])
            tr.dma("q_sync", s_vh[:],
                   A["d_vmy"].rearrange("(kt p) v -> p kt v", p=128)[:, :, kvl*128:(kvl+1)*128],
                   reads=[b_vmy], writes=[b_vh])
            for r4 in range(4):
                hh = kvl * 4 + r4
                hb = r4 % 2
                tr.dma("q_sync", s_qh[hb][:], A["d_qmy"][hh*128:(hh+1)*128, :],
                       reads=[b_qmy], writes=[b_qh[hb]])
                for qc in range(8):
                    q0 = qc * 512
                    gsl = slice(gchunk(qc)*512, gchunk(qc)*512 + 512)
                    nkt = (q0 + 512) // 128
                    pvb = ucnt[0] % 2
                    prev = [None]

                    def flush_prev(kt, nkt=nkt, pvb=pvb):
                        if prev[0] is None:
                            return
                        src, b_src, pkt = prev[0]
                        prev[0] = None
                        tr.mmgroup([(lambda e: e.matmul(p_pv[pvb][:], s_vh[:, gtile(pkt), :], src[:],
                                                        start=(pkt == 0), stop=(pkt == nkt-1)))],
                                   reads=[b_vh, b_src], writes=[b_ppv[pvb]])
                        tr.mmgroup([(lambda e: e.matmul(p_l[pvb][:], C.s_ones_col[:], src[:],
                                                        start=(pkt == 0), stop=(pkt == nkt-1)))],
                                   reads=[C.b_ones_col, b_src], writes=[b_pl[pvb]])

                    for kt in range(nkt):
                        ktg = gtile(kt)
                        sb = kt % 3
                        tr.mmgroup([(lambda e, ktg=ktg, sb=sb, hb=hb, gsl=gsl:
                                     e.matmul(p_st[sb][:], s_kth[:, ktg*128:(ktg+1)*128], s_qh[hb][:, gsl],
                                              start=True, stop=True))],
                                   reads=[b_kth, b_qh[hb]], writes=[b_pst[sb]])
                        tr.op("act", lambda e, sb=sb: e.activation(s_pT[sb][:], p_st[sb][:], AF.Exp, scale=ISCALE),
                              reads=[b_pst[sb]], writes=[b_pT[sb]])
                        src, b_src = s_pT[sb], b_pT[sb]
                        di = kt - q0 // 128
                        if di >= 0:
                            tr.op("dve", lambda e, sb=sb, di=di: e.tensor_tensor(s_pTm[sb][:], s_pT[sb][:], s_msk[:, di, :], ALU.mult),
                                  reads=[b_pT[sb], b_msk], writes=[b_pTm[sb]])
                            src, b_src = s_pTm[sb], b_pTm[sb]
                        if kt == 2:
                            emit_norm()
                        flush_prev(kt)
                        prev[0] = (src, b_src, kt)
                    flush_prev(nkt)
                    if pend[0] is not None:
                        emit_norm()
                    pend[0] = (hh, gsl, pvb)
                    ucnt[0] += 1
        emit_norm()
        if DBG:
            dbg_aT = nc.dram_tensor("d_aTdump", [128, 16, 4096], BF16, kind="ExternalOutput").ap()
            tr.dma("q_act", dbg_aT, s_aT[:], reads=[b_aT])
        tr.dma("q_act", A["d_aga_in"].rearrange("(h p) t -> p h t", p=128),
               s_aT[:], reads=[b_aT])
        toks = [(sname, c) for sname, c in tr.counts.items() if sname.startswith("q_act")]
        tr.wait("pool", toks)
        tr.collective("AllGather", ALU.bypass, A["d_aga_in"], A["d_aga_out"], b_aga)
    tr.barrier()
    at_stack.close()

    with contextlib.ExitStack() as st:
        s_aT2, b_aT2 = C.alloc("l1_aT2", [128, NH, T], BF16, st)
        tr.wait("pool", [b_aga.w])
        rv = nc.gpsimd.snap(rkreg, min_val=0, max_val=1)
        tr.dma("q_pool", s_aT2[:],
               A["d_aga_out"].rearrange("(h p) t -> p h t", p=128)[:, :, bass.ds(rv*T, T)],
               reads=[b_aga], writes=[b_aT2])
        s_wo2, b_wo2 = zip(*[C.alloc(f"l1_wo{i}", [128, NH, 512], BF16, st) for i in range(2)])
        s_xc, b_xc = C.alloc("l1_xc", [128, 512], F32, st)
        s_g1, b_g1 = C.alloc("l1_g1", [128, 512], BF16, st)
        s_c2, b_c2 = C.alloc("l1_c2", [128, 512], F32, st)
        s_x2, b_x2 = zip(*[C.alloc(f"l1_x2{i}", [128, 512], F32, st) for i in range(2)])
        p_s, b_ps = zip(*[C.palloc(f"l1_ps{i}", [128, 512], st) for i in range(2)])
        for cc in range(8):
            csl = slice(cc*512, (cc+1)*512)
            wob = cc % 2
            s_wo, b_wo = s_wo2[wob], b_wo2[wob]
            tr.dma("q_pool", s_wo[:],
                   A["wo1"].rearrange("(ot p) c -> p ot c", p=128)[:, :, csl],
                   writes=[b_wo])
            for tt in range(NT):
                pb = tt % 2
                tsl = slice(tt*128, (tt+1)*128)
                tr.mmgroup([(lambda e, ot=ot, pb=pb, tsl=tsl, s_wo=s_wo:
                             e.matmul(p_s[pb][:], s_aT2[:, ot, tsl], s_wo[:, ot, :],
                                      start=(ot == 0), stop=(ot == NH-1)))
                            for ot in range(NH)],
                           reads=[b_aT2, b_wo], writes=[b_ps[pb]])
                tr.dma("q_sync", s_xc[:], x1_ap[tsl, csl], writes=[b_xc])
                tr.dma("q_sync", s_g1[:], A["d_gs1"][tt, :, csl], writes=[b_g1])
                tr.op("dve", lambda e, pb=pb: e.tensor_mul(s_c2[:], p_s[pb][:], s_g1[:]),
                      reads=[b_ps[pb], b_g1], writes=[b_c2])
                xb = tt % 2
                tr.op("dve", lambda e, xb=xb: e.tensor_add(s_x2[xb][:], s_c2[:], s_xc[:]),
                      reads=[b_c2, b_xc], writes=[b_x2[xb]])
                tr.dma("q_act", out_ap[tsl, csl], s_x2[xb][:], reads=[b_x2[xb]])
    tr.barrier()
    lay.close()


DBG = False

def _scratch(nc, name, shape, dtype):
    kind = "ExternalOutput" if (DBG and name in (
        "d_h", "d_phiq", "d_gs0", "d_xnT", "d_zsg",
        "d_h1", "d_gs1", "d_w1bf", "d_qmy", "d_kmy", "d_vmy", "d_wdbf")) else "Internal"
    return nc.dram_tensor(name, list(shape), dtype, kind=kind).ap()


def _common_dram(nc, A, need_l0, need_l1):
    ext = []
    def inp(name, shape, dtype):
        A[name] = nc.dram_tensor(name, list(shape), dtype, kind="ExternalInput").ap()
        ext.append(name)
    inp("ident", (128, 128), BF16)
    if need_l0:
        inp("x", (T, D), F32)
        inp("wqT", (D, D), F32); inp("wkT", (D, 1024), F32)
        inp("wvT", (D, 1024), F32); inp("woT", (D, D), F32)
        inp("bq_c", (128, 32), F32); inp("bk_c", (128, 8), F32)
        inp("bv_c", (128, 8), F32); inp("bo", (1, D), F32)
        inp("qg", (NOI, D, 64), U8); inp("qu", (NOI, D, 64), U8)
        inp("sge", (NOI, 32, 64), F32); inp("sgo", (NOI, 32, 64), F32)
        inp("sue", (NOI, 32, 64), F32); inp("suo", (NOI, 32, 64), F32)
        inp("sgef", (32, DI // 2), F32); inp("sgof", (32, DI // 2), F32)
        inp("suef", (32, DI // 2), F32); inp("suof", (32, DI // 2), F32)
        inp("zg8", (32, DI // 2), U8); inp("zu8", (32, DI // 2), U8)
        inp("qd", (8, DI, 256), U8)
        inp("sd_b", (8, NOI, 512), F32)
        inp("sdef", (NOI, 2048), F32); inp("sdof", (NOI, 2048), F32)
        inp("zd8", (NOI, 2048), U8)
        inp("stair32", (128, 63), BF16); inp("stair86", (128, 171), BF16)
        A["d_h"] = _scratch(nc, "d_h", (NOI, 128, T), BF16)
        A["d_phiq"] = _scratch(nc, "d_phiq", (NDT, 128, T), BF16)
        A["d_gs0"] = _scratch(nc, "d_gs0", (NT, 128, D), BF16)
        A["d_zsg"] = _scratch(nc, "d_zsg", (32, DI), BF16)
        A["d_zsu"] = _scratch(nc, "d_zsu", (32, DI), BF16)
        A["d_wdbf"] = _scratch(nc, "d_wdbf", (8, DI, 512), BF16)
        A["d_cc0in"] = _scratch(nc, "d_cc0in", (129, 1024), F32)
        A["d_cc0out"] = _scratch(nc, "d_cc0out", (129, 1024), F32)
    if need_l1:
        inp("wq1", (D, D), F32); inp("wk1", (D, 1024), F32)
        inp("wv1", (D, 1024), F32); inp("wo1", (D, D), F32)
        inp("k0r", (NDT, NDT, 128, 128), U8)
        inp("k1r", (NDT, NDT, 128, 128), U8)
        inp("k2r", (NDT, 128, D), U8)
        inp("s012", (1, 4), F32)
        inp("cvec_c", (128, NDT), F32)
        inp("cos_c", (128, T), F32)
        inp("sin_c", (128, T), F32)
        inp("masks", (4, 128, 512), BF16)
        inp("rk", (1, 1), mybir.dt.uint32)
        A["d_agk_in"] = _scratch(nc, "d_agk_in", (1024, T), BF16)
        A["d_agk_out"] = _scratch(nc, "d_agk_out", (2048, T), BF16)
        A["d_agv_in"] = _scratch(nc, "d_agv_in", (T, 1024), BF16)
        A["d_agv_out"] = _scratch(nc, "d_agv_out", (2 * T, 1024), BF16)
        A["d_agq_in"] = _scratch(nc, "d_agq_in", (4096, T), BF16)
        A["d_agq_out"] = _scratch(nc, "d_agq_out", (8192, T), BF16)
        A["d_aga_in"] = _scratch(nc, "d_aga_in", (2048, 2 * T), BF16)
        A["d_aga_out"] = _scratch(nc, "d_aga_out", (4096, 2 * T), BF16)
        A["d_qmy"] = _scratch(nc, "d_qmy", (2048, 2 * T), BF16)
        A["d_kmy"] = _scratch(nc, "d_kmy", (512, 2 * T), BF16)
        A["d_vmy"] = _scratch(nc, "d_vmy", (2 * T, 512), BF16)
        A["d_h1"] = _scratch(nc, "d_h1", (NDT, 128, T), BF16)
        A["d_w1bf"] = _scratch(nc, "d_w1bf", (NDT, NDT, 128, 128), BF16)
        A["d_w2bf"] = _scratch(nc, "d_w2bf", (NDT, 128, D), BF16)
        A["d_gs1"] = _scratch(nc, "d_gs1", (NT, 128, D), BF16)
    return ext


def build_l0():
    nc = bass.Bass("TRN2", target_bir_lowering=False)
    A = {}
    ext = _common_dram(nc, A, True, False)
    x1 = nc.dram_tensor("x1", [T, D], F32, kind="ExternalOutput").ap()
    C = Ctx(nc, A)
    emit_layer0(C, A["x"], x1)
    C.tr.barrier()
    C.stack.close()
    C.tr.ctx.close()
    return nc, ext


def build_fused():
    nc = bass.Bass("TRN2", target_bir_lowering=False)
    A = {}
    ext = _common_dram(nc, A, True, True)
    A["d_x1"] = _scratch(nc, "d_x1", (T, D), F32)
    out = nc.dram_tensor("xout", [T, D], F32, kind="ExternalOutput").ap()
    C = Ctx(nc, A)
    emit_layer0(C, A["x"], A["d_x1"], preconv_l1=True)
    emit_layer1(C, A["d_x1"], out)
    C.tr.barrier()
    C.stack.close()
    C.tr.ctx.close()
    return nc, ext


def build_l1():
    nc = bass.Bass("TRN2", target_bir_lowering=False)
    A = {}
    ext = _common_dram(nc, A, False, True)
    x1 = nc.dram_tensor("x1in", [T, D], F32, kind="ExternalInput").ap()
    ext.append("x1in")
    out = nc.dram_tensor("xout", [T, D], F32, kind="ExternalOutput").ap()
    C = Ctx(nc, A)
    emit_layer1(C, x1, out)
    C.tr.barrier()
    C.stack.close()
    C.tr.ctx.close()
    return nc, ext



def _host_shared(inputs):
    S = {}
    S["ident"] = np.eye(128, dtype=BF16NP)
    f32 = lambda a: np.ascontiguousarray(np.asarray(a), dtype=np.float32)
    S["wqT"] = f32(np.asarray(inputs["L0_attn_Wq"]).T)
    S["wkT"] = f32(np.asarray(inputs["L0_attn_Wk"]).T)
    S["wvT"] = f32(np.asarray(inputs["L0_attn_Wv"]).T)
    S["woT"] = f32(np.asarray(inputs["L0_attn_Wo"]).T)
    S["bq_c"] = f32(np.asarray(inputs["L0_attn_bq"]).reshape(32, 128).T)
    S["bk_c"] = f32(np.asarray(inputs["L0_attn_bk"]).reshape(8, 128).T)
    S["bv_c"] = f32(np.asarray(inputs["L0_attn_bv"]).reshape(8, 128).T)
    S["bo"] = f32(np.asarray(inputs["L0_attn_bo"]).reshape(1, D))
    for nm, qk, sk, zk in (("g", "L0_ffn_gate_qweight", "L0_ffn_gate_scales", "L0_ffn_gate_qzeros"),
                           ("u", "L0_ffn_up_qweight", "L0_ffn_up_scales", "L0_ffn_up_qzeros")):
        qb = np.ascontiguousarray(np.asarray(inputs[qk])).view(np.uint8)
        S["q" + nm] = np.ascontiguousarray(qb.reshape(D, NOI, 64).transpose(1, 0, 2))
        sc = f32(np.asarray(inputs[sk]))
        s4 = sc.reshape(32, NOI, 64, 2)
        S["s" + nm + "e"] = np.ascontiguousarray(s4[:, :, :, 0].transpose(1, 0, 2))
        S["s" + nm + "o"] = np.ascontiguousarray(s4[:, :, :, 1].transpose(1, 0, 2))
        S["s" + nm + "ef"] = np.ascontiguousarray(sc[:, 0::2])
        S["s" + nm + "of"] = np.ascontiguousarray(sc[:, 1::2])
        S["z" + nm + "8"] = np.ascontiguousarray(np.asarray(inputs[zk])).view(np.uint8)
    db = np.ascontiguousarray(np.asarray(inputs["L0_ffn_down_qweight"])).view(np.uint8)
    S["qd"] = np.ascontiguousarray(db.reshape(DI, 8, 256).transpose(1, 0, 2))
    ds = f32(np.asarray(inputs["L0_ffn_down_scales"]))
    S["sd_b"] = np.ascontiguousarray(ds.reshape(NOI, 8, 512).transpose(1, 0, 2))
    S["sdef"] = np.ascontiguousarray(ds[:, 0::2])
    S["sdof"] = np.ascontiguousarray(ds[:, 1::2])
    S["zd8"] = np.ascontiguousarray(np.asarray(inputs["L0_ffn_down_qzeros"])).view(np.uint8)
    st32 = np.zeros((128, 63), dtype=BF16NP); st32[:, 31] = 1
    S["stair32"] = st32
    st86 = np.zeros((128, 171), dtype=BF16NP); st86[:, 85] = 1
    S["stair86"] = st86
    S["wq1"] = f32(inputs["L1_attn_Wq"])
    S["wk1"] = f32(inputs["L1_attn_Wk"])
    S["wv1"] = f32(inputs["L1_attn_Wv"])
    S["wo1"] = f32(inputs["L1_attn_Wo"])
    for nm, key in (("k0r", "L1_ffn_k0"), ("k1r", "L1_ffn_k1")):
        ku = np.ascontiguousarray(np.asarray(inputs[key])).view(np.uint8)
        S[nm] = np.ascontiguousarray(ku.reshape(NDT, 128, NDT, 128).transpose(2, 0, 1, 3))
    k2u = np.ascontiguousarray(np.asarray(inputs["L1_ffn_k2"])).view(np.uint8)
    S["k2r"] = np.ascontiguousarray(k2u.reshape(NDT, 128, D))
    S["s012"] = np.array([[float(np.asarray(inputs["L1_ffn_s0"])),
                           float(np.asarray(inputs["L1_ffn_s1"])),
                           float(np.asarray(inputs["L1_ffn_s2"])), 0.0]], dtype=np.float32)
    S["cvec_c"] = np.ascontiguousarray(
        np.where(np.arange(D) < 3277, 2.0, -8.0).astype(np.float32).reshape(NDT, 128).T)
    masks = np.zeros((4, 128, 512), dtype=BF16NP)
    for i in range(4):
        pp = np.arange(128)[:, None]
        ff = np.arange(512)[None, :]
        masks[i] = (ff >= pp + i * 128).astype(BF16NP)
    S["masks"] = masks
    inv = (10000.0 ** (-np.arange(64) / 64.0)).astype(np.float64)
    for h in (0, 1):
        glob = np.concatenate([np.arange(s0, s1) for (s0, s1) in _tok_slices(h)])
        ang = glob[None, :] * inv[:, None]
        cos = np.cos(ang).astype(np.float32)
        sin = np.sin(ang).astype(np.float32)
        S[f"cos_c{h}"] = np.ascontiguousarray(np.concatenate([cos, cos], axis=0))
        S[f"sin_c{h}"] = np.ascontiguousarray(np.concatenate([-sin, sin], axis=0))
    return S


def _tok_slices(h):
    if h == 0:
        return [(0, 1024), (3072, 4096)]
    return [(1024, 3072)]


def _host_percore(inputs, core):
    b, h = core // 2, core % 2
    x = np.asarray(inputs["L0_attn_hidden_states"])
    parts = [np.asarray(x[b, s0:s1, :], dtype=np.float32) for (s0, s1) in _tok_slices(h)]
    return {"x": np.ascontiguousarray(np.concatenate(parts, axis=0))}


def kernel(**inputs):
    S = _host_shared(inputs)
    nc, ext = build_fused()
    in_maps = []
    for c in range(8):
        hh = c % 2
        m = {k: S[k] for k in ext if k in S}
        m.update(_host_percore(inputs, c))
        m["cos_c"] = S[f"cos_c{hh}"]
        m["sin_c"] = S[f"sin_c{hh}"]
        m["rk"] = np.array([[hh]], dtype=np.uint32)
        in_maps.append(m)
    res = run_bass_kernel_spmd(nc, in_maps, core_ids=list(range(8)))
    out = np.zeros((4, 4096, 4096), dtype=np.float32)
    for c in range(8):
        b, hh = c // 2, c % 2
        xo = np.asarray(res.results[c]["xout"])
        ofs = 0
        for (s0, s1) in _tok_slices(hh):
            n = s1 - s0
            out[b, s0:s1, :] = xo[ofs:ofs+n]
            ofs += n
    return out


# version 05
# speedup vs baseline: 1.2525x; 1.0222x over previous
import contextlib
import numpy as np
import ml_dtypes
import concourse.bass as bass
import concourse.mybir as mybir
from concourse.bass_utils import run_bass_kernel_spmd

F32 = mybir.dt.float32
BF16 = mybir.dt.bfloat16
U8 = mybir.dt.uint8
AF = mybir.ActivationFunctionType
ALU = mybir.AluOpType
BF16NP = ml_dtypes.bfloat16

T = 2048
D = 4096
NT = T // 128
NDT = D // 128
DI = 11008
NOI = DI // 128
NKV = 8
NH = 32
EPS = 1e-5
GROUPS2 = [[0, 1], [2, 3], [4, 5], [6, 7]]
ISCALE = float(1.0 / np.sqrt(128.0))


class Buf:
    __slots__ = ("w", "r")
    def __init__(self):
        self.w = None
        self.r = []

class TK:
    def __init__(self, nc):
        self.nc = nc
        self.ctx = contextlib.ExitStack()
        self.sems = {}
        self.counts = {}
        self.satisfied = {}
        self.eng = {"pe": nc.tensor, "act": nc.scalar, "dve": nc.vector,
                    "pool": nc.gpsimd, "sync": nc.sync}
        for n in ("pe", "act", "dve", "pool"):
            self._new_sem(n)
        self.NSLOT = 8
        self.qrr = {}
        for q in ("q_sync", "q_pool", "q_act"):
            self.qrr[q] = 0
            for i in range(self.NSLOT):
                self._new_sem(f"{q}_{i}")
        self.ccn = 0

    def _new_sem(self, name):
        self.sems[name] = self.ctx.enter_context(self.nc.semaphore(name))
        self.counts[name] = 0


    def cc_sem(self):
        name = f"cc_{self.ccn}"
        self.ccn += 1
        self._new_sem(name)
        return name

    def wait(self, engine, toks):
        eng = self.eng[engine]
        need = {}
        for (s, v) in toks:
            if v <= 0 or self.satisfied.get((engine, s), 0) >= v:
                continue
            need[s] = max(need.get(s, 0), v)
        for s, v in need.items():
            eng.wait_ge(self.sems[s], v)
            self.satisfied[(engine, s)] = v

    def inc(self, inst, sem, by):
        inst.then_inc(self.sems[sem], by)
        self.counts[sem] += by
        return (sem, self.counts[sem])

    def barrier(self):
        toks = [(s, c) for s, c in self.counts.items()
                if c > 0 and not s.startswith("cc_")]
        for e in ("pe", "act", "dve", "pool", "sync"):
            self.wait(e, toks)

    def _pre(self, engine, reads, writes):
        toks = []
        for b in reads:
            if b.w: toks.append(b.w)
        for b in writes:
            if b.w: toks.append(b.w)
            toks.extend(b.r)
        self.wait(engine, toks)

    def _post(self, t, reads, writes):
        for b in writes:
            b.w = t; b.r = []
        for b in reads:
            b.r.append(t)
        return t

    def op(self, engine, fn, reads=(), writes=()):
        self._pre(engine, reads, writes)
        inst = fn(self.eng[engine])
        return self._post(self.inc(inst, engine, 1), reads, writes)

    def dma(self, q, out, in_, reads=(), writes=(), **kw):
        engine = {"q_sync": "sync", "q_pool": "pool", "q_act": "act"}[q]
        self._pre(engine, reads, writes)
        sem = f"{q}_{self.qrr[q]}"
        self.qrr[q] = (self.qrr[q] + 1) % self.NSLOT
        self.wait(engine, [(sem, self.counts[sem])])
        inst = self.eng[engine].dma_start(out=out, in_=in_, **kw)
        return self._post(self.inc(inst, sem, 16), reads, writes)

    def mmgroup(self, mms, reads=(), writes=()):
        self._pre("pe", reads, writes)
        inst = None
        for f in mms:
            inst = f(self.eng["pe"])
        return self._post(self.inc(inst, "pe", 1), reads, writes)

    def collective(self, kind, op, in_ap, out_ap, write_tok_buf):
        sem = self.cc_sem()
        inst = self.nc.gpsimd.collective_compute(
            kind, op, replica_groups=GROUPS2,
            ins=[in_ap.opt()], outs=[out_ap.opt()])
        inst.then_inc(self.sems[sem], 1)
        self.counts[sem] += 1
        write_tok_buf.w = (sem, 1)
        write_tok_buf.r = []
        return sem


class Ctx:
    def __init__(self, nc, A):
        self.nc, self.A = nc, A
        self.tr = TK(nc)
        self.stack = contextlib.ExitStack()
        sb = lambda n, s, d: self.stack.enter_context(nc.sbuf_tensor(n, s, d))
        self.s_ident = sb("c_ident", [128, 128], BF16)
        self.s_ones_col = sb("c_ones_col", [128, 1], BF16)
        self.s_ones_row = sb("c_ones_row", [1, 128], BF16)
        self.b_ident, self.b_ones_col, self.b_ones_row = Buf(), Buf(), Buf()
        self.tr.dma("q_sync", self.s_ident[:], A["ident"], writes=[self.b_ident])
        self.tr.op("pool", lambda e: e.memset(self.s_ones_col[:], 1.0), writes=[self.b_ones_col])
        self.tr.op("pool", lambda e: e.memset(self.s_ones_row[:], 1.0), writes=[self.b_ones_row])

    def alloc(self, name, shape, dtype, stack):
        return stack.enter_context(self.nc.sbuf_tensor(name, shape, dtype)), Buf()

    def palloc(self, name, shape, stack, dtype=F32):
        return stack.enter_context(self.nc.psum_tensor(name, shape, dtype)), Buf()



def emit_lnT(C, x_ap, s_xnT, b_xnT, tag):
    tr = C.tr
    with contextlib.ExitStack() as st:
        s_x, b_x, s_xnb, b_xnb = [], [], [], []
        for i in range(2):
            t, b = C.alloc(f"{tag}_x{i}", [128, D], F32, st); s_x.append(t); b_x.append(b)
            t, b = C.alloc(f"{tag}_n{i}", [128, D], BF16, st); s_xnb.append(t); b_xnb.append(b)
        s_junk1, b_junk1 = C.alloc(f"{tag}_junk", [128, D], F32, st)
        s_junk, b_junk = (s_junk1, s_junk1), (b_junk1, b_junk1)
        s_stt, b_stt = zip(*[C.alloc(f"{tag}_st{i}", [128, 8], F32, st) for i in range(2)])
        p_tr, b_ptr = zip(*[C.palloc(f"{tag}_p{i}", [128, 128], st, BF16) for i in range(2)])
        for tt in range(NT):
            xb = tt % 2
            s_st, b_st = s_stt[xb], b_stt[xb]
            s_junk2, b_junk2 = s_junk[xb], b_junk[xb]
            tr.dma("q_sync", s_x[xb][:], x_ap[tt*128:(tt+1)*128, :], writes=[b_x[xb]])
            tr.op("dve", lambda e: e.reduce_sum(s_st[:, 0:1], s_x[xb][:], axis=mybir.AxisListType.X),
                  reads=[b_x[xb]], writes=[b_st])
            tr.op("act", lambda e: e.activation(s_junk2[:], s_x[xb][:], AF.Square, accum_out=s_st[:, 1:2]),
                  reads=[b_x[xb]], writes=[b_junk2, b_st])
            tr.op("dve", lambda e: e.tensor_scalar(s_st[:, 2:3], s_st[:, 0:1], 1.0 / D, None, ALU.mult),
                  reads=[b_st], writes=[b_st])
            tr.op("dve", lambda e: e.tensor_scalar(s_st[:, 3:4], s_st[:, 1:2], 1.0 / D, float(EPS), ALU.mult, ALU.add),
                  reads=[b_st], writes=[b_st])
            tr.op("dve", lambda e: e.scalar_tensor_tensor(s_st[:, 4:5], s_st[:, 2:3], 1.0, s_st[:, 2:3], ALU.mult, ALU.mult),
                  reads=[b_st], writes=[b_st])
            tr.op("dve", lambda e: e.tensor_sub(s_st[:, 3:4], s_st[:, 3:4], s_st[:, 4:5]),
                  reads=[b_st], writes=[b_st])
            tr.op("act", lambda e: e.activation(s_st[:, 5:6], s_st[:, 3:4], AF.Ln),
                  reads=[b_st], writes=[b_st])
            tr.op("act", lambda e: e.activation(s_st[:, 6:7], s_st[:, 5:6], AF.Exp, scale=-0.5),
                  reads=[b_st], writes=[b_st])
            tr.op("dve", lambda e: e.scalar_tensor_tensor(s_st[:, 7:8], s_st[:, 2:3], -1.0, s_st[:, 6:7], ALU.mult, ALU.mult),
                  reads=[b_st], writes=[b_st])
            tr.op("dve", lambda e: e.tensor_scalar(s_xnb[xb][:], s_x[xb][:], s_st[:, 6:7], s_st[:, 7:8], ALU.mult, ALU.add),
                  reads=[b_x[xb], b_st], writes=[b_xnb[xb]])
            for dt in range(NDT):
                pb = dt % 2
                tr.op("pe", lambda e: e.transpose(p_tr[pb][:], s_xnb[xb][:, dt*128:(dt+1)*128], C.s_ident[:]),
                      reads=[b_xnb[xb], C.b_ident], writes=[b_ptr[pb]])
                tr.op("dve", lambda e: e.tensor_copy(s_xnT[:, dt, tt*128:(tt+1)*128], p_tr[pb][:]),
                      reads=[b_ptr[pb]], writes=[b_xnT])
    tr.barrier()


def emit_layer0(C, x_ap, x1_ap, preconv_l1=False):
    nc, tr, A = C.nc, C.tr, C.A
    lay = contextlib.ExitStack()
    s_sg86, b_sg86 = C.alloc("l0_sg86", [86, T], BF16, lay)
    b_cc0 = Buf()

    era = contextlib.ExitStack()
    s_xnT, b_xnT = C.alloc("l0_xnT", [128, NDT, T], BF16, era)
    s_bias, b_bias = C.alloc("l0_bias", [128, 48], F32, era)
    tr.dma("q_sync", s_bias[:, 0:32], A["bq_c"], writes=[b_bias])
    tr.dma("q_sync", s_bias[:, 32:40], A["bk_c"], writes=[b_bias])
    tr.dma("q_sync", s_bias[:, 40:48], A["bv_c"], writes=[b_bias])
    s_sg, b_sg = C.alloc("l0_sg", [32, T], BF16, era)

    emit_lnT(C, x_ap, s_xnT, b_xnT, "l0ln")
    if DBG:
        A["d_xnT"] = _scratch(nc, "d_xnT", (128, NDT, T), BF16)
        tr.dma("q_act", A["d_xnT"], s_xnT[:], reads=[b_xnT])
        tr.barrier()

    with contextlib.ExitStack() as st:
        s_w, b_w = zip(*[C.alloc(f"l0kv_w{i}", [128, NDT, 128], BF16, st) for i in range(2)])
        s_t1, b_t1 = C.alloc("l0kv_t1", [128, 512], F32, st)
        s_t2, b_t2 = C.alloc("l0kv_t2", [128, 512], F32, st)
        s_t3, b_t3 = C.alloc("l0kv_t3", [128, 512], F32, st)
        s_kT, b_kT = C.alloc("l0_kT", [128, T], BF16, st)
        s_vT, b_vT = C.alloc("l0_vT", [128, T], BF16, st)
        s_kx, b_kx = C.alloc("l0_kx", [128, NT, 128], BF16, st)
        s_vx, b_vx = C.alloc("l0_vx", [128, NT, 128], BF16, st)
        s_zero, b_zero = C.alloc("l0kv_zero", [128, 512], F32, st)
        tr.op("pool", lambda e: e.memset(s_zero[:], 0.0), writes=[b_zero])
        p_y, b_py = zip(*[C.palloc(f"l0kv_p{i}", [128, 512], st) for i in range(2)])
        p_t2, b_pt2 = zip(*[C.palloc(f"l0kv_pt{i}", [128, 128], st, BF16) for i in range(2)])
        p_kv, b_pkv = C.palloc("l0_pkv", [128, 1024], st)
        p_ks, b_pks = C.palloc("l0_pks", [1, 1024], st)

        for kh in range(NKV):
            for m, wname, bcol, s_dst, b_dst, do_phi in (
                    (0, "wkT", 32 + kh, s_kT, b_kT, True),
                    (1, "wvT", 40 + kh, s_vT, b_vT, False)):
                wb = m
                tr.dma("q_pool", s_w[wb][:],
                       A[wname].rearrange("(dt p) o -> p dt o", p=128)[:, :, kh*128:(kh+1)*128],
                       writes=[b_w[wb]])
                for tc in range(4):
                    pb = tc % 2
                    sl = slice(tc*512, (tc+1)*512)
                    tr.mmgroup([(lambda e, dt=dt, pb=pb, wb=wb, sl=sl:
                                 e.matmul(p_y[pb][:], s_w[wb][:, dt, :], s_xnT[:, dt, sl],
                                          start=(dt == 0), stop=(dt == NDT-1)))
                                for dt in range(NDT)],
                               reads=[b_w[wb], b_xnT], writes=[b_py[pb]])
                    if do_phi:
                        tr.op("dve", lambda e: e.tensor_scalar(s_t2[:], p_y[pb][:], s_bias[:, bcol:bcol+1], 0.0, ALU.add, ALU.min),
                              reads=[b_py[pb], b_bias], writes=[b_t2])
                        tr.op("act", lambda e: e.activation(s_t3[:], s_t2[:], AF.Exp),
                              reads=[b_t2], writes=[b_t3])
                        tr.op("dve", lambda e: e.scalar_tensor_tensor(s_t1[:], p_y[pb][:], s_bias[:, bcol:bcol+1], s_zero[:], ALU.add, ALU.max),
                              reads=[b_py[pb], b_bias, b_zero], writes=[b_t1])
                        tr.op("dve", lambda e: e.tensor_add(s_dst[:, sl], s_t1[:], s_t3[:]),
                              reads=[b_t1, b_t3], writes=[b_dst])
                    else:
                        tr.op("dve", lambda e: e.tensor_scalar(s_dst[:, sl], p_y[pb][:], s_bias[:, bcol:bcol+1], None, ALU.add),
                              reads=[b_py[pb], b_bias], writes=[b_dst])
            for s_src, b_src, s_d, b_d in ((s_kT, b_kT, s_kx, b_kx), (s_vT, b_vT, s_vx, b_vx)):
                for tt in range(NT):
                    pb = tt % 2
                    tr.op("pe", lambda e: e.transpose(p_t2[pb][:], s_src[:, tt*128:(tt+1)*128], C.s_ident[:]),
                          reads=[b_src, C.b_ident], writes=[b_pt2[pb]])
                    tr.op("dve", lambda e: e.tensor_copy(s_d[:, tt, :], p_t2[pb][:]),
                          reads=[b_pt2[pb]], writes=[b_d])
            sl_kv = slice(kh*128, (kh+1)*128)
            tr.mmgroup([(lambda e, tt=tt, sl_kv=sl_kv:
                         e.matmul(p_kv[:, sl_kv], s_kx[:, tt, :], s_vx[:, tt, :],
                                  start=(tt == 0), stop=(tt == NT-1)))
                        for tt in range(NT)],
                       reads=[b_kx, b_vx], writes=[b_pkv])
            tr.mmgroup([(lambda e, tt=tt, sl_kv=sl_kv:
                         e.matmul(p_ks[:, sl_kv], C.s_ones_col[:], s_kx[:, tt, :],
                                  start=(tt == 0), stop=(tt == NT-1)))
                        for tt in range(NT)],
                       reads=[b_kx, C.b_ones_col], writes=[b_pks])
        s_kv, b_kv = C.alloc("l0_skv", [128, 1024], F32, st)
        s_ks, b_ks = C.alloc("l0_sks", [1, 1024], F32, st)
        tr.op("act", lambda e: e.copy(s_kv[:], p_kv[:]), reads=[b_pkv], writes=[b_kv])
        tr.op("act", lambda e: e.copy(s_ks[:], p_ks[:]), reads=[b_pks], writes=[b_ks])
        b_ccin = Buf()
        tr.dma("q_sync", A["d_cc0in"][0:128, :], s_kv[:], reads=[b_kv], writes=[b_ccin])
        tr.dma("q_sync", A["d_cc0in"][128:129, :], s_ks[:], reads=[b_ks], writes=[b_ccin])
        tr.wait("pool", [b_ccin.w])
        tr.collective("AllReduce", ALU.add, A["d_cc0in"], A["d_cc0out"], b_cc0)
    tr.barrier()

    with contextlib.ExitStack() as st:
        s_stair32, b_st32 = C.alloc("l0_st32", [128, 63], BF16, st)
        tr.dma("q_sync", s_stair32[:], A["stair32"], writes=[b_st32])
        p_sg, b_psg = C.palloc("l0_psg", [32, T], st)
        for tc in range(4):
            sl = slice(tc*512, (tc+1)*512)
            tr.mmgroup([(lambda e, dt=dt, sl=sl:
                         e.matmul(p_sg[:, sl], s_stair32[:, 31-dt:63-dt], s_xnT[:, dt, sl],
                                  start=(dt == 0), stop=(dt == NDT-1)))
                        for dt in range(NDT)],
                       reads=[b_st32, b_xnT], writes=[b_psg])
        tr.op("act", lambda e: e.copy(s_sg[:], p_sg[:]), reads=[b_psg], writes=[b_sg])
        s_z8, b_z8 = C.alloc("l0_z8", [32, DI // 2], U8, st)
        s_zn, b_zn = C.alloc("l0_zn", [32, DI], U8, st)
        s_sf, b_sf = C.alloc("l0_sf", [32, DI // 2], F32, st)
        s_zs, b_zs = C.alloc("l0_zs", [32, DI], BF16, st)
        for nm in ("g", "u"):
            tr.dma("q_sync", s_z8[:], A[f"z{nm}8"], writes=[b_z8])
            tr.op("dve", lambda e: e.tensor_single_scalar(s_zn[:, 0:DI//2], s_z8[:], 15, ALU.bitwise_and),
                  reads=[b_z8], writes=[b_zn])
            tr.op("dve", lambda e: e.tensor_single_scalar(s_zn[:, DI//2:DI], s_z8[:], 4, ALU.logical_shift_right),
                  reads=[b_z8], writes=[b_zn])
            zv = s_zs[:].rearrange("p (n two) -> p n two", two=2)
            for par in (0, 1):
                snm = f"s{nm}ef" if par == 0 else f"s{nm}of"
                off = 0 if par == 0 else DI // 2
                tr.dma("q_sync", s_sf[:], A[snm], writes=[b_sf])
                tr.op("dve", lambda e, zv=zv, off=off, par=par: e.scalar_tensor_tensor(
                        zv[:, :, par:par+1],
                        s_zn[:, off:off+DI//2].rearrange("p (n o) -> p n o", o=1), -1.0,
                        s_sf[:].rearrange("p (n o) -> p n o", o=1), ALU.mult, ALU.mult),
                      reads=[b_zn, b_sf], writes=[b_zs])
            tr.dma("q_act", A[f"d_zs{nm}"], s_zs[:], reads=[b_zs])
    tr.barrier()

    with contextlib.ExitStack() as st:
        s_stair86, b_st86 = C.alloc("l0_st86", [128, 171], BF16, st)
        tr.dma("q_sync", s_stair86[:], A["stair86"], writes=[b_st86])
        bufs = {}
        for nm in ("g", "u"):
            e = {}
            t_, b_ = C.alloc(f"l0q8{nm}", [128, NDT, 64], U8, st)
            e["q8"], e["b_q8"] = (t_, t_), (b_, b_)
            e["nib"], e["b_nib"] = C.alloc(f"l0nib{nm}", [128, NDT, 128], U8, st)
            if nm == "g":
                e["wt"], e["b_wt"] = zip(*[C.alloc(f"l0wt{nm}{i}", [128, NDT, 128], BF16, st) for i in range(2)])
            else:
                t_, b_ = C.alloc(f"l0wt{nm}", [128, NDT, 128], BF16, st)
                e["wt"], e["b_wt"] = (t_, t_), (b_, b_)
            e["sc"], e["b_sc"] = C.alloc(f"l0sc{nm}", [128, 2, NDT, 64], BF16, st)
            e["zss"], e["b_zss"] = zip(*[C.alloc(f"l0zss{nm}{i}", [32, 128], BF16, st) for i in range(2)])
            e["p"], e["b_p"] = zip(*[C.palloc(f"l0p{nm}{i}", [128, 512], st) for i in range(2)])
            bufs[nm] = e
        p_s86, b_ps86 = C.palloc("l0_ps86", [86, T], st)
        s_silu, b_silu = C.alloc("l0_silu", [128, 512], F32, st)
        s_h, b_h = zip(*[C.alloc(f"l0_h{i}", [128, T], BF16, st) for i in range(2)])

        qsrc = {"g": "qg", "u": "qu"}
        ssrc = {"g": ("sge", "sgo"), "u": ("sue", "suo")}
        for ot in range(NOI):
            ob = ot % 2
            for nm in ("g", "u"):
                e = bufs[nm]
                tr.dma("q_sync", e["q8"][ob][:],
                       A[qsrc[nm]][ot].rearrange("(dt p) b -> p dt b", p=128),
                       writes=[e["b_q8"][ob]])
                se, so = ssrc[nm]
                tr.dma("q_pool", e["sc"][:, 0], A[se][ot].partition_broadcast(128), writes=[e["b_sc"]])
                tr.dma("q_pool", e["sc"][:, 1], A[so][ot].partition_broadcast(128), writes=[e["b_sc"]])
                tr.dma("q_sync", e["zss"][ob][:], A[f"d_zs{nm}"][:, ot*128:(ot+1)*128], writes=[e["b_zss"][ob]])
                tr.op("dve", lambda e2, e=e, ob=ob: e2.tensor_single_scalar(e["nib"][:, :, 0:64], e["q8"][ob][:], 15, ALU.bitwise_and),
                      reads=[e["b_q8"][ob]], writes=[e["b_nib"]])
                tr.op("dve", lambda e2, e=e, ob=ob: e2.tensor_single_scalar(e["nib"][:, :, 64:128], e["q8"][ob][:], 4, ALU.logical_shift_right),
                      reads=[e["b_q8"][ob]], writes=[e["b_nib"]])
                wv = e["wt"][ob][:].rearrange("p dt (n two) -> p dt n two", two=2)
                for par, bo in ((0, 0), (1, 64)):
                    tr.op("dve", lambda e2, e=e, wv=wv, par=par, bo=bo: e2.tensor_tensor(
                            wv[:, :, :, par:par+1],
                            e["nib"][:, :, bo:bo+64].rearrange("p dt (n o) -> p dt n o", o=1),
                            e["sc"][:, par].rearrange("p dt (n o) -> p dt n o", o=1), ALU.mult),
                          reads=[e["b_nib"], e["b_sc"]], writes=[e["b_wt"][ob]])
            for tc in range(4):
                sl = slice(tc*512, (tc+1)*512)
                for nm in ("g", "u"):
                    e = bufs[nm]
                    mms = [(lambda e2, e=e, dt=dt, ob=ob, sl=sl:
                            e2.matmul(e["p"][ob][:], e["wt"][ob][:, dt, :], s_xnT[:, dt, sl],
                                      start=(dt == 0), stop=False))
                           for dt in range(NDT)]
                    mms.append(lambda e2, e=e, ob=ob, sl=sl:
                               e2.matmul(e["p"][ob][:], e["zss"][ob][:], s_sg[:, sl],
                                         start=False, stop=True))
                    tr.mmgroup(mms, reads=[e["b_wt"][ob], b_xnT, e["b_zss"][ob], b_sg],
                               writes=[e["b_p"][ob]])
                tr.op("act", lambda e2: e2.activation(s_silu[:], bufs["g"]["p"][ob][:], AF.Silu),
                      reads=[bufs["g"]["b_p"][ob]], writes=[b_silu])
                tr.op("dve", lambda e2: e2.tensor_tensor(s_h[ob][:, sl], s_silu[:], bufs["u"]["p"][ob][:], ALU.mult),
                      reads=[b_silu, bufs["u"]["b_p"][ob]], writes=[b_h[ob]])
                tr.mmgroup([(lambda e2, ot=ot, sl=sl, ob=ob:
                             e2.matmul(p_s86[:, sl], s_stair86[:, 85-ot:171-ot], s_h[ob][:, sl],
                                       start=(ot == 0), stop=(ot == NOI-1)))],
                           reads=[b_st86, b_h[ob]], writes=[b_ps86])
            tr.dma("q_act", A["d_h"][ot], s_h[ob][:], reads=[b_h[ob]])
        tr.op("act", lambda e: e.copy(s_sg86[:], p_s86[:]), reads=[b_ps86], writes=[b_sg86])
    tr.barrier()

    with contextlib.ExitStack() as st:
        s_w, b_w = zip(*[C.alloc(f"l0q_w{i}", [128, NDT, 128], BF16, st) for i in range(2)])
        s_t1, b_t1 = C.alloc("l0q_t1", [128, 512], F32, st)
        s_t2, b_t2 = C.alloc("l0q_t2", [128, 512], F32, st)
        s_t3, b_t3 = C.alloc("l0q_t3", [128, 512], F32, st)
        s_phi, b_phi = zip(*[C.alloc(f"l0q_f{i}", [128, T], BF16, st) for i in range(2)])
        s_zero, b_zero = C.alloc("l0q_zero", [128, 512], F32, st)
        tr.op("pool", lambda e: e.memset(s_zero[:], 0.0), writes=[b_zero])
        p_y, b_py = zip(*[C.palloc(f"l0q_p{i}", [128, 512], st) for i in range(2)])
        for ot in range(NDT):
            wb = ot % 2
            tr.dma("q_pool", s_w[wb][:],
                   A["wqT"].rearrange("(dt p) o -> p dt o", p=128)[:, :, ot*128:(ot+1)*128],
                   writes=[b_w[wb]])
            for tc in range(4):
                pb = tc % 2
                sl = slice(tc*512, (tc+1)*512)
                tr.mmgroup([(lambda e, dt=dt, pb=pb, wb=wb, sl=sl:
                             e.matmul(p_y[pb][:], s_w[wb][:, dt, :], s_xnT[:, dt, sl],
                                      start=(dt == 0), stop=(dt == NDT-1)))
                            for dt in range(NDT)],
                           reads=[b_w[wb], b_xnT], writes=[b_py[pb]])
                tr.op("dve", lambda e: e.tensor_scalar(s_t2[:], p_y[pb][:], s_bias[:, ot:ot+1], 0.0, ALU.add, ALU.min),
                      reads=[b_py[pb], b_bias], writes=[b_t2])
                tr.op("act", lambda e: e.activation(s_t3[:], s_t2[:], AF.Exp),
                      reads=[b_t2], writes=[b_t3])
                tr.op("dve", lambda e: e.scalar_tensor_tensor(s_t1[:], p_y[pb][:], s_bias[:, ot:ot+1], s_zero[:], ALU.add, ALU.max),
                      reads=[b_py[pb], b_bias, b_zero], writes=[b_t1])
                tr.op("dve", lambda e: e.tensor_add(s_phi[wb][:, sl], s_t1[:], s_t3[:]),
                      reads=[b_t1, b_t3], writes=[b_phi[wb]])
            tr.dma("q_act", A["d_phiq"][ot], s_phi[wb][:], reads=[b_phi[wb]])
    tr.barrier()
    era.close()

    with contextlib.ExitStack() as st:
        with contextlib.ExitStack() as pst:
            s_zsd, b_zsd = C.alloc("l0_zsd", [86, 4096], BF16, st)
            s_zd8, b_zd8 = C.alloc("l0_zd8", [86, 2048], U8, pst)
            s_zdn, b_zdn = C.alloc("l0_zdn", [86, 4096], U8, pst)
            s_sdf, b_sdf = C.alloc("l0_sdf", [86, 2048], F32, pst)
            tr.dma("q_sync", s_zd8[:], A["zd8"], writes=[b_zd8])
            tr.op("dve", lambda e: e.tensor_single_scalar(s_zdn[:, 0:2048], s_zd8[:], 15, ALU.bitwise_and),
                  reads=[b_zd8], writes=[b_zdn])
            tr.op("dve", lambda e: e.tensor_single_scalar(s_zdn[:, 2048:4096], s_zd8[:], 4, ALU.logical_shift_right),
                  reads=[b_zd8], writes=[b_zdn])
            zv = s_zsd[:].rearrange("p (n two) -> p n two", two=2)
            for par in (0, 1):
                snm = "sdef" if par == 0 else "sdof"
                off = 0 if par == 0 else 2048
                tr.dma("q_sync", s_sdf[:], A[snm], writes=[b_sdf])
                tr.op("dve", lambda e, par=par, off=off: e.scalar_tensor_tensor(
                        zv[:, :, par:par+1],
                        s_zdn[:, off:off+2048].rearrange("p (n o) -> p n o", o=1),
                        -1.0,
                        s_sdf[:].rearrange("p (n o) -> p n o", o=1), ALU.mult, ALU.mult),
                      reads=[b_zdn, b_sdf], writes=[b_zsd])
            tr.barrier()
        QB = (22, 22, 22, 20)
        QO = (0, 22, 44, 66)
        s_qd8b1, b_qd8b1 = C.alloc("l0_qd8b", [128, 22, 256], U8, st)
        s_qd8b, b_qd8b = (s_qd8b1, s_qd8b1), (b_qd8b1, b_qd8b1)
        s_dnb, b_dnb = C.alloc("l0_dnb", [128, 22, 512], U8, st)
        s_scdb, b_scdb = C.alloc("l0_scdb", [128, 22, 512], BF16, st)
        s_wdt, b_wdt = C.alloc("l0_wdt", [128, 22, 512], BF16, st)
        b_wdbf = [Buf() for _ in range(8)]

        def deq_step(cc, qi):
            nq, oq = QB[qi], QO[qi]
            qb = qi % 2
            tr.dma("q_sync", s_qd8b[qb][:, 0:nq, :],
                   A["qd"][cc, oq*128:(oq+nq)*128, :].rearrange("(ot p) b -> p ot b", p=128),
                   writes=[b_qd8b[qb]])
            tr.dma("q_pool", s_scdb[:, 0:nq, :],
                   A["sd_b"][cc, oq:oq+nq, :].partition_broadcast(128), writes=[b_scdb])
            tr.op("dve", lambda e: e.tensor_single_scalar(s_dnb[:, 0:nq, 0:256], s_qd8b[qb][:, 0:nq, :], 15, ALU.bitwise_and),
                  reads=[b_qd8b[qb]], writes=[b_dnb])
            tr.op("dve", lambda e: e.tensor_single_scalar(s_dnb[:, 0:nq, 256:512], s_qd8b[qb][:, 0:nq, :], 4, ALU.logical_shift_right),
                  reads=[b_qd8b[qb]], writes=[b_dnb])
            wv = s_wdt[:, 0:nq, :].rearrange("p ot (n two) -> p ot n two", two=2)
            sv = s_scdb[:, 0:nq, :].rearrange("p ot (n two) -> p ot n two", two=2)
            for par, bo in ((0, 0), (1, 256)):
                tr.op("dve", lambda e, wv=wv, sv=sv, par=par, bo=bo: e.tensor_tensor(
                        wv[:, :, :, par:par+1],
                        s_dnb[:, 0:nq, bo:bo+256].rearrange("p ot (n o) -> p ot n o", o=1),
                        sv[:, :, :, par:par+1], ALU.mult),
                      reads=[b_dnb, b_scdb], writes=[b_wdt])
            tr.dma("q_act", A["d_wdbf"][cc].rearrange("(ot p) c2 -> p ot c2", p=128)[:, oq:oq+nq, :],
                   s_wdt[:, 0:nq, :], reads=[b_wdt], writes=[b_wdbf[cc]])

        s_wd, b_wd = C.alloc("l0_wd", [128, NOI, 512], BF16, st)
        s_ht, b_ht = zip(*[C.alloc(f"l0_ht{i}", [128, NOI, 128], BF16, st) for i in range(2)])
        s_gs, b_gs = zip(*[C.alloc(f"l0_gs{i}", [128, 512], BF16, st) for i in range(2)])
        p_d, b_pd = zip(*[C.palloc(f"l0_pd{i}", [128, 512], st) for i in range(2)])

        for qi in range(4):
            deq_step(0, qi)
        for cc in range(8):
            csl = slice(cc*512, (cc+1)*512)
            if cc + 1 < 8:
                for qi in range(4):
                    deq_step(cc + 1, qi)
            tr.dma("q_act", s_wd[:], A["d_wdbf"][cc].rearrange("(ot p) c2 -> p ot c2", p=128),
                   reads=[b_wdbf[cc]], writes=[b_wd])
            if DBG and cc == 0:
                dbg_wd2 = nc.dram_tensor("d_wd2dump", [128, NOI, 512], BF16, kind="ExternalOutput").ap()
                tr.dma("q_act", dbg_wd2, s_wd[:], reads=[b_wd])
                dbg_zsd2 = nc.dram_tensor("d_zsd2dump", [86, 4096], BF16, kind="ExternalOutput").ap()
                tr.dma("q_act", dbg_zsd2, s_zsd[:], reads=[b_zsd])
                dbg_sg2 = nc.dram_tensor("d_sg2dump", [86, T], BF16, kind="ExternalOutput").ap()
                tr.dma("q_act", dbg_sg2, s_sg86[:], reads=[b_sg86])
            for tt in range(NT):
                hb = tt % 2
                tr.dma("q_sync" if hb else "q_pool", s_ht[hb][:],
                       A["d_h"][:, :, tt*128:(tt+1)*128].rearrange("ot p t -> p ot t"),
                       writes=[b_ht[hb]])
                pb = tt % 2
                mms = [(lambda e, ot=ot, hb=hb, pb=pb:
                        e.matmul(p_d[pb][:], s_ht[hb][:, ot, :], s_wd[:, ot, :],
                                 start=(ot == 0), stop=False))
                       for ot in range(NOI)]
                mms.append(lambda e, pb=pb, tt=tt, csl=csl:
                           e.matmul(p_d[pb][:], s_sg86[:, tt*128:(tt+1)*128], s_zsd[:, csl],
                                    start=False, stop=True))
                tr.mmgroup(mms, reads=[b_ht[hb], b_wd, b_sg86, b_zsd], writes=[b_pd[pb]])
                tr.op("act", lambda e, pb=pb: e.activation(s_gs[pb][:], p_d[pb][:], AF.Silu),
                      reads=[b_pd[pb]], writes=[b_gs[pb]])
                tr.dma("q_act", A["d_gs0"][tt, :, csl], s_gs[pb][:], reads=[b_gs[pb]])
    tr.barrier()

    s_aT, b_aT = C.alloc("l0_aT", [128, NH, T], BF16, lay)
    with contextlib.ExitStack() as st:
        s_kvb, b_kvb = C.alloc("l0_kvb", [128, 1024], BF16, st)
        s_rk, b_rk = C.alloc("l0_rk", [128, 8], F32, st)
        tr.wait("pool", [b_cc0.w])
        if DBG:
            dbg_cc = nc.dram_tensor("d_ccdump", [129, 1024], F32, kind="ExternalOutput").ap()
            tr.dma("q_sync", dbg_cc, A["d_cc0out"], reads=[b_cc0])
        tr.dma("q_pool", s_kvb[:], A["d_cc0out"][0:128, :], reads=[b_cc0], writes=[b_kvb])
        with C.nc.allow_non_contiguous_dma("tiny ksum transpose load"):
            tr.dma("q_sync", s_rk[:],
                   A["d_cc0out"][128:129, :].rearrange("o (h d) -> d (o h)", h=8),
                   reads=[b_cc0], writes=[b_rk])
        tr.op("dve", lambda e: e.tensor_scalar(s_rk[:], s_rk[:], 1e-10, None, ALU.add),
              reads=[b_rk], writes=[b_rk])
        tr.op("dve", lambda e: e.reciprocal(s_rk[:], s_rk[:]), reads=[b_rk], writes=[b_rk])
        s_ph, b_ph = zip(*[C.alloc(f"l0_ph{i}", [128, T], BF16, st) for i in range(2)])
        p_a, b_pa = zip(*[C.palloc(f"l0_pa{i}", [128, 512], st) for i in range(2)])
        for h in range(NH):
            hb = h % 2
            kvh = h // 4
            tr.dma("q_sync", s_ph[hb][:], A["d_phiq"][h], writes=[b_ph[hb]])
            for tc in range(4):
                pb = tc % 2
                sl = slice(tc*512, (tc+1)*512)
                tr.mmgroup([(lambda e, pb=pb, hb=hb, kvh=kvh, sl=sl:
                             e.matmul(p_a[pb][:], s_kvb[:, kvh*128:(kvh+1)*128], s_ph[hb][:, sl],
                                      start=True, stop=True))],
                           reads=[b_kvb, b_ph[hb]], writes=[b_pa[pb]])
                tr.op("dve", lambda e, pb=pb, h=h, sl=sl, kvh=kvh: e.tensor_scalar(
                        s_aT[:, h, sl], p_a[pb][:], s_rk[:, kvh:kvh+1], None, ALU.mult),
                      reads=[b_pa[pb], b_rk], writes=[b_aT])
        if preconv_l1:
            A2 = C.A
            s_w8c, b_w8c = zip(*[C.alloc(f"l0_w8c{i}", [128, NDT, 128], U8, st) for i in range(2)])
            s_wbc, b_wbc = C.alloc("l0_wbc", [128, NDT, 128], BF16, st)
            for o2t in range(NDT):
                wb8 = o2t % 2
                tr.dma("q_pool", s_w8c[wb8][:], A2["k1r"][o2t].rearrange("dt p o -> p dt o"),
                       writes=[b_w8c[wb8]])
                tr.op("dve", lambda e, wb8=wb8: e.tensor_scalar(s_wbc[:], s_w8c[wb8][:], 2.0, -1.0, ALU.mult, ALU.add),
                      reads=[b_w8c[wb8]], writes=[b_wbc])
                tr.dma("q_sync" if o2t % 2 else "q_act", A2["d_w1bf"][o2t].rearrange("dt p o -> p dt o"), s_wbc[:],
                       reads=[b_wbc])
            for d2t in range(NDT):
                wb8 = d2t % 2
                tr.dma("q_pool", s_w8c[wb8][:].rearrange("p dt o -> p (dt o)"), A2["k2r"][d2t],
                       writes=[b_w8c[wb8]])
                tr.op("dve", lambda e, wb8=wb8: e.tensor_scalar(
                        s_wbc[:].rearrange("p dt o -> p (dt o)"),
                        s_w8c[wb8][:].rearrange("p dt o -> p (dt o)"), 2.0, -1.0, ALU.mult, ALU.add),
                      reads=[b_w8c[wb8]], writes=[b_wbc])
                tr.dma("q_sync" if d2t % 2 else "q_act", A2["d_w2bf"][d2t], s_wbc[:].rearrange("p dt o -> p (dt o)"),
                       reads=[b_wbc])
            C.preconv_done = True
        s_wo, b_wo = C.alloc("l0_wo", [128, NH, 512], BF16, st)
        s_bo, b_bo = C.alloc("l0_bo", [128, 512], F32, st)
        s_xc, b_xc = C.alloc("l0_xc", [128, 512], F32, st)
        s_g0, b_g0 = C.alloc("l0_g0", [128, 512], BF16, st)
        s_c1, b_c1 = C.alloc("l0_c1", [128, 512], F32, st)
        s_c2, b_c2 = C.alloc("l0_c2", [128, 512], F32, st)
        s_x1, b_x1 = zip(*[C.alloc(f"l0_x1{i}", [128, 512], F32, st) for i in range(2)])
        p_s, b_ps = zip(*[C.palloc(f"l0_ps{i}", [128, 512], st) for i in range(2)])
        for cc in range(8):
            csl = slice(cc*512, (cc+1)*512)
            tr.dma("q_pool", s_wo[:],
                   A["woT"].rearrange("(ot p) c -> p ot c", p=128)[:, :, csl],
                   writes=[b_wo])
            tr.dma("q_sync", s_bo[:].rearrange("p (o n) -> p o n", o=1),
                   A["bo"][0:1, csl].partition_broadcast(128), writes=[b_bo])
            for tt in range(NT):
                pb = tt % 2
                tsl = slice(tt*128, (tt+1)*128)
                tr.mmgroup([(lambda e, ot=ot, pb=pb, tsl=tsl:
                             e.matmul(p_s[pb][:], s_aT[:, ot, tsl], s_wo[:, ot, :],
                                      start=(ot == 0), stop=(ot == NH-1)))
                            for ot in range(NH)],
                           reads=[b_aT, b_wo], writes=[b_ps[pb]])
                tr.dma("q_sync", s_xc[:], x_ap[tsl, csl], writes=[b_xc])
                tr.dma("q_sync", s_g0[:], A["d_gs0"][tt, :, csl], writes=[b_g0])
                tr.op("dve", lambda e, pb=pb: e.tensor_add(s_c1[:], p_s[pb][:], s_bo[:]),
                      reads=[b_ps[pb], b_bo], writes=[b_c1])
                tr.op("dve", lambda e: e.tensor_mul(s_c2[:], s_c1[:], s_g0[:]),
                      reads=[b_c1, b_g0], writes=[b_c2])
                xb = tt % 2
                tr.op("dve", lambda e, xb=xb: e.tensor_add(s_x1[xb][:], s_c2[:], s_xc[:]),
                      reads=[b_c2, b_xc], writes=[b_x1[xb]])
                tr.dma("q_act", x1_ap[tsl, csl], s_x1[xb][:], reads=[b_x1[xb]])
    tr.barrier()
    lay.close()




def emit_layer1(C, x1_ap, out_ap):
    nc, tr, A = C.nc, C.tr, C.A
    lay = contextlib.ExitStack()
    b_agk, b_agv, b_agq, b_aga = Buf(), Buf(), Buf(), Buf()
    s_rk, b_rk = C.alloc("l1_rk", [1, 1], mybir.dt.uint32, lay)
    tr.dma("q_sync", s_rk[:], A["rk"], writes=[b_rk])
    rkreg = nc.gpsimd.alloc_register("rkreg")
    tr.op("pool", lambda e: e.reg_load(rkreg, s_rk[0:1, 0:1]), reads=[b_rk])
    rv = nc.gpsimd.snap(rkreg, min_val=0, max_val=1)
    s_s, b_s = C.alloc("l1_s", [128, 4], F32, lay)
    s_cv, b_cv = C.alloc("l1_cv", [128, NDT], F32, lay)
    tr.dma("q_sync", s_s[:].rearrange("p (o n) -> p o n", o=1),
           A["s012"][0:1, :].partition_broadcast(128), writes=[b_s])
    tr.dma("q_sync", s_cv[:], A["cvec_c"], writes=[b_cv])

    era = contextlib.ExitStack()
    s_xnT, b_xnT = C.alloc("l1_xnT", [128, NDT, T], BF16, era)
    emit_lnT(C, x1_ap, s_xnT, b_xnT, "l1ln")
    if DBG:
        dbg_xn1 = nc.dram_tensor("d_xn1T", [128, NDT, T], BF16, kind="ExternalOutput").ap()
        tr.dma("q_act", dbg_xn1, s_xnT[:], reads=[b_xnT])

    s_cos, b_cos = C.alloc("l1_cos", [128, T], F32, era)
    s_sin, b_sin = C.alloc("l1_sin", [128, T], F32, era)
    tr.dma("q_sync", s_cos[:], A["cos_c"], writes=[b_cos])
    tr.dma("q_sync", s_sin[:], A["sin_c"], writes=[b_sin])

    def rope_evac(st, p_y, b_py, sl, dst, b_dst, bufs):
        s_r1, b_r1, s_r2, b_r2 = bufs
        tr.op("dve", lambda e: e.tensor_tensor(s_r1[:], p_y[:], s_cos[:, sl], ALU.mult),
              reads=[b_py, b_cos], writes=[b_r1])
        tr.op("dve", lambda e: e.tensor_tensor(s_r2[0:64, :], p_y[64:128, :], s_sin[0:64, sl], ALU.mult),
              reads=[b_py, b_sin], writes=[b_r2])
        tr.op("dve", lambda e: e.tensor_tensor(s_r2[64:128, :], p_y[0:64, :], s_sin[64:128, sl], ALU.mult),
              reads=[b_py, b_sin], writes=[b_r2])
        tr.op("dve", lambda e: e.tensor_add(dst, s_r1[:], s_r2[:]),
              reads=[b_r1, b_r2], writes=[b_dst])

    with contextlib.ExitStack() as st:
        s_w, b_w = zip(*[C.alloc(f"l1kv_w{i}", [128, NDT, 128], BF16, st) for i in range(2)])
        s_r1, b_r1 = C.alloc("l1kv_r1", [128, 512], F32, st)
        s_r2, b_r2 = C.alloc("l1kv_r2", [128, 512], F32, st)
        rbufs = (s_r1, b_r1, s_r2, b_r2)
        s_kT, b_kT = C.alloc("l1_kT", [128, T], BF16, st)
        s_vT, b_vT = C.alloc("l1_vT", [128, T], BF16, st)
        s_vx, b_vx = C.alloc("l1_vx", [128, NT, 128], BF16, st)
        p_y, b_py = zip(*[C.palloc(f"l1kv_p{i}", [128, 512], st) for i in range(2)])
        p_t2, b_pt2 = zip(*[C.palloc(f"l1kv_pt{i}", [128, 128], st, BF16) for i in range(2)])
        for kh in range(NKV):
            for m, wname in ((0, "wk1"), (1, "wv1")):
                wb = m
                tr.dma("q_pool", s_w[wb][:],
                       A[wname].rearrange("(dt p) o -> p dt o", p=128)[:, :, kh*128:(kh+1)*128],
                       writes=[b_w[wb]])
                for tc in range(4):
                    pb = tc % 2
                    sl = slice(tc*512, (tc+1)*512)
                    tr.mmgroup([(lambda e, dt=dt, pb=pb, wb=wb, sl=sl:
                                 e.matmul(p_y[pb][:], s_w[wb][:, dt, :], s_xnT[:, dt, sl],
                                          start=(dt == 0), stop=(dt == NDT-1)))
                                for dt in range(NDT)],
                               reads=[b_w[wb], b_xnT], writes=[b_py[pb]])
                    if m == 0:
                        rope_evac(st, p_y[pb], b_py[pb], sl, s_kT[:, sl], b_kT, rbufs)
                    else:
                        tr.op("dve", lambda e, pb=pb, sl=sl: e.tensor_copy(s_vT[:, sl], p_y[pb][:]),
                              reads=[b_py[pb]], writes=[b_vT])
            tr.dma("q_act", A["d_agk_in"].rearrange("(h p) t -> p h t", p=128)[:, kh, :],
                   s_kT[:], reads=[b_kT])
            for tt in range(NT):
                pb = tt % 2
                tr.op("pe", lambda e, tt=tt, pb=pb: e.transpose(p_t2[pb][:], s_vT[:, tt*128:(tt+1)*128], C.s_ident[:]),
                      reads=[b_vT, C.b_ident], writes=[b_pt2[pb]])
                tr.op("dve", lambda e, tt=tt, pb=pb: e.tensor_copy(s_vx[:, tt, :], p_t2[pb][:]),
                      reads=[b_pt2[pb]], writes=[b_vx])
            tr.dma("q_act", A["d_agv_in"].rearrange("(tt p) v -> p tt v", p=128)[:, :, kh*128:(kh+1)*128],
                   s_vx[:], reads=[b_vx])
        toks = [(sname, c) for sname, c in tr.counts.items() if sname.startswith("q_act")]
        tr.wait("pool", toks)
        tr.collective("AllGather", ALU.bypass, A["d_agk_in"], A["d_agk_out"], b_agk)
        tr.collective("AllGather", ALU.bypass, A["d_agv_in"], A["d_agv_out"], b_agv)
    tr.barrier()

    with contextlib.ExitStack() as st:
        s_w, b_w = zip(*[C.alloc(f"l1q_w{i}", [128, NDT, 128], BF16, st) for i in range(2)])
        s_r1, b_r1 = C.alloc("l1q_r1", [128, 512], F32, st)
        s_r2, b_r2 = C.alloc("l1q_r2", [128, 512], F32, st)
        rbufs = (s_r1, b_r1, s_r2, b_r2)
        s_qt, b_qt = zip(*[C.alloc(f"l1q_t{i}", [128, T], BF16, st) for i in range(2)])
        p_y, b_py = zip(*[C.palloc(f"l1q_p{i}", [128, 512], st) for i in range(2)])
        for ot in range(NDT):
            wb = ot % 2
            tr.dma("q_pool", s_w[wb][:],
                   A["wq1"].rearrange("(dt p) o -> p dt o", p=128)[:, :, ot*128:(ot+1)*128],
                   writes=[b_w[wb]])
            for tc in range(4):
                pb = tc % 2
                sl = slice(tc*512, (tc+1)*512)
                tr.mmgroup([(lambda e, dt=dt, pb=pb, wb=wb, sl=sl:
                             e.matmul(p_y[pb][:], s_w[wb][:, dt, :], s_xnT[:, dt, sl],
                                      start=(dt == 0), stop=(dt == NDT-1)))
                            for dt in range(NDT)],
                           reads=[b_w[wb], b_xnT], writes=[b_py[pb]])
                rope_evac(st, p_y[pb], b_py[pb], sl, s_qt[wb][:, sl], b_qt[wb], rbufs)
            tr.dma("q_act", A["d_agq_in"].rearrange("(h p) t -> p h t", p=128)[:, ot, :],
                   s_qt[wb][:], reads=[b_qt[wb]])
        toks = [(sname, c) for sname, c in tr.counts.items() if sname.startswith("q_act")]
        tr.wait("pool", toks)
        tr.collective("AllGather", ALU.bypass, A["d_agq_in"], A["d_agq_out"], b_agq)
    tr.barrier()

    with contextlib.ExitStack() as st:
        s_w8, b_w8 = zip(*[C.alloc(f"l1w8_{i}", [128, NDT, 128], U8, st) for i in range(2)])
        s_wb_, b_wb_ = zip(*[C.alloc(f"l1wb_{i}", [128, NDT, 128], BF16, st) for i in range(2)])
        for o2t in (() if getattr(C, "preconv_done", False) else range(NDT)):
            wb = o2t % 2
            tr.dma("q_sync", s_w8[wb][:], A["k1r"][o2t].rearrange("dt p o -> p dt o"),
                   writes=[b_w8[wb]])
            tr.op("dve", lambda e, wb=wb: e.tensor_scalar(s_wb_[wb][:], s_w8[wb][:], 2.0, -1.0, ALU.mult, ALU.add),
                  reads=[b_w8[wb]], writes=[b_wb_[wb]])
            tr.dma("q_act", A["d_w1bf"][o2t].rearrange("dt p o -> p dt o"), s_wb_[wb][:],
                   reads=[b_wb_[wb]])
        for d2t in (() if getattr(C, "preconv_done", False) else range(NDT)):
            wb = d2t % 2
            tr.dma("q_sync", s_w8[wb][:].rearrange("p dt o -> p (dt o)"),
                   A["k2r"][d2t], writes=[b_w8[wb]])
            tr.op("dve", lambda e, wb=wb: e.tensor_scalar(
                    s_wb_[wb][:].rearrange("p dt o -> p (dt o)"),
                    s_w8[wb][:].rearrange("p dt o -> p (dt o)"), 2.0, -1.0, ALU.mult, ALU.add),
                  reads=[b_w8[wb]], writes=[b_wb_[wb]])
            tr.dma("q_act", A["d_w2bf"][d2t], s_wb_[wb][:].rearrange("p dt o -> p (dt o)"),
                  reads=[b_wb_[wb]])
        s_t1, b_t1 = C.alloc("l1m1_t1", [128, 512], F32, st)
        s_h1, b_h1 = zip(*[C.alloc(f"l1_h1{i}", [128, T], BF16, st) for i in range(2)])
        p_y, b_py = zip(*[C.palloc(f"l1m1_p{i}", [128, 512], st) for i in range(2)])
        for ot in range(NDT):
            wb = ot % 2
            tr.dma("q_sync", s_w8[wb][:], A["k0r"][ot].rearrange("dt p o -> p dt o"),
                   writes=[b_w8[wb]])
            tr.op("dve", lambda e, wb=wb: e.tensor_scalar(s_wb_[wb][:], s_w8[wb][:], 2.0, -1.0, ALU.mult, ALU.add),
                  reads=[b_w8[wb]], writes=[b_wb_[wb]])
            for tc in range(4):
                pb = tc % 2
                sl = slice(tc*512, (tc+1)*512)
                tr.mmgroup([(lambda e, dt=dt, pb=pb, wb=wb, sl=sl:
                             e.matmul(p_y[pb][:], s_wb_[wb][:, dt, :], s_xnT[:, dt, sl],
                                      start=(dt == 0), stop=(dt == NDT-1)))
                            for dt in range(NDT)],
                           reads=[b_wb_[wb], b_xnT], writes=[b_py[pb]])
                tr.op("act", lambda e, pb=pb: e.activation(s_t1[:], p_y[pb][:], AF.Relu, scale=s_s[:, 0:1]),
                      reads=[b_py[pb], b_s], writes=[b_t1])
                tr.op("dve", lambda e, wb=wb, sl=sl, ot=ot: e.tensor_scalar(s_h1[wb][:, sl], s_t1[:], s_cv[:, ot:ot+1], None, ALU.mult),
                      reads=[b_t1, b_cv], writes=[b_h1[wb]])
            tr.dma("q_act", A["d_h1"][ot], s_h1[wb][:], reads=[b_h1[wb]])
    tr.barrier()
    era.close()

    with contextlib.ExitStack() as st:
        s_h1c, b_h1c = C.alloc("l1_h1c", [128, NDT, 512], BF16, st)
        s_h2c, b_h2c = C.alloc("l1_h2c", [128, NDT, 512], BF16, st)
        s_w1b, b_w1b = zip(*[C.alloc(f"l1w1b{i}", [128, NDT, 128], BF16, st) for i in range(2)])
        s_w2b, b_w2b = C.alloc("l1w2b", [128, NDT, 512], BF16, st)
        s_t1, b_t1 = C.alloc("l1m2_t1", [128, 512], F32, st)
        s_gs1, b_gs1 = zip(*[C.alloc(f"l1_gs1{i}", [128, 512], BF16, st) for i in range(2)])
        p_y, b_py = zip(*[C.palloc(f"l1m2_p{i}", [128, 512], st) for i in range(2)])
        for tc in range(4):
            sl = slice(tc*512, (tc+1)*512)
            tr.dma("q_sync", s_h1c[:], A["d_h1"][:, :, sl].rearrange("ot p t -> p ot t"),
                   writes=[b_h1c])
            for o2t in range(NDT):
                wb = o2t % 2
                tr.dma("q_pool", s_w1b[wb][:], A["d_w1bf"][o2t].rearrange("dt p o -> p dt o"),
                       writes=[b_w1b[wb]])
                pb = o2t % 2
                tr.mmgroup([(lambda e, dt=dt, pb=pb, wb=wb:
                             e.matmul(p_y[pb][:], s_w1b[wb][:, dt, :], s_h1c[:, dt, :],
                                      start=(dt == 0), stop=(dt == NDT-1)))
                            for dt in range(NDT)],
                           reads=[b_w1b[wb], b_h1c], writes=[b_py[pb]])
                tr.op("act", lambda e, pb=pb: e.activation(s_t1[:], p_y[pb][:], AF.Relu, scale=s_s[:, 1:2]),
                      reads=[b_py[pb], b_s], writes=[b_t1])
                tr.op("dve", lambda e, o2t=o2t: e.tensor_scalar(s_h2c[:, o2t, :], s_t1[:], s_cv[:, o2t:o2t+1], None, ALU.mult),
                      reads=[b_t1, b_cv], writes=[b_h2c])
            for cc in range(8):
                csl = slice(cc*512, (cc+1)*512)
                tr.dma("q_pool", s_w2b[:], A["d_w2bf"][:, :, csl].rearrange("dt p c -> p dt c"),
                       writes=[b_w2b])
                for t4 in range(4):
                    tt = tc*4 + t4
                    pb = t4 % 2
                    tsl = slice(t4*128, (t4+1)*128)
                    tr.mmgroup([(lambda e, dt=dt, pb=pb, tsl=tsl:
                                 e.matmul(p_y[pb][:], s_h2c[:, dt, tsl], s_w2b[:, dt, :],
                                          start=(dt == 0), stop=(dt == NDT-1)))
                                for dt in range(NDT)],
                               reads=[b_h2c, b_w2b], writes=[b_py[pb]])
                    gb = t4 % 2
                    tr.op("act", lambda e, pb=pb, gb=gb: e.activation(s_gs1[gb][:], p_y[pb][:], AF.Silu, scale=s_s[:, 2:3]),
                          reads=[b_py[pb], b_s], writes=[b_gs1[gb]])
                    tr.dma("q_act", A["d_gs1"][tt, :, csl], s_gs1[gb][:], reads=[b_gs1[gb]])
    tr.barrier()

    def gtile(t):
        return t if t < 8 else (t + 8 if t < 24 else t - 16)
    def gchunk(c):
        return c if c < 2 else (c + 2 if c < 6 else c - 4)
    at_stack = contextlib.ExitStack()
    s_aT, b_aT = C.alloc("l1_aT", [128, 16, 4096], BF16, at_stack)
    b_qmy, b_kmy, b_vmy = Buf(), Buf(), Buf()
    tr.wait("pool", [b_agk.w, b_agv.w, b_agq.w])
    rv = nc.gpsimd.snap(rkreg, min_val=0, max_val=1)
    rvT = rv * 2048
    rvTb = rv * 2048 + 4096
    rv5 = rv * 512
    rv5b = rv * 512 + 1024
    tr.dma("q_pool", A["d_qmy"][:, 0:T], A["d_agq_out"][bass.ds(rvT, 2048), :],
           reads=[b_agq], writes=[b_qmy])
    tr.dma("q_pool", A["d_qmy"][:, T:2*T], A["d_agq_out"][bass.ds(rvTb, 2048), :],
           reads=[b_agq], writes=[b_qmy])
    tr.dma("q_pool", A["d_kmy"][:, 0:T], A["d_agk_out"][bass.ds(rv5, 512), :],
           reads=[b_agk], writes=[b_kmy])
    tr.dma("q_pool", A["d_kmy"][:, T:2*T], A["d_agk_out"][bass.ds(rv5b, 512), :],
           reads=[b_agk], writes=[b_kmy])
    tr.dma("q_pool", A["d_vmy"], A["d_agv_out"][:, bass.ds(rv5, 512)],
           reads=[b_agv], writes=[b_vmy])
    b_aga_q = [Buf() for _ in range(4)]
    b_amy = Buf()

    def spill_quarter(kq):
        tr.dma("q_act",
               A[f"d_aga_in{kq}"].rearrange("(h p) t -> p h t", p=128),
               s_aT[:, kq*4:(kq+1)*4, :], reads=[b_aT])
        toks = [(sname, c) for sname, c in tr.counts.items() if sname.startswith("q_act")]
        tr.wait("pool", toks)
        tr.collective("AllGather", ALU.bypass, A[f"d_aga_in{kq}"], A[f"d_aga_out{kq}"], b_aga_q[kq])
        tr.wait("pool", [b_aga_q[kq].w])
        tr.dma("q_pool", A["d_amy"][kq*1024:(kq+1)*1024, :],
               A[f"d_aga_out{kq}"][:, bass.ds(rvT, 2048)],
               reads=[b_aga_q[kq]], writes=[b_amy])
    with contextlib.ExitStack() as st:
        s_kth, b_kth = C.alloc("l1_kth", [128, 4096], BF16, st)
        s_vh, b_vh = C.alloc("l1_vh", [128, 32, 128], BF16, st)
        s_qh, b_qh = zip(*[C.alloc(f"l1_qh{i}", [128, 4096], BF16, st) for i in range(2)])
        s_msk, b_msk = C.alloc("l1_msk", [128, 4, 512], BF16, st)
        tr.dma("q_sync", s_msk[:], A["masks"].rearrange("m p f -> p m f"), writes=[b_msk])
        s_pT, b_pT = zip(*[C.alloc(f"l1_pT{i}", [128, 512], BF16, st) for i in range(3)])
        s_pTm, b_pTm = zip(*[C.alloc(f"l1_pTm{i}", [128, 512], BF16, st) for i in range(3)])
        s_rl, b_rl = C.alloc("l1_rl", [1, 512], F32, st)
        s_rlb, b_rlb = C.alloc("l1_rlb", [1, 512], BF16, st)
        s_rb, b_rb = C.alloc("l1_rb", [128, 512], F32, st)
        p_st, b_pst = zip(*[C.palloc(f"l1_pst{i}", [128, 512], st) for i in range(3)])
        p_pv, b_ppv = zip(*[C.palloc(f"l1_ppv{i}", [128, 512], st) for i in range(2)])
        p_l, b_pl = zip(*[C.palloc(f"l1_pl{i}", [1, 512], st) for i in range(2)])
        p_bc, b_pbc = C.palloc("l1_pbc", [128, 512], st)
        ucnt = [0]
        pend = [None]

        def emit_norm():
            if pend[0] is None:
                return
            hh, gsl, pvb = pend[0]
            pend[0] = None
            tr.op("dve", lambda e: e.reciprocal(s_rl[:], p_l[pvb][:]),
                  reads=[b_pl[pvb]], writes=[b_rl])
            tr.op("dve", lambda e: e.tensor_copy(s_rlb[:], s_rl[:]),
                  reads=[b_rl], writes=[b_rlb])
            tr.mmgroup([(lambda e: e.matmul(p_bc[:], C.s_ones_row[:], s_rlb[:],
                                            start=True, stop=True))],
                       reads=[C.b_ones_row, b_rlb], writes=[b_pbc])
            tr.op("act", lambda e: e.copy(s_rb[:], p_bc[:]), reads=[b_pbc], writes=[b_rb])
            tr.op("dve", lambda e: e.tensor_tensor(s_aT[:, hh, gsl], p_pv[pvb][:], s_rb[:], ALU.mult),
                  reads=[b_ppv[pvb], b_rb], writes=[b_aT])

        for kvl in range(4):
            if kvl >= 1:
                emit_norm()
                spill_quarter(kvl - 1)
            tr.dma("q_sync", s_kth[:], A["d_kmy"][kvl*128:(kvl+1)*128, :],
                   reads=[b_kmy], writes=[b_kth])
            tr.dma("q_sync", s_vh[:],
                   A["d_vmy"].rearrange("(kt p) v -> p kt v", p=128)[:, :, kvl*128:(kvl+1)*128],
                   reads=[b_vmy], writes=[b_vh])
            for r4 in range(4):
                hh = kvl * 4 + r4
                hb = r4 % 2
                tr.dma("q_sync", s_qh[hb][:], A["d_qmy"][hh*128:(hh+1)*128, :],
                       reads=[b_qmy], writes=[b_qh[hb]])
                for qc in range(8):
                    q0 = qc * 512
                    gsl = slice(gchunk(qc)*512, gchunk(qc)*512 + 512)
                    nkt = (q0 + 512) // 128
                    pvb = ucnt[0] % 2
                    prev = [None]

                    def flush_prev(kt, nkt=nkt, pvb=pvb):
                        if prev[0] is None:
                            return
                        src, b_src, pkt = prev[0]
                        prev[0] = None
                        tr.mmgroup([(lambda e: e.matmul(p_pv[pvb][:], s_vh[:, gtile(pkt), :], src[:],
                                                        start=(pkt == 0), stop=(pkt == nkt-1)))],
                                   reads=[b_vh, b_src], writes=[b_ppv[pvb]])
                        tr.mmgroup([(lambda e: e.matmul(p_l[pvb][:], C.s_ones_col[:], src[:],
                                                        start=(pkt == 0), stop=(pkt == nkt-1)))],
                                   reads=[C.b_ones_col, b_src], writes=[b_pl[pvb]])

                    for kt in range(nkt):
                        ktg = gtile(kt)
                        sb = kt % 3
                        tr.mmgroup([(lambda e, ktg=ktg, sb=sb, hb=hb, gsl=gsl:
                                     e.matmul(p_st[sb][:], s_kth[:, ktg*128:(ktg+1)*128], s_qh[hb][:, gsl],
                                              start=True, stop=True))],
                                   reads=[b_kth, b_qh[hb]], writes=[b_pst[sb]])
                        tr.op("act", lambda e, sb=sb: e.activation(s_pT[sb][:], p_st[sb][:], AF.Exp, scale=ISCALE),
                              reads=[b_pst[sb]], writes=[b_pT[sb]])
                        src, b_src = s_pT[sb], b_pT[sb]
                        di = kt - q0 // 128
                        if di >= 0:
                            tr.op("dve", lambda e, sb=sb, di=di: e.tensor_tensor(s_pTm[sb][:], s_pT[sb][:], s_msk[:, di, :], ALU.mult),
                                  reads=[b_pT[sb], b_msk], writes=[b_pTm[sb]])
                            src, b_src = s_pTm[sb], b_pTm[sb]
                        if kt == 2:
                            emit_norm()
                        flush_prev(kt)
                        prev[0] = (src, b_src, kt)
                    flush_prev(nkt)
                    if pend[0] is not None:
                        emit_norm()
                    pend[0] = (hh, gsl, pvb)
                    ucnt[0] += 1
        emit_norm()
        spill_quarter(3)
        if DBG:
            dbg_aT = nc.dram_tensor("d_aTdump", [128, 16, 4096], BF16, kind="ExternalOutput").ap()
            tr.dma("q_act", dbg_aT, s_aT[:], reads=[b_aT])
    tr.barrier()
    at_stack.close()

    with contextlib.ExitStack() as st:
        s_aT2, b_aT2 = C.alloc("l1_aT2", [128, NH, T], BF16, st)
        tr.dma("q_sync", s_aT2[:],
               A["d_amy"].rearrange("(h p) t -> p h t", p=128),
               reads=[b_amy], writes=[b_aT2])
        def amyot(h):
            hh, rk_ = h % 16, h // 16
            return (hh // 4) * 8 + rk_ * 4 + (hh % 4)
        s_wo2, b_wo2 = zip(*[C.alloc(f"l1_wo{i}", [128, NH, 512], BF16, st) for i in range(2)])
        s_xc, b_xc = C.alloc("l1_xc", [128, 512], F32, st)
        s_g1, b_g1 = C.alloc("l1_g1", [128, 512], BF16, st)
        s_c2, b_c2 = C.alloc("l1_c2", [128, 512], F32, st)
        s_x2, b_x2 = zip(*[C.alloc(f"l1_x2{i}", [128, 512], F32, st) for i in range(2)])
        p_s, b_ps = zip(*[C.palloc(f"l1_ps{i}", [128, 512], st) for i in range(2)])
        for cc in range(8):
            csl = slice(cc*512, (cc+1)*512)
            wob = cc % 2
            s_wo, b_wo = s_wo2[wob], b_wo2[wob]
            tr.dma("q_pool", s_wo[:],
                   A["wo1"].rearrange("(ot p) c -> p ot c", p=128)[:, :, csl],
                   writes=[b_wo])
            for tt in range(NT):
                pb = tt % 2
                tsl = slice(tt*128, (tt+1)*128)
                tr.mmgroup([(lambda e, ot=ot, pb=pb, tsl=tsl, s_wo=s_wo:
                             e.matmul(p_s[pb][:], s_aT2[:, amyot(ot), tsl], s_wo[:, ot, :],
                                      start=(ot == 0), stop=(ot == NH-1)))
                            for ot in range(NH)],
                           reads=[b_aT2, b_wo], writes=[b_ps[pb]])
                tr.dma("q_sync", s_xc[:], x1_ap[tsl, csl], writes=[b_xc])
                tr.dma("q_sync", s_g1[:], A["d_gs1"][tt, :, csl], writes=[b_g1])
                tr.op("dve", lambda e, pb=pb: e.tensor_mul(s_c2[:], p_s[pb][:], s_g1[:]),
                      reads=[b_ps[pb], b_g1], writes=[b_c2])
                xb = tt % 2
                tr.op("dve", lambda e, xb=xb: e.tensor_add(s_x2[xb][:], s_c2[:], s_xc[:]),
                      reads=[b_c2, b_xc], writes=[b_x2[xb]])
                tr.dma("q_act", out_ap[tsl, csl], s_x2[xb][:], reads=[b_x2[xb]])
    tr.barrier()
    lay.close()


DBG = False

def _scratch(nc, name, shape, dtype):
    kind = "ExternalOutput" if (DBG and name in (
        "d_h", "d_phiq", "d_gs0", "d_xnT", "d_zsg",
        "d_h1", "d_gs1", "d_w1bf", "d_qmy", "d_kmy", "d_vmy", "d_wdbf")) else "Internal"
    return nc.dram_tensor(name, list(shape), dtype, kind=kind).ap()


def _common_dram(nc, A, need_l0, need_l1):
    ext = []
    def inp(name, shape, dtype):
        A[name] = nc.dram_tensor(name, list(shape), dtype, kind="ExternalInput").ap()
        ext.append(name)
    inp("ident", (128, 128), BF16)
    if need_l0:
        inp("x", (T, D), F32)
        inp("wqT", (D, D), F32); inp("wkT", (D, 1024), F32)
        inp("wvT", (D, 1024), F32); inp("woT", (D, D), F32)
        inp("bq_c", (128, 32), F32); inp("bk_c", (128, 8), F32)
        inp("bv_c", (128, 8), F32); inp("bo", (1, D), F32)
        inp("qg", (NOI, D, 64), U8); inp("qu", (NOI, D, 64), U8)
        inp("sge", (NOI, 32, 64), F32); inp("sgo", (NOI, 32, 64), F32)
        inp("sue", (NOI, 32, 64), F32); inp("suo", (NOI, 32, 64), F32)
        inp("sgef", (32, DI // 2), F32); inp("sgof", (32, DI // 2), F32)
        inp("suef", (32, DI // 2), F32); inp("suof", (32, DI // 2), F32)
        inp("zg8", (32, DI // 2), U8); inp("zu8", (32, DI // 2), U8)
        inp("qd", (8, DI, 256), U8)
        inp("sd_b", (8, NOI, 512), F32)
        inp("sdef", (NOI, 2048), F32); inp("sdof", (NOI, 2048), F32)
        inp("zd8", (NOI, 2048), U8)
        inp("stair32", (128, 63), BF16); inp("stair86", (128, 171), BF16)
        A["d_h"] = _scratch(nc, "d_h", (NOI, 128, T), BF16)
        A["d_phiq"] = _scratch(nc, "d_phiq", (NDT, 128, T), BF16)
        A["d_gs0"] = _scratch(nc, "d_gs0", (NT, 128, D), BF16)
        A["d_zsg"] = _scratch(nc, "d_zsg", (32, DI), BF16)
        A["d_zsu"] = _scratch(nc, "d_zsu", (32, DI), BF16)
        A["d_wdbf"] = _scratch(nc, "d_wdbf", (8, DI, 512), BF16)
        A["d_cc0in"] = _scratch(nc, "d_cc0in", (129, 1024), F32)
        A["d_cc0out"] = _scratch(nc, "d_cc0out", (129, 1024), F32)
    if need_l1:
        inp("wq1", (D, D), F32); inp("wk1", (D, 1024), F32)
        inp("wv1", (D, 1024), F32); inp("wo1", (D, D), F32)
        inp("k0r", (NDT, NDT, 128, 128), U8)
        inp("k1r", (NDT, NDT, 128, 128), U8)
        inp("k2r", (NDT, 128, D), U8)
        inp("s012", (1, 4), F32)
        inp("cvec_c", (128, NDT), F32)
        inp("cos_c", (128, T), F32)
        inp("sin_c", (128, T), F32)
        inp("masks", (4, 128, 512), BF16)
        inp("rk", (1, 1), mybir.dt.uint32)
        A["d_agk_in"] = _scratch(nc, "d_agk_in", (1024, T), BF16)
        A["d_agk_out"] = _scratch(nc, "d_agk_out", (2048, T), BF16)
        A["d_agv_in"] = _scratch(nc, "d_agv_in", (T, 1024), BF16)
        A["d_agv_out"] = _scratch(nc, "d_agv_out", (2 * T, 1024), BF16)
        A["d_agq_in"] = _scratch(nc, "d_agq_in", (4096, T), BF16)
        A["d_agq_out"] = _scratch(nc, "d_agq_out", (8192, T), BF16)
        for k in range(4):
            A[f"d_aga_in{k}"] = _scratch(nc, f"d_aga_in{k}", (512, 2 * T), BF16)
            A[f"d_aga_out{k}"] = _scratch(nc, f"d_aga_out{k}", (1024, 2 * T), BF16)
        A["d_amy"] = _scratch(nc, "d_amy", (4096, T), BF16)
        A["d_qmy"] = _scratch(nc, "d_qmy", (2048, 2 * T), BF16)
        A["d_kmy"] = _scratch(nc, "d_kmy", (512, 2 * T), BF16)
        A["d_vmy"] = _scratch(nc, "d_vmy", (2 * T, 512), BF16)
        A["d_h1"] = _scratch(nc, "d_h1", (NDT, 128, T), BF16)
        A["d_w1bf"] = _scratch(nc, "d_w1bf", (NDT, NDT, 128, 128), BF16)
        A["d_w2bf"] = _scratch(nc, "d_w2bf", (NDT, 128, D), BF16)
        A["d_gs1"] = _scratch(nc, "d_gs1", (NT, 128, D), BF16)
    return ext


def build_l0():
    nc = bass.Bass("TRN2", target_bir_lowering=False)
    A = {}
    ext = _common_dram(nc, A, True, False)
    x1 = nc.dram_tensor("x1", [T, D], F32, kind="ExternalOutput").ap()
    C = Ctx(nc, A)
    emit_layer0(C, A["x"], x1)
    C.tr.barrier()
    C.stack.close()
    C.tr.ctx.close()
    return nc, ext


def build_fused():
    nc = bass.Bass("TRN2", target_bir_lowering=False)
    A = {}
    ext = _common_dram(nc, A, True, True)
    A["d_x1"] = _scratch(nc, "d_x1", (T, D), F32)
    out = nc.dram_tensor("xout", [T, D], F32, kind="ExternalOutput").ap()
    C = Ctx(nc, A)
    emit_layer0(C, A["x"], A["d_x1"], preconv_l1=True)
    emit_layer1(C, A["d_x1"], out)
    C.tr.barrier()
    C.stack.close()
    C.tr.ctx.close()
    return nc, ext


def build_l1():
    nc = bass.Bass("TRN2", target_bir_lowering=False)
    A = {}
    ext = _common_dram(nc, A, False, True)
    x1 = nc.dram_tensor("x1in", [T, D], F32, kind="ExternalInput").ap()
    ext.append("x1in")
    out = nc.dram_tensor("xout", [T, D], F32, kind="ExternalOutput").ap()
    C = Ctx(nc, A)
    emit_layer1(C, x1, out)
    C.tr.barrier()
    C.stack.close()
    C.tr.ctx.close()
    return nc, ext



def _host_shared(inputs):
    S = {}
    S["ident"] = np.eye(128, dtype=BF16NP)
    f32 = lambda a: np.ascontiguousarray(np.asarray(a), dtype=np.float32)
    S["wqT"] = f32(np.asarray(inputs["L0_attn_Wq"]).T)
    S["wkT"] = f32(np.asarray(inputs["L0_attn_Wk"]).T)
    S["wvT"] = f32(np.asarray(inputs["L0_attn_Wv"]).T)
    S["woT"] = f32(np.asarray(inputs["L0_attn_Wo"]).T)
    S["bq_c"] = f32(np.asarray(inputs["L0_attn_bq"]).reshape(32, 128).T)
    S["bk_c"] = f32(np.asarray(inputs["L0_attn_bk"]).reshape(8, 128).T)
    S["bv_c"] = f32(np.asarray(inputs["L0_attn_bv"]).reshape(8, 128).T)
    S["bo"] = f32(np.asarray(inputs["L0_attn_bo"]).reshape(1, D))
    for nm, qk, sk, zk in (("g", "L0_ffn_gate_qweight", "L0_ffn_gate_scales", "L0_ffn_gate_qzeros"),
                           ("u", "L0_ffn_up_qweight", "L0_ffn_up_scales", "L0_ffn_up_qzeros")):
        qb = np.ascontiguousarray(np.asarray(inputs[qk])).view(np.uint8)
        S["q" + nm] = np.ascontiguousarray(qb.reshape(D, NOI, 64).transpose(1, 0, 2))
        sc = f32(np.asarray(inputs[sk]))
        s4 = sc.reshape(32, NOI, 64, 2)
        S["s" + nm + "e"] = np.ascontiguousarray(s4[:, :, :, 0].transpose(1, 0, 2))
        S["s" + nm + "o"] = np.ascontiguousarray(s4[:, :, :, 1].transpose(1, 0, 2))
        S["s" + nm + "ef"] = np.ascontiguousarray(sc[:, 0::2])
        S["s" + nm + "of"] = np.ascontiguousarray(sc[:, 1::2])
        S["z" + nm + "8"] = np.ascontiguousarray(np.asarray(inputs[zk])).view(np.uint8)
    db = np.ascontiguousarray(np.asarray(inputs["L0_ffn_down_qweight"])).view(np.uint8)
    S["qd"] = np.ascontiguousarray(db.reshape(DI, 8, 256).transpose(1, 0, 2))
    ds = f32(np.asarray(inputs["L0_ffn_down_scales"]))
    S["sd_b"] = np.ascontiguousarray(ds.reshape(NOI, 8, 512).transpose(1, 0, 2))
    S["sdef"] = np.ascontiguousarray(ds[:, 0::2])
    S["sdof"] = np.ascontiguousarray(ds[:, 1::2])
    S["zd8"] = np.ascontiguousarray(np.asarray(inputs["L0_ffn_down_qzeros"])).view(np.uint8)
    st32 = np.zeros((128, 63), dtype=BF16NP); st32[:, 31] = 1
    S["stair32"] = st32
    st86 = np.zeros((128, 171), dtype=BF16NP); st86[:, 85] = 1
    S["stair86"] = st86
    S["wq1"] = f32(inputs["L1_attn_Wq"])
    S["wk1"] = f32(inputs["L1_attn_Wk"])
    S["wv1"] = f32(inputs["L1_attn_Wv"])
    S["wo1"] = f32(inputs["L1_attn_Wo"])
    for nm, key in (("k0r", "L1_ffn_k0"), ("k1r", "L1_ffn_k1")):
        ku = np.ascontiguousarray(np.asarray(inputs[key])).view(np.uint8)
        S[nm] = np.ascontiguousarray(ku.reshape(NDT, 128, NDT, 128).transpose(2, 0, 1, 3))
    k2u = np.ascontiguousarray(np.asarray(inputs["L1_ffn_k2"])).view(np.uint8)
    S["k2r"] = np.ascontiguousarray(k2u.reshape(NDT, 128, D))
    S["s012"] = np.array([[float(np.asarray(inputs["L1_ffn_s0"])),
                           float(np.asarray(inputs["L1_ffn_s1"])),
                           float(np.asarray(inputs["L1_ffn_s2"])), 0.0]], dtype=np.float32)
    S["cvec_c"] = np.ascontiguousarray(
        np.where(np.arange(D) < 3277, 2.0, -8.0).astype(np.float32).reshape(NDT, 128).T)
    masks = np.zeros((4, 128, 512), dtype=BF16NP)
    for i in range(4):
        pp = np.arange(128)[:, None]
        ff = np.arange(512)[None, :]
        masks[i] = (ff >= pp + i * 128).astype(BF16NP)
    S["masks"] = masks
    inv = (10000.0 ** (-np.arange(64) / 64.0)).astype(np.float64)
    for h in (0, 1):
        glob = np.concatenate([np.arange(s0, s1) for (s0, s1) in _tok_slices(h)])
        ang = glob[None, :] * inv[:, None]
        cos = np.cos(ang).astype(np.float32)
        sin = np.sin(ang).astype(np.float32)
        S[f"cos_c{h}"] = np.ascontiguousarray(np.concatenate([cos, cos], axis=0))
        S[f"sin_c{h}"] = np.ascontiguousarray(np.concatenate([-sin, sin], axis=0))
    return S


def _tok_slices(h):
    if h == 0:
        return [(0, 1024), (3072, 4096)]
    return [(1024, 3072)]


def _host_percore(inputs, core):
    b, h = core // 2, core % 2
    x = np.asarray(inputs["L0_attn_hidden_states"])
    parts = [np.asarray(x[b, s0:s1, :], dtype=np.float32) for (s0, s1) in _tok_slices(h)]
    return {"x": np.ascontiguousarray(np.concatenate(parts, axis=0))}


def kernel(**inputs):
    S = _host_shared(inputs)
    nc, ext = build_fused()
    in_maps = []
    for c in range(8):
        hh = c % 2
        m = {k: S[k] for k in ext if k in S}
        m.update(_host_percore(inputs, c))
        m["cos_c"] = S[f"cos_c{hh}"]
        m["sin_c"] = S[f"sin_c{hh}"]
        m["rk"] = np.array([[hh]], dtype=np.uint32)
        in_maps.append(m)
    res = run_bass_kernel_spmd(nc, in_maps, core_ids=list(range(8)))
    out = np.zeros((4, 4096, 4096), dtype=np.float32)
    for c in range(8):
        b, hh = c // 2, c % 2
        xo = np.asarray(res.results[c]["xout"])
        ofs = 0
        for (s0, s1) in _tok_slices(hh):
            n = s1 - s0
            out[b, s0:s1, :] = xo[ofs:ofs+n]
            ofs += n
    return out


# version 06
# speedup vs baseline: 1.2929x; 1.0322x over previous
import contextlib
import numpy as np
import ml_dtypes
import concourse.bass as bass
import concourse.mybir as mybir
from concourse.bass_utils import run_bass_kernel_spmd

F32 = mybir.dt.float32
BF16 = mybir.dt.bfloat16
U8 = mybir.dt.uint8
AF = mybir.ActivationFunctionType
ALU = mybir.AluOpType
BF16NP = ml_dtypes.bfloat16

T = 2048
D = 4096
NT = T // 128
NDT = D // 128
DI = 11008
NOI = DI // 128
NKV = 8
NH = 32
EPS = 1e-5
GROUPS2 = [[0, 1], [2, 3], [4, 5], [6, 7]]
ISCALE = float(1.0 / np.sqrt(128.0))


class Buf:
    __slots__ = ("w", "r")
    def __init__(self):
        self.w = None
        self.r = []

class TK:
    def __init__(self, nc):
        self.nc = nc
        self.ctx = contextlib.ExitStack()
        self.sems = {}
        self.counts = {}
        self.satisfied = {}
        self.eng = {"pe": nc.tensor, "act": nc.scalar, "dve": nc.vector,
                    "pool": nc.gpsimd, "sync": nc.sync}
        for n in ("pe", "act", "dve", "pool"):
            self._new_sem(n)
        self.NSLOT = 8
        self.qrr = {}
        for q in ("q_sync", "q_pool", "q_act"):
            self.qrr[q] = 0
            for i in range(self.NSLOT):
                self._new_sem(f"{q}_{i}")
        self.ccn = 0

    def _new_sem(self, name):
        self.sems[name] = self.ctx.enter_context(self.nc.semaphore(name))
        self.counts[name] = 0


    def cc_sem(self):
        name = f"cc_{self.ccn}"
        self.ccn += 1
        self._new_sem(name)
        return name

    def wait(self, engine, toks):
        eng = self.eng[engine]
        need = {}
        for (s, v) in toks:
            if v <= 0 or self.satisfied.get((engine, s), 0) >= v:
                continue
            need[s] = max(need.get(s, 0), v)
        for s, v in need.items():
            eng.wait_ge(self.sems[s], v)
            self.satisfied[(engine, s)] = v

    def inc(self, inst, sem, by):
        inst.then_inc(self.sems[sem], by)
        self.counts[sem] += by
        return (sem, self.counts[sem])

    def barrier(self):
        toks = [(s, c) for s, c in self.counts.items()
                if c > 0 and not s.startswith("cc_")]
        for e in ("pe", "act", "dve", "pool", "sync"):
            self.wait(e, toks)

    def _pre(self, engine, reads, writes):
        toks = []
        for b in reads:
            if b.w: toks.append(b.w)
        for b in writes:
            if b.w: toks.append(b.w)
            toks.extend(b.r)
        self.wait(engine, toks)

    def _post(self, t, reads, writes):
        for b in writes:
            b.w = t; b.r = []
        for b in reads:
            b.r.append(t)
        return t

    def op(self, engine, fn, reads=(), writes=()):
        self._pre(engine, reads, writes)
        inst = fn(self.eng[engine])
        return self._post(self.inc(inst, engine, 1), reads, writes)

    def dma(self, q, out, in_, reads=(), writes=(), **kw):
        engine = {"q_sync": "sync", "q_pool": "pool", "q_act": "act"}[q]
        self._pre(engine, reads, writes)
        sem = f"{q}_{self.qrr[q]}"
        self.qrr[q] = (self.qrr[q] + 1) % self.NSLOT
        self.wait(engine, [(sem, self.counts[sem])])
        inst = self.eng[engine].dma_start(out=out, in_=in_, **kw)
        return self._post(self.inc(inst, sem, 16), reads, writes)

    def mmgroup(self, mms, reads=(), writes=()):
        self._pre("pe", reads, writes)
        inst = None
        for f in mms:
            inst = f(self.eng["pe"])
        return self._post(self.inc(inst, "pe", 1), reads, writes)

    def collective(self, kind, op, in_ap, out_ap, write_tok_buf):
        sem = self.cc_sem()
        inst = self.nc.gpsimd.collective_compute(
            kind, op, replica_groups=GROUPS2,
            ins=[in_ap.opt()], outs=[out_ap.opt()])
        inst.then_inc(self.sems[sem], 1)
        self.counts[sem] += 1
        write_tok_buf.w = (sem, 1)
        write_tok_buf.r = []
        return sem


class Ctx:
    def __init__(self, nc, A):
        self.nc, self.A = nc, A
        self.tr = TK(nc)
        self.stack = contextlib.ExitStack()
        sb = lambda n, s, d: self.stack.enter_context(nc.sbuf_tensor(n, s, d))
        self.s_ident = sb("c_ident", [128, 128], BF16)
        self.s_ones_col = sb("c_ones_col", [128, 1], BF16)
        self.s_ones_row = sb("c_ones_row", [1, 128], BF16)
        self.b_ident, self.b_ones_col, self.b_ones_row = Buf(), Buf(), Buf()
        self.tr.dma("q_sync", self.s_ident[:], A["ident"], writes=[self.b_ident])
        self.tr.op("pool", lambda e: e.memset(self.s_ones_col[:], 1.0), writes=[self.b_ones_col])
        self.tr.op("pool", lambda e: e.memset(self.s_ones_row[:], 1.0), writes=[self.b_ones_row])

    def alloc(self, name, shape, dtype, stack):
        return stack.enter_context(self.nc.sbuf_tensor(name, shape, dtype)), Buf()

    def palloc(self, name, shape, stack, dtype=F32):
        return stack.enter_context(self.nc.psum_tensor(name, shape, dtype)), Buf()



def emit_lnT(C, x_ap, s_xnT, b_xnT, tag):
    tr = C.tr
    with contextlib.ExitStack() as st:
        s_x, b_x, s_xnb, b_xnb = [], [], [], []
        for i in range(2):
            t, b = C.alloc(f"{tag}_x{i}", [128, D], F32, st); s_x.append(t); b_x.append(b)
            t, b = C.alloc(f"{tag}_n{i}", [128, D], BF16, st); s_xnb.append(t); b_xnb.append(b)
        s_junk1, b_junk1 = C.alloc(f"{tag}_junk", [128, D], F32, st)
        s_junk, b_junk = (s_junk1, s_junk1), (b_junk1, b_junk1)
        s_stt, b_stt = zip(*[C.alloc(f"{tag}_st{i}", [128, 8], F32, st) for i in range(2)])
        p_tr, b_ptr = zip(*[C.palloc(f"{tag}_p{i}", [128, 128], st, BF16) for i in range(2)])
        for tt in range(NT):
            xb = tt % 2
            s_st, b_st = s_stt[xb], b_stt[xb]
            s_junk2, b_junk2 = s_junk[xb], b_junk[xb]
            tr.dma("q_sync", s_x[xb][:], x_ap[tt*128:(tt+1)*128, :], writes=[b_x[xb]])
            tr.op("dve", lambda e: e.reduce_sum(s_st[:, 0:1], s_x[xb][:], axis=mybir.AxisListType.X),
                  reads=[b_x[xb]], writes=[b_st])
            tr.op("act", lambda e: e.activation(s_junk2[:], s_x[xb][:], AF.Square, accum_out=s_st[:, 1:2]),
                  reads=[b_x[xb]], writes=[b_junk2, b_st])
            tr.op("dve", lambda e: e.tensor_scalar(s_st[:, 2:3], s_st[:, 0:1], 1.0 / D, None, ALU.mult),
                  reads=[b_st], writes=[b_st])
            tr.op("dve", lambda e: e.tensor_scalar(s_st[:, 3:4], s_st[:, 1:2], 1.0 / D, float(EPS), ALU.mult, ALU.add),
                  reads=[b_st], writes=[b_st])
            tr.op("dve", lambda e: e.scalar_tensor_tensor(s_st[:, 4:5], s_st[:, 2:3], 1.0, s_st[:, 2:3], ALU.mult, ALU.mult),
                  reads=[b_st], writes=[b_st])
            tr.op("dve", lambda e: e.tensor_sub(s_st[:, 3:4], s_st[:, 3:4], s_st[:, 4:5]),
                  reads=[b_st], writes=[b_st])
            tr.op("act", lambda e: e.activation(s_st[:, 5:6], s_st[:, 3:4], AF.Ln),
                  reads=[b_st], writes=[b_st])
            tr.op("act", lambda e: e.activation(s_st[:, 6:7], s_st[:, 5:6], AF.Exp, scale=-0.5),
                  reads=[b_st], writes=[b_st])
            tr.op("dve", lambda e: e.scalar_tensor_tensor(s_st[:, 7:8], s_st[:, 2:3], -1.0, s_st[:, 6:7], ALU.mult, ALU.mult),
                  reads=[b_st], writes=[b_st])
            tr.op("dve", lambda e: e.tensor_scalar(s_xnb[xb][:], s_x[xb][:], s_st[:, 6:7], s_st[:, 7:8], ALU.mult, ALU.add),
                  reads=[b_x[xb], b_st], writes=[b_xnb[xb]])
            for dt in range(NDT):
                pb = dt % 2
                tr.op("pe", lambda e: e.transpose(p_tr[pb][:], s_xnb[xb][:, dt*128:(dt+1)*128], C.s_ident[:]),
                      reads=[b_xnb[xb], C.b_ident], writes=[b_ptr[pb]])
                tr.op("dve", lambda e: e.tensor_copy(s_xnT[:, dt, tt*128:(tt+1)*128], p_tr[pb][:]),
                      reads=[b_ptr[pb]], writes=[b_xnT])
    tr.barrier()


def emit_layer0(C, x_ap, x1_ap, preconv_l1=False):
    nc, tr, A = C.nc, C.tr, C.A
    lay = contextlib.ExitStack()
    s_sg86, b_sg86 = C.alloc("l0_sg86", [86, T], BF16, lay)
    b_cc0 = Buf()

    era = contextlib.ExitStack()
    s_xnT, b_xnT = C.alloc("l0_xnT", [128, NDT, T], BF16, era)
    s_bias, b_bias = C.alloc("l0_bias", [128, 48], F32, era)
    tr.dma("q_sync", s_bias[:, 0:32], A["bq_c"], writes=[b_bias])
    tr.dma("q_sync", s_bias[:, 32:40], A["bk_c"], writes=[b_bias])
    tr.dma("q_sync", s_bias[:, 40:48], A["bv_c"], writes=[b_bias])
    s_sg, b_sg = C.alloc("l0_sg", [32, T], BF16, era)

    emit_lnT(C, x_ap, s_xnT, b_xnT, "l0ln")
    if DBG:
        A["d_xnT"] = _scratch(nc, "d_xnT", (128, NDT, T), BF16)
        tr.dma("q_act", A["d_xnT"], s_xnT[:], reads=[b_xnT])
        tr.barrier()

    with contextlib.ExitStack() as st:
        s_w, b_w = zip(*[C.alloc(f"l0kv_w{i}", [128, NDT, 128], BF16, st) for i in range(2)])
        s_t1, b_t1 = C.alloc("l0kv_t1", [128, 512], F32, st)
        s_t2, b_t2 = C.alloc("l0kv_t2", [128, 512], F32, st)
        s_t3, b_t3 = C.alloc("l0kv_t3", [128, 512], F32, st)
        s_kT, b_kT = C.alloc("l0_kT", [128, T], BF16, st)
        s_vT, b_vT = C.alloc("l0_vT", [128, T], BF16, st)
        s_kx, b_kx = C.alloc("l0_kx", [128, NT, 128], BF16, st)
        s_vx, b_vx = C.alloc("l0_vx", [128, NT, 128], BF16, st)
        s_zero, b_zero = C.alloc("l0kv_zero", [128, 512], F32, st)
        tr.op("pool", lambda e: e.memset(s_zero[:], 0.0), writes=[b_zero])
        p_y, b_py = zip(*[C.palloc(f"l0kv_p{i}", [128, 512], st) for i in range(2)])
        p_t2, b_pt2 = zip(*[C.palloc(f"l0kv_pt{i}", [128, 128], st, BF16) for i in range(2)])
        p_kv, b_pkv = C.palloc("l0_pkv", [128, 1024], st)
        p_ks, b_pks = C.palloc("l0_pks", [1, 1024], st)

        for kh in range(NKV):
            for m, wname, bcol, s_dst, b_dst, do_phi in (
                    (0, "wkT", 32 + kh, s_kT, b_kT, True),
                    (1, "wvT", 40 + kh, s_vT, b_vT, False)):
                wb = m
                tr.dma("q_pool", s_w[wb][:],
                       A[wname].rearrange("(dt p) o -> p dt o", p=128)[:, :, kh*128:(kh+1)*128],
                       writes=[b_w[wb]])
                for tc in range(4):
                    pb = tc % 2
                    sl = slice(tc*512, (tc+1)*512)
                    tr.mmgroup([(lambda e, dt=dt, pb=pb, wb=wb, sl=sl:
                                 e.matmul(p_y[pb][:], s_w[wb][:, dt, :], s_xnT[:, dt, sl],
                                          start=(dt == 0), stop=(dt == NDT-1)))
                                for dt in range(NDT)],
                               reads=[b_w[wb], b_xnT], writes=[b_py[pb]])
                    if do_phi:
                        tr.op("dve", lambda e: e.tensor_scalar(s_t2[:], p_y[pb][:], s_bias[:, bcol:bcol+1], 0.0, ALU.add, ALU.min),
                              reads=[b_py[pb], b_bias], writes=[b_t2])
                        tr.op("act", lambda e: e.activation(s_t3[:], s_t2[:], AF.Exp),
                              reads=[b_t2], writes=[b_t3])
                        tr.op("dve", lambda e: e.scalar_tensor_tensor(s_t1[:], p_y[pb][:], s_bias[:, bcol:bcol+1], s_zero[:], ALU.add, ALU.max),
                              reads=[b_py[pb], b_bias, b_zero], writes=[b_t1])
                        tr.op("dve", lambda e: e.tensor_add(s_dst[:, sl], s_t1[:], s_t3[:]),
                              reads=[b_t1, b_t3], writes=[b_dst])
                    else:
                        tr.op("dve", lambda e: e.tensor_scalar(s_dst[:, sl], p_y[pb][:], s_bias[:, bcol:bcol+1], None, ALU.add),
                              reads=[b_py[pb], b_bias], writes=[b_dst])
            for s_src, b_src, s_d, b_d in ((s_kT, b_kT, s_kx, b_kx), (s_vT, b_vT, s_vx, b_vx)):
                for tt in range(NT):
                    pb = tt % 2
                    tr.op("pe", lambda e: e.transpose(p_t2[pb][:], s_src[:, tt*128:(tt+1)*128], C.s_ident[:]),
                          reads=[b_src, C.b_ident], writes=[b_pt2[pb]])
                    tr.op("dve", lambda e: e.tensor_copy(s_d[:, tt, :], p_t2[pb][:]),
                          reads=[b_pt2[pb]], writes=[b_d])
            sl_kv = slice(kh*128, (kh+1)*128)
            tr.mmgroup([(lambda e, tt=tt, sl_kv=sl_kv:
                         e.matmul(p_kv[:, sl_kv], s_kx[:, tt, :], s_vx[:, tt, :],
                                  start=(tt == 0), stop=(tt == NT-1)))
                        for tt in range(NT)],
                       reads=[b_kx, b_vx], writes=[b_pkv])
            tr.mmgroup([(lambda e, tt=tt, sl_kv=sl_kv:
                         e.matmul(p_ks[:, sl_kv], C.s_ones_col[:], s_kx[:, tt, :],
                                  start=(tt == 0), stop=(tt == NT-1)))
                        for tt in range(NT)],
                       reads=[b_kx, C.b_ones_col], writes=[b_pks])
        s_kv, b_kv = C.alloc("l0_skv", [128, 1024], F32, st)
        s_ks, b_ks = C.alloc("l0_sks", [1, 1024], F32, st)
        tr.op("act", lambda e: e.copy(s_kv[:], p_kv[:]), reads=[b_pkv], writes=[b_kv])
        tr.op("act", lambda e: e.copy(s_ks[:], p_ks[:]), reads=[b_pks], writes=[b_ks])
        b_ccin = Buf()
        tr.dma("q_sync", A["d_cc0in"][0:128, :], s_kv[:], reads=[b_kv], writes=[b_ccin])
        tr.dma("q_sync", A["d_cc0in"][128:129, :], s_ks[:], reads=[b_ks], writes=[b_ccin])
        tr.wait("pool", [b_ccin.w])
        tr.collective("AllReduce", ALU.add, A["d_cc0in"], A["d_cc0out"], b_cc0)
    tr.barrier()

    with contextlib.ExitStack() as st:
        s_stair32, b_st32 = C.alloc("l0_st32", [128, 63], BF16, st)
        tr.dma("q_sync", s_stair32[:], A["stair32"], writes=[b_st32])
        p_sg, b_psg = C.palloc("l0_psg", [32, T], st)
        for tc in range(4):
            sl = slice(tc*512, (tc+1)*512)
            tr.mmgroup([(lambda e, dt=dt, sl=sl:
                         e.matmul(p_sg[:, sl], s_stair32[:, 31-dt:63-dt], s_xnT[:, dt, sl],
                                  start=(dt == 0), stop=(dt == NDT-1)))
                        for dt in range(NDT)],
                       reads=[b_st32, b_xnT], writes=[b_psg])
        tr.op("act", lambda e: e.copy(s_sg[:], p_sg[:]), reads=[b_psg], writes=[b_sg])
        s_z8, b_z8 = C.alloc("l0_z8", [32, DI // 2], U8, st)
        s_zn, b_zn = C.alloc("l0_zn", [32, DI], U8, st)
        s_sf, b_sf = C.alloc("l0_sf", [32, DI // 2], F32, st)
        s_zs, b_zs = C.alloc("l0_zs", [32, DI], BF16, st)
        for nm in ("g", "u"):
            tr.dma("q_sync", s_z8[:], A[f"z{nm}8"], writes=[b_z8])
            tr.op("dve", lambda e: e.tensor_single_scalar(s_zn[:, 0:DI//2], s_z8[:], 15, ALU.bitwise_and),
                  reads=[b_z8], writes=[b_zn])
            tr.op("dve", lambda e: e.tensor_single_scalar(s_zn[:, DI//2:DI], s_z8[:], 4, ALU.logical_shift_right),
                  reads=[b_z8], writes=[b_zn])
            zv = s_zs[:].rearrange("p (n two) -> p n two", two=2)
            for par in (0, 1):
                snm = f"s{nm}ef" if par == 0 else f"s{nm}of"
                off = 0 if par == 0 else DI // 2
                tr.dma("q_sync", s_sf[:], A[snm], writes=[b_sf])
                tr.op("dve", lambda e, zv=zv, off=off, par=par: e.scalar_tensor_tensor(
                        zv[:, :, par:par+1],
                        s_zn[:, off:off+DI//2].rearrange("p (n o) -> p n o", o=1), -1.0,
                        s_sf[:].rearrange("p (n o) -> p n o", o=1), ALU.mult, ALU.mult),
                      reads=[b_zn, b_sf], writes=[b_zs])
            tr.dma("q_act", A[f"d_zs{nm}"], s_zs[:], reads=[b_zs])
    tr.barrier()

    with contextlib.ExitStack() as st:
        s_stair86, b_st86 = C.alloc("l0_st86", [128, 171], BF16, st)
        tr.dma("q_sync", s_stair86[:], A["stair86"], writes=[b_st86])
        bufs = {}
        for nm in ("g", "u"):
            e = {}
            t_, b_ = C.alloc(f"l0q8{nm}", [128, NDT, 64], U8, st)
            e["q8"], e["b_q8"] = (t_, t_), (b_, b_)
            e["nib"], e["b_nib"] = C.alloc(f"l0nib{nm}", [128, NDT, 128], U8, st)
            if nm == "g":
                e["wt"], e["b_wt"] = zip(*[C.alloc(f"l0wt{nm}{i}", [128, NDT, 128], BF16, st) for i in range(2)])
            else:
                t_, b_ = C.alloc(f"l0wt{nm}", [128, NDT, 128], BF16, st)
                e["wt"], e["b_wt"] = (t_, t_), (b_, b_)
            e["sc"], e["b_sc"] = C.alloc(f"l0sc{nm}", [128, 2, NDT, 64], BF16, st)
            e["zss"], e["b_zss"] = zip(*[C.alloc(f"l0zss{nm}{i}", [32, 128], BF16, st) for i in range(2)])
            e["p"], e["b_p"] = zip(*[C.palloc(f"l0p{nm}{i}", [128, 512], st) for i in range(2)])
            bufs[nm] = e
        p_s86, b_ps86 = C.palloc("l0_ps86", [86, T], st)
        s_silu, b_silu = C.alloc("l0_silu", [128, 512], F32, st)
        s_h, b_h = zip(*[C.alloc(f"l0_h{i}", [128, T], BF16, st) for i in range(2)])

        qsrc = {"g": "qg", "u": "qu"}
        ssrc = {"g": ("sge", "sgo"), "u": ("sue", "suo")}
        for ot in range(NOI):
            ob = ot % 2
            for nm in ("g", "u"):
                e = bufs[nm]
                tr.dma("q_sync", e["q8"][ob][:],
                       A[qsrc[nm]][ot].rearrange("(dt p) b -> p dt b", p=128),
                       writes=[e["b_q8"][ob]])
                se, so = ssrc[nm]
                tr.dma("q_pool", e["sc"][:, 0], A[se][ot].partition_broadcast(128), writes=[e["b_sc"]])
                tr.dma("q_pool", e["sc"][:, 1], A[so][ot].partition_broadcast(128), writes=[e["b_sc"]])
                tr.dma("q_sync", e["zss"][ob][:], A[f"d_zs{nm}"][:, ot*128:(ot+1)*128], writes=[e["b_zss"][ob]])
                tr.op("dve", lambda e2, e=e, ob=ob: e2.tensor_single_scalar(e["nib"][:, :, 0:64], e["q8"][ob][:], 15, ALU.bitwise_and),
                      reads=[e["b_q8"][ob]], writes=[e["b_nib"]])
                tr.op("dve", lambda e2, e=e, ob=ob: e2.tensor_single_scalar(e["nib"][:, :, 64:128], e["q8"][ob][:], 4, ALU.logical_shift_right),
                      reads=[e["b_q8"][ob]], writes=[e["b_nib"]])
                wv = e["wt"][ob][:].rearrange("p dt (n two) -> p dt n two", two=2)
                for par, bo in ((0, 0), (1, 64)):
                    tr.op("dve", lambda e2, e=e, wv=wv, par=par, bo=bo: e2.tensor_tensor(
                            wv[:, :, :, par:par+1],
                            e["nib"][:, :, bo:bo+64].rearrange("p dt (n o) -> p dt n o", o=1),
                            e["sc"][:, par].rearrange("p dt (n o) -> p dt n o", o=1), ALU.mult),
                          reads=[e["b_nib"], e["b_sc"]], writes=[e["b_wt"][ob]])
            for tc in range(4):
                sl = slice(tc*512, (tc+1)*512)
                for nm in ("g", "u"):
                    e = bufs[nm]
                    mms = [(lambda e2, e=e, dt=dt, ob=ob, sl=sl:
                            e2.matmul(e["p"][ob][:], e["wt"][ob][:, dt, :], s_xnT[:, dt, sl],
                                      start=(dt == 0), stop=False))
                           for dt in range(NDT)]
                    mms.append(lambda e2, e=e, ob=ob, sl=sl:
                               e2.matmul(e["p"][ob][:], e["zss"][ob][:], s_sg[:, sl],
                                         start=False, stop=True))
                    tr.mmgroup(mms, reads=[e["b_wt"][ob], b_xnT, e["b_zss"][ob], b_sg],
                               writes=[e["b_p"][ob]])
                tr.op("act", lambda e2: e2.activation(s_silu[:], bufs["g"]["p"][ob][:], AF.Silu),
                      reads=[bufs["g"]["b_p"][ob]], writes=[b_silu])
                tr.op("dve", lambda e2: e2.tensor_tensor(s_h[ob][:, sl], s_silu[:], bufs["u"]["p"][ob][:], ALU.mult),
                      reads=[b_silu, bufs["u"]["b_p"][ob]], writes=[b_h[ob]])
                tr.mmgroup([(lambda e2, ot=ot, sl=sl, ob=ob:
                             e2.matmul(p_s86[:, sl], s_stair86[:, 85-ot:171-ot], s_h[ob][:, sl],
                                       start=(ot == 0), stop=(ot == NOI-1)))],
                           reads=[b_st86, b_h[ob]], writes=[b_ps86])
            tr.dma("q_act", A["d_h"][ot], s_h[ob][:], reads=[b_h[ob]])
        tr.op("act", lambda e: e.copy(s_sg86[:], p_s86[:]), reads=[b_ps86], writes=[b_sg86])
    tr.barrier()

    with contextlib.ExitStack() as st:
        s_w, b_w = zip(*[C.alloc(f"l0q_w{i}", [128, NDT, 128], BF16, st) for i in range(2)])
        s_t1, b_t1 = C.alloc("l0q_t1", [128, 512], F32, st)
        s_t2, b_t2 = C.alloc("l0q_t2", [128, 512], F32, st)
        s_t3, b_t3 = C.alloc("l0q_t3", [128, 512], F32, st)
        s_phi, b_phi = zip(*[C.alloc(f"l0q_f{i}", [128, T], BF16, st) for i in range(2)])
        s_zero, b_zero = C.alloc("l0q_zero", [128, 512], F32, st)
        tr.op("pool", lambda e: e.memset(s_zero[:], 0.0), writes=[b_zero])
        p_y, b_py = zip(*[C.palloc(f"l0q_p{i}", [128, 512], st) for i in range(2)])
        for ot in range(NDT):
            wb = ot % 2
            tr.dma("q_pool", s_w[wb][:],
                   A["wqT"].rearrange("(dt p) o -> p dt o", p=128)[:, :, ot*128:(ot+1)*128],
                   writes=[b_w[wb]])
            for tc in range(4):
                pb = tc % 2
                sl = slice(tc*512, (tc+1)*512)
                tr.mmgroup([(lambda e, dt=dt, pb=pb, wb=wb, sl=sl:
                             e.matmul(p_y[pb][:], s_w[wb][:, dt, :], s_xnT[:, dt, sl],
                                      start=(dt == 0), stop=(dt == NDT-1)))
                            for dt in range(NDT)],
                           reads=[b_w[wb], b_xnT], writes=[b_py[pb]])
                tr.op("dve", lambda e: e.tensor_scalar(s_t2[:], p_y[pb][:], s_bias[:, ot:ot+1], 0.0, ALU.add, ALU.min),
                      reads=[b_py[pb], b_bias], writes=[b_t2])
                tr.op("act", lambda e: e.activation(s_t3[:], s_t2[:], AF.Exp),
                      reads=[b_t2], writes=[b_t3])
                tr.op("dve", lambda e: e.scalar_tensor_tensor(s_t1[:], p_y[pb][:], s_bias[:, ot:ot+1], s_zero[:], ALU.add, ALU.max),
                      reads=[b_py[pb], b_bias, b_zero], writes=[b_t1])
                tr.op("dve", lambda e: e.tensor_add(s_phi[wb][:, sl], s_t1[:], s_t3[:]),
                      reads=[b_t1, b_t3], writes=[b_phi[wb]])
            tr.dma("q_act", A["d_phiq"][ot], s_phi[wb][:], reads=[b_phi[wb]])
    tr.barrier()
    era.close()

    with contextlib.ExitStack() as st:
        with contextlib.ExitStack() as pst:
            s_zsd, b_zsd = C.alloc("l0_zsd", [86, 4096], BF16, st)
            s_zd8, b_zd8 = C.alloc("l0_zd8", [86, 2048], U8, pst)
            s_zdn, b_zdn = C.alloc("l0_zdn", [86, 4096], U8, pst)
            s_sdf, b_sdf = C.alloc("l0_sdf", [86, 2048], F32, pst)
            tr.dma("q_sync", s_zd8[:], A["zd8"], writes=[b_zd8])
            tr.op("dve", lambda e: e.tensor_single_scalar(s_zdn[:, 0:2048], s_zd8[:], 15, ALU.bitwise_and),
                  reads=[b_zd8], writes=[b_zdn])
            tr.op("dve", lambda e: e.tensor_single_scalar(s_zdn[:, 2048:4096], s_zd8[:], 4, ALU.logical_shift_right),
                  reads=[b_zd8], writes=[b_zdn])
            zv = s_zsd[:].rearrange("p (n two) -> p n two", two=2)
            for par in (0, 1):
                snm = "sdef" if par == 0 else "sdof"
                off = 0 if par == 0 else 2048
                tr.dma("q_sync", s_sdf[:], A[snm], writes=[b_sdf])
                tr.op("dve", lambda e, par=par, off=off: e.scalar_tensor_tensor(
                        zv[:, :, par:par+1],
                        s_zdn[:, off:off+2048].rearrange("p (n o) -> p n o", o=1),
                        -1.0,
                        s_sdf[:].rearrange("p (n o) -> p n o", o=1), ALU.mult, ALU.mult),
                      reads=[b_zdn, b_sdf], writes=[b_zsd])
            tr.barrier()
        QB = (22, 22, 22, 20)
        QO = (0, 22, 44, 66)
        s_qd8b1, b_qd8b1 = C.alloc("l0_qd8b", [128, 22, 256], U8, st)
        s_qd8b, b_qd8b = (s_qd8b1, s_qd8b1), (b_qd8b1, b_qd8b1)
        s_dnb, b_dnb = C.alloc("l0_dnb", [128, 22, 512], U8, st)
        s_scdb, b_scdb = C.alloc("l0_scdb", [128, 22, 512], BF16, st)
        s_wdt, b_wdt = C.alloc("l0_wdt", [128, 22, 512], BF16, st)
        b_wdbf = [Buf() for _ in range(8)]

        def deq_step(cc, qi):
            nq, oq = QB[qi], QO[qi]
            qb = qi % 2
            tr.dma("q_sync", s_qd8b[qb][:, 0:nq, :],
                   A["qd"][cc, oq*128:(oq+nq)*128, :].rearrange("(ot p) b -> p ot b", p=128),
                   writes=[b_qd8b[qb]])
            tr.dma("q_pool", s_scdb[:, 0:nq, :],
                   A["sd_b"][cc, oq:oq+nq, :].partition_broadcast(128), writes=[b_scdb])
            tr.op("dve", lambda e: e.tensor_single_scalar(s_dnb[:, 0:nq, 0:256], s_qd8b[qb][:, 0:nq, :], 15, ALU.bitwise_and),
                  reads=[b_qd8b[qb]], writes=[b_dnb])
            tr.op("dve", lambda e: e.tensor_single_scalar(s_dnb[:, 0:nq, 256:512], s_qd8b[qb][:, 0:nq, :], 4, ALU.logical_shift_right),
                  reads=[b_qd8b[qb]], writes=[b_dnb])
            wv = s_wdt[:, 0:nq, :].rearrange("p ot (n two) -> p ot n two", two=2)
            sv = s_scdb[:, 0:nq, :].rearrange("p ot (n two) -> p ot n two", two=2)
            for par, bo in ((0, 0), (1, 256)):
                tr.op("dve", lambda e, wv=wv, sv=sv, par=par, bo=bo: e.tensor_tensor(
                        wv[:, :, :, par:par+1],
                        s_dnb[:, 0:nq, bo:bo+256].rearrange("p ot (n o) -> p ot n o", o=1),
                        sv[:, :, :, par:par+1], ALU.mult),
                      reads=[b_dnb, b_scdb], writes=[b_wdt])
            tr.dma("q_act", A["d_wdbf"][cc].rearrange("(ot p) c2 -> p ot c2", p=128)[:, oq:oq+nq, :],
                   s_wdt[:, 0:nq, :], reads=[b_wdt], writes=[b_wdbf[cc]])

        s_wd, _bwdunused = C.alloc("l0_wd", [128, NOI, 512], BF16, st)
        OSPLIT = ((0, 29), (29, 58), (58, 86))
        b_wd3 = [Buf(), Buf(), Buf()]
        s_ht, b_ht = zip(*[C.alloc(f"l0_ht{i}", [128, NOI, 128], BF16, st) for i in range(2)])
        s_gs, b_gs = zip(*[C.alloc(f"l0_gs{i}", [128, 512], BF16, st) for i in range(2)])
        p_d, b_pd = zip(*[C.palloc(f"l0_pd{i}", [128, 512], st) for i in range(2)])

        for qi in range(4):
            deq_step(0, qi)
        for cc in range(8):
            csl = slice(cc*512, (cc+1)*512)
            if cc + 1 < 8:
                for qi in range(4):
                    deq_step(cc + 1, qi)
            wdsrc = A["d_wdbf"][cc].rearrange("(ot p) c2 -> p ot c2", p=128)
            for ci, (o0, o1) in enumerate(OSPLIT):
                tr.dma(("q_sync", "q_pool", "q_act")[ci], s_wd[:, o0:o1, :],
                       wdsrc[:, o0:o1, :], reads=[b_wdbf[cc]], writes=[b_wd3[ci]])
            if DBG and cc == 0:
                dbg_wd2 = nc.dram_tensor("d_wd2dump", [128, NOI, 512], BF16, kind="ExternalOutput").ap()
                tr.dma("q_act", dbg_wd2, s_wd[:], reads=b_wd3)
                dbg_zsd2 = nc.dram_tensor("d_zsd2dump", [86, 4096], BF16, kind="ExternalOutput").ap()
                tr.dma("q_act", dbg_zsd2, s_zsd[:], reads=[b_zsd])
                dbg_sg2 = nc.dram_tensor("d_sg2dump", [86, T], BF16, kind="ExternalOutput").ap()
                tr.dma("q_act", dbg_sg2, s_sg86[:], reads=[b_sg86])
            for tt in range(NT):
                hb = tt % 2
                tr.dma("q_sync" if hb else "q_pool", s_ht[hb][:],
                       A["d_h"][:, :, tt*128:(tt+1)*128].rearrange("ot p t -> p ot t"),
                       writes=[b_ht[hb]])
                pb = tt % 2
                for ci, (o0, o1) in enumerate(OSPLIT):
                    mms = [(lambda e, ot=ot, hb=hb, pb=pb:
                            e.matmul(p_d[pb][:], s_ht[hb][:, ot, :], s_wd[:, ot, :],
                                     start=(ot == 0), stop=False))
                           for ot in range(o0, o1)]
                    rds = [b_ht[hb], b_wd3[ci]]
                    if ci == 2:
                        mms.append(lambda e, pb=pb, tt=tt, csl=csl:
                                   e.matmul(p_d[pb][:], s_sg86[:, tt*128:(tt+1)*128], s_zsd[:, csl],
                                            start=False, stop=True))
                        rds += [b_sg86, b_zsd]
                    tr.mmgroup(mms, reads=rds, writes=[b_pd[pb]])
                tr.op("act", lambda e, pb=pb: e.activation(s_gs[pb][:], p_d[pb][:], AF.Silu),
                      reads=[b_pd[pb]], writes=[b_gs[pb]])
                tr.dma("q_act", A["d_gs0"][tt, :, csl], s_gs[pb][:], reads=[b_gs[pb]])
    tr.barrier()

    s_aT, b_aT = C.alloc("l0_aT", [128, NH, T], BF16, lay)
    with contextlib.ExitStack() as st:
        s_kvb, b_kvb = C.alloc("l0_kvb", [128, 1024], BF16, st)
        s_rk, b_rk = C.alloc("l0_rk", [128, 8], F32, st)
        tr.wait("pool", [b_cc0.w])
        if DBG:
            dbg_cc = nc.dram_tensor("d_ccdump", [129, 1024], F32, kind="ExternalOutput").ap()
            tr.dma("q_sync", dbg_cc, A["d_cc0out"], reads=[b_cc0])
        tr.dma("q_pool", s_kvb[:], A["d_cc0out"][0:128, :], reads=[b_cc0], writes=[b_kvb])
        with C.nc.allow_non_contiguous_dma("tiny ksum transpose load"):
            tr.dma("q_sync", s_rk[:],
                   A["d_cc0out"][128:129, :].rearrange("o (h d) -> d (o h)", h=8),
                   reads=[b_cc0], writes=[b_rk])
        tr.op("dve", lambda e: e.tensor_scalar(s_rk[:], s_rk[:], 1e-10, None, ALU.add),
              reads=[b_rk], writes=[b_rk])
        tr.op("dve", lambda e: e.reciprocal(s_rk[:], s_rk[:]), reads=[b_rk], writes=[b_rk])
        s_ph, b_ph = zip(*[C.alloc(f"l0_ph{i}", [128, T], BF16, st) for i in range(2)])
        p_a, b_pa = zip(*[C.palloc(f"l0_pa{i}", [128, 512], st) for i in range(2)])
        for h in range(NH):
            hb = h % 2
            kvh = h // 4
            tr.dma("q_sync", s_ph[hb][:], A["d_phiq"][h], writes=[b_ph[hb]])
            for tc in range(4):
                pb = tc % 2
                sl = slice(tc*512, (tc+1)*512)
                tr.mmgroup([(lambda e, pb=pb, hb=hb, kvh=kvh, sl=sl:
                             e.matmul(p_a[pb][:], s_kvb[:, kvh*128:(kvh+1)*128], s_ph[hb][:, sl],
                                      start=True, stop=True))],
                           reads=[b_kvb, b_ph[hb]], writes=[b_pa[pb]])
                tr.op("dve", lambda e, pb=pb, h=h, sl=sl, kvh=kvh: e.tensor_scalar(
                        s_aT[:, h, sl], p_a[pb][:], s_rk[:, kvh:kvh+1], None, ALU.mult),
                      reads=[b_pa[pb], b_rk], writes=[b_aT])
        if preconv_l1:
            A2 = C.A
            s_w8c, b_w8c = zip(*[C.alloc(f"l0_w8c{i}", [128, NDT, 128], U8, st) for i in range(2)])
            s_wbc, b_wbc = C.alloc("l0_wbc", [128, NDT, 128], BF16, st)
            for o2t in range(NDT):
                wb8 = o2t % 2
                tr.dma("q_pool", s_w8c[wb8][:], A2["k1r"][o2t].rearrange("dt p o -> p dt o"),
                       writes=[b_w8c[wb8]])
                tr.op("dve", lambda e, wb8=wb8: e.tensor_scalar(s_wbc[:], s_w8c[wb8][:], 2.0, -1.0, ALU.mult, ALU.add),
                      reads=[b_w8c[wb8]], writes=[b_wbc])
                tr.dma("q_sync" if o2t % 2 else "q_act", A2["d_w1bf"][o2t].rearrange("dt p o -> p dt o"), s_wbc[:],
                       reads=[b_wbc])
            for d2t in range(NDT):
                wb8 = d2t % 2
                tr.dma("q_pool", s_w8c[wb8][:].rearrange("p dt o -> p (dt o)"), A2["k2r"][d2t],
                       writes=[b_w8c[wb8]])
                tr.op("dve", lambda e, wb8=wb8: e.tensor_scalar(
                        s_wbc[:].rearrange("p dt o -> p (dt o)"),
                        s_w8c[wb8][:].rearrange("p dt o -> p (dt o)"), 2.0, -1.0, ALU.mult, ALU.add),
                      reads=[b_w8c[wb8]], writes=[b_wbc])
                tr.dma("q_sync" if d2t % 2 else "q_act", A2["d_w2bf"][d2t], s_wbc[:].rearrange("p dt o -> p (dt o)"),
                       reads=[b_wbc])
            C.preconv_done = True
        s_wo, _bwo_unused = C.alloc("l0_wo", [128, NH, 512], BF16, st)
        b_wo2 = [Buf(), Buf()]
        s_bo, b_bo = C.alloc("l0_bo", [128, 512], F32, st)
        s_xc, b_xc = C.alloc("l0_xc", [128, 512], F32, st)
        s_g0, b_g0 = C.alloc("l0_g0", [128, 512], BF16, st)
        s_c1, b_c1 = C.alloc("l0_c1", [128, 512], F32, st)
        s_c2, b_c2 = C.alloc("l0_c2", [128, 512], F32, st)
        s_x1, b_x1 = zip(*[C.alloc(f"l0_x1{i}", [128, 512], F32, st) for i in range(2)])
        p_s, b_ps = zip(*[C.palloc(f"l0_ps{i}", [128, 512], st) for i in range(2)])
        for cc in range(8):
            csl = slice(cc*512, (cc+1)*512)
            wosrc = A["woT"].rearrange("(ot p) c -> p ot c", p=128)
            for ci, (o0, o1) in enumerate(((0, 16), (16, 32))):
                tr.dma("q_pool", s_wo[:, o0:o1, :],
                       wosrc[:, o0:o1, csl], writes=[b_wo2[ci]])
            tr.dma("q_sync", s_bo[:].rearrange("p (o n) -> p o n", o=1),
                   A["bo"][0:1, csl].partition_broadcast(128), writes=[b_bo])
            for tt in range(NT):
                pb = tt % 2
                tsl = slice(tt*128, (tt+1)*128)
                for ci, (o0, o1) in enumerate(((0, 16), (16, 32))):
                    tr.mmgroup([(lambda e, ot=ot, pb=pb, tsl=tsl:
                                 e.matmul(p_s[pb][:], s_aT[:, ot, tsl], s_wo[:, ot, :],
                                          start=(ot == 0), stop=(ot == NH-1)))
                                for ot in range(o0, o1)],
                               reads=[b_aT, b_wo2[ci]], writes=[b_ps[pb]])
                tr.dma("q_sync", s_xc[:], x_ap[tsl, csl], writes=[b_xc])
                tr.dma("q_sync", s_g0[:], A["d_gs0"][tt, :, csl], writes=[b_g0])
                tr.op("dve", lambda e, pb=pb: e.tensor_add(s_c1[:], p_s[pb][:], s_bo[:]),
                      reads=[b_ps[pb], b_bo], writes=[b_c1])
                tr.op("dve", lambda e: e.tensor_mul(s_c2[:], s_c1[:], s_g0[:]),
                      reads=[b_c1, b_g0], writes=[b_c2])
                xb = tt % 2
                tr.op("dve", lambda e, xb=xb: e.tensor_add(s_x1[xb][:], s_c2[:], s_xc[:]),
                      reads=[b_c2, b_xc], writes=[b_x1[xb]])
                tr.dma("q_act", x1_ap[tsl, csl], s_x1[xb][:], reads=[b_x1[xb]])
    tr.barrier()
    lay.close()




def emit_layer1(C, x1_ap, out_ap):
    nc, tr, A = C.nc, C.tr, C.A
    lay = contextlib.ExitStack()
    b_agk, b_agv, b_agq, b_aga = Buf(), Buf(), Buf(), Buf()
    s_rk, b_rk = C.alloc("l1_rk", [1, 1], mybir.dt.uint32, lay)
    tr.dma("q_sync", s_rk[:], A["rk"], writes=[b_rk])
    rkreg = nc.gpsimd.alloc_register("rkreg")
    tr.op("pool", lambda e: e.reg_load(rkreg, s_rk[0:1, 0:1]), reads=[b_rk])
    rv = nc.gpsimd.snap(rkreg, min_val=0, max_val=1)
    s_s, b_s = C.alloc("l1_s", [128, 4], F32, lay)
    s_cv, b_cv = C.alloc("l1_cv", [128, NDT], F32, lay)
    tr.dma("q_sync", s_s[:].rearrange("p (o n) -> p o n", o=1),
           A["s012"][0:1, :].partition_broadcast(128), writes=[b_s])
    tr.dma("q_sync", s_cv[:], A["cvec_c"], writes=[b_cv])

    era = contextlib.ExitStack()
    s_xnT, b_xnT = C.alloc("l1_xnT", [128, NDT, T], BF16, era)
    emit_lnT(C, x1_ap, s_xnT, b_xnT, "l1ln")
    if DBG:
        dbg_xn1 = nc.dram_tensor("d_xn1T", [128, NDT, T], BF16, kind="ExternalOutput").ap()
        tr.dma("q_act", dbg_xn1, s_xnT[:], reads=[b_xnT])

    s_cos, b_cos = C.alloc("l1_cos", [128, T], F32, era)
    s_sin, b_sin = C.alloc("l1_sin", [128, T], F32, era)
    tr.dma("q_sync", s_cos[:], A["cos_c"], writes=[b_cos])
    tr.dma("q_sync", s_sin[:], A["sin_c"], writes=[b_sin])

    def rope_evac(st, p_y, b_py, sl, dst, b_dst, bufs):
        s_r1, b_r1, s_r2, b_r2 = bufs
        tr.op("dve", lambda e: e.tensor_tensor(s_r1[:], p_y[:], s_cos[:, sl], ALU.mult),
              reads=[b_py, b_cos], writes=[b_r1])
        tr.op("dve", lambda e: e.tensor_tensor(s_r2[0:64, :], p_y[64:128, :], s_sin[0:64, sl], ALU.mult),
              reads=[b_py, b_sin], writes=[b_r2])
        tr.op("dve", lambda e: e.tensor_tensor(s_r2[64:128, :], p_y[0:64, :], s_sin[64:128, sl], ALU.mult),
              reads=[b_py, b_sin], writes=[b_r2])
        tr.op("dve", lambda e: e.tensor_add(dst, s_r1[:], s_r2[:]),
              reads=[b_r1, b_r2], writes=[b_dst])

    with contextlib.ExitStack() as st:
        s_w, b_w = zip(*[C.alloc(f"l1kv_w{i}", [128, NDT, 128], BF16, st) for i in range(2)])
        s_r1, b_r1 = C.alloc("l1kv_r1", [128, 512], F32, st)
        s_r2, b_r2 = C.alloc("l1kv_r2", [128, 512], F32, st)
        rbufs = (s_r1, b_r1, s_r2, b_r2)
        s_kT, b_kT = C.alloc("l1_kT", [128, T], BF16, st)
        s_vT, b_vT = C.alloc("l1_vT", [128, T], BF16, st)
        s_vx, b_vx = C.alloc("l1_vx", [128, NT, 128], BF16, st)
        p_y, b_py = zip(*[C.palloc(f"l1kv_p{i}", [128, 512], st) for i in range(2)])
        p_t2, b_pt2 = zip(*[C.palloc(f"l1kv_pt{i}", [128, 128], st, BF16) for i in range(2)])
        for kh in range(NKV):
            for m, wname in ((0, "wk1"), (1, "wv1")):
                wb = m
                tr.dma("q_pool", s_w[wb][:],
                       A[wname].rearrange("(dt p) o -> p dt o", p=128)[:, :, kh*128:(kh+1)*128],
                       writes=[b_w[wb]])
                for tc in range(4):
                    pb = tc % 2
                    sl = slice(tc*512, (tc+1)*512)
                    tr.mmgroup([(lambda e, dt=dt, pb=pb, wb=wb, sl=sl:
                                 e.matmul(p_y[pb][:], s_w[wb][:, dt, :], s_xnT[:, dt, sl],
                                          start=(dt == 0), stop=(dt == NDT-1)))
                                for dt in range(NDT)],
                               reads=[b_w[wb], b_xnT], writes=[b_py[pb]])
                    if m == 0:
                        rope_evac(st, p_y[pb], b_py[pb], sl, s_kT[:, sl], b_kT, rbufs)
                    else:
                        tr.op("dve", lambda e, pb=pb, sl=sl: e.tensor_copy(s_vT[:, sl], p_y[pb][:]),
                              reads=[b_py[pb]], writes=[b_vT])
            tr.dma("q_act", A["d_agk_in"].rearrange("(h p) t -> p h t", p=128)[:, kh, :],
                   s_kT[:], reads=[b_kT])
            for tt in range(NT):
                pb = tt % 2
                tr.op("pe", lambda e, tt=tt, pb=pb: e.transpose(p_t2[pb][:], s_vT[:, tt*128:(tt+1)*128], C.s_ident[:]),
                      reads=[b_vT, C.b_ident], writes=[b_pt2[pb]])
                tr.op("dve", lambda e, tt=tt, pb=pb: e.tensor_copy(s_vx[:, tt, :], p_t2[pb][:]),
                      reads=[b_pt2[pb]], writes=[b_vx])
            tr.dma("q_act", A["d_agv_in"].rearrange("(tt p) v -> p tt v", p=128)[:, :, kh*128:(kh+1)*128],
                   s_vx[:], reads=[b_vx])
        toks = [(sname, c) for sname, c in tr.counts.items() if sname.startswith("q_act")]
        tr.wait("pool", toks)
        tr.collective("AllGather", ALU.bypass, A["d_agk_in"], A["d_agk_out"], b_agk)
        tr.collective("AllGather", ALU.bypass, A["d_agv_in"], A["d_agv_out"], b_agv)
    tr.barrier()

    with contextlib.ExitStack() as st:
        s_w, b_w = zip(*[C.alloc(f"l1q_w{i}", [128, NDT, 128], BF16, st) for i in range(2)])
        s_r1, b_r1 = C.alloc("l1q_r1", [128, 512], F32, st)
        s_r2, b_r2 = C.alloc("l1q_r2", [128, 512], F32, st)
        rbufs = (s_r1, b_r1, s_r2, b_r2)
        s_qt, b_qt = zip(*[C.alloc(f"l1q_t{i}", [128, T], BF16, st) for i in range(2)])
        p_y, b_py = zip(*[C.palloc(f"l1q_p{i}", [128, 512], st) for i in range(2)])
        for ot in range(NDT):
            wb = ot % 2
            tr.dma("q_pool", s_w[wb][:],
                   A["wq1"].rearrange("(dt p) o -> p dt o", p=128)[:, :, ot*128:(ot+1)*128],
                   writes=[b_w[wb]])
            for tc in range(4):
                pb = tc % 2
                sl = slice(tc*512, (tc+1)*512)
                tr.mmgroup([(lambda e, dt=dt, pb=pb, wb=wb, sl=sl:
                             e.matmul(p_y[pb][:], s_w[wb][:, dt, :], s_xnT[:, dt, sl],
                                      start=(dt == 0), stop=(dt == NDT-1)))
                            for dt in range(NDT)],
                           reads=[b_w[wb], b_xnT], writes=[b_py[pb]])
                rope_evac(st, p_y[pb], b_py[pb], sl, s_qt[wb][:, sl], b_qt[wb], rbufs)
            tr.dma("q_act", A["d_agq_in"].rearrange("(h p) t -> p h t", p=128)[:, ot, :],
                   s_qt[wb][:], reads=[b_qt[wb]])
        toks = [(sname, c) for sname, c in tr.counts.items() if sname.startswith("q_act")]
        tr.wait("pool", toks)
        tr.collective("AllGather", ALU.bypass, A["d_agq_in"], A["d_agq_out"], b_agq)
    tr.barrier()

    with contextlib.ExitStack() as st:
        s_w8, b_w8 = zip(*[C.alloc(f"l1w8_{i}", [128, NDT, 128], U8, st) for i in range(2)])
        s_wb_, b_wb_ = zip(*[C.alloc(f"l1wb_{i}", [128, NDT, 128], BF16, st) for i in range(2)])
        for o2t in (() if getattr(C, "preconv_done", False) else range(NDT)):
            wb = o2t % 2
            tr.dma("q_sync", s_w8[wb][:], A["k1r"][o2t].rearrange("dt p o -> p dt o"),
                   writes=[b_w8[wb]])
            tr.op("dve", lambda e, wb=wb: e.tensor_scalar(s_wb_[wb][:], s_w8[wb][:], 2.0, -1.0, ALU.mult, ALU.add),
                  reads=[b_w8[wb]], writes=[b_wb_[wb]])
            tr.dma("q_act", A["d_w1bf"][o2t].rearrange("dt p o -> p dt o"), s_wb_[wb][:],
                   reads=[b_wb_[wb]])
        for d2t in (() if getattr(C, "preconv_done", False) else range(NDT)):
            wb = d2t % 2
            tr.dma("q_sync", s_w8[wb][:].rearrange("p dt o -> p (dt o)"),
                   A["k2r"][d2t], writes=[b_w8[wb]])
            tr.op("dve", lambda e, wb=wb: e.tensor_scalar(
                    s_wb_[wb][:].rearrange("p dt o -> p (dt o)"),
                    s_w8[wb][:].rearrange("p dt o -> p (dt o)"), 2.0, -1.0, ALU.mult, ALU.add),
                  reads=[b_w8[wb]], writes=[b_wb_[wb]])
            tr.dma("q_act", A["d_w2bf"][d2t], s_wb_[wb][:].rearrange("p dt o -> p (dt o)"),
                  reads=[b_wb_[wb]])
        s_t1, b_t1 = C.alloc("l1m1_t1", [128, 512], F32, st)
        s_h1, b_h1 = zip(*[C.alloc(f"l1_h1{i}", [128, T], BF16, st) for i in range(2)])
        p_y, b_py = zip(*[C.palloc(f"l1m1_p{i}", [128, 512], st) for i in range(2)])
        for ot in range(NDT):
            wb = ot % 2
            tr.dma("q_sync", s_w8[wb][:], A["k0r"][ot].rearrange("dt p o -> p dt o"),
                   writes=[b_w8[wb]])
            tr.op("dve", lambda e, wb=wb: e.tensor_scalar(s_wb_[wb][:], s_w8[wb][:], 2.0, -1.0, ALU.mult, ALU.add),
                  reads=[b_w8[wb]], writes=[b_wb_[wb]])
            for tc in range(4):
                pb = tc % 2
                sl = slice(tc*512, (tc+1)*512)
                tr.mmgroup([(lambda e, dt=dt, pb=pb, wb=wb, sl=sl:
                             e.matmul(p_y[pb][:], s_wb_[wb][:, dt, :], s_xnT[:, dt, sl],
                                      start=(dt == 0), stop=(dt == NDT-1)))
                            for dt in range(NDT)],
                           reads=[b_wb_[wb], b_xnT], writes=[b_py[pb]])
                tr.op("act", lambda e, pb=pb: e.activation(s_t1[:], p_y[pb][:], AF.Relu, scale=s_s[:, 0:1]),
                      reads=[b_py[pb], b_s], writes=[b_t1])
                tr.op("dve", lambda e, wb=wb, sl=sl, ot=ot: e.tensor_scalar(s_h1[wb][:, sl], s_t1[:], s_cv[:, ot:ot+1], None, ALU.mult),
                      reads=[b_t1, b_cv], writes=[b_h1[wb]])
            tr.dma("q_act", A["d_h1"][ot], s_h1[wb][:], reads=[b_h1[wb]])
    tr.barrier()
    era.close()

    with contextlib.ExitStack() as st:
        s_h1c, _bh1c_unused = C.alloc("l1_h1c", [128, NDT, 512], BF16, st)
        b_h1c2 = [Buf(), Buf()]
        s_h2c, b_h2c = C.alloc("l1_h2c", [128, NDT, 512], BF16, st)
        s_w1b, b_w1b = zip(*[C.alloc(f"l1w1b{i}", [128, NDT, 128], BF16, st) for i in range(2)])
        s_w2b, b_w2b = zip(*[C.alloc(f"l1w2b{i}", [128, NDT, 512], BF16, st) for i in range(2)])
        s_t1, b_t1 = C.alloc("l1m2_t1", [128, 512], F32, st)
        s_gs1, b_gs1 = zip(*[C.alloc(f"l1_gs1{i}", [128, 512], BF16, st) for i in range(2)])
        p_y, b_py = zip(*[C.palloc(f"l1m2_p{i}", [128, 512], st) for i in range(2)])
        DTSPL = ((0, 16), (16, 32))
        for tc in range(4):
            sl = slice(tc*512, (tc+1)*512)
            h1src = A["d_h1"][:, :, sl].rearrange("ot p t -> p ot t")
            for ci, (d0, d1) in enumerate(DTSPL):
                tr.dma(("q_sync", "q_act")[ci], s_h1c[:, d0:d1, :], h1src[:, d0:d1, :],
                       writes=[b_h1c2[ci]])
            for o2t in range(NDT):
                wb = o2t % 2
                tr.dma("q_pool", s_w1b[wb][:], A["d_w1bf"][o2t].rearrange("dt p o -> p dt o"),
                       writes=[b_w1b[wb]])
                pb = o2t % 2
                for ci, (d0, d1) in enumerate(DTSPL):
                    tr.mmgroup([(lambda e, dt=dt, pb=pb, wb=wb:
                                 e.matmul(p_y[pb][:], s_w1b[wb][:, dt, :], s_h1c[:, dt, :],
                                          start=(dt == 0), stop=(dt == NDT-1)))
                                for dt in range(d0, d1)],
                               reads=[b_w1b[wb], b_h1c2[ci]], writes=[b_py[pb]])
                tr.op("act", lambda e, pb=pb: e.activation(s_t1[:], p_y[pb][:], AF.Relu, scale=s_s[:, 1:2]),
                      reads=[b_py[pb], b_s], writes=[b_t1])
                tr.op("dve", lambda e, o2t=o2t: e.tensor_scalar(s_h2c[:, o2t, :], s_t1[:], s_cv[:, o2t:o2t+1], None, ALU.mult),
                      reads=[b_t1, b_cv], writes=[b_h2c])
            for cc in range(8):
                csl = slice(cc*512, (cc+1)*512)
                w2b = cc % 2
                w2src = A["d_w2bf"][:, :, csl].rearrange("dt p c -> p dt c")
                for ci, (d0, d1) in enumerate(DTSPL):
                    tr.dma(("q_pool", "q_sync")[ci], s_w2b[w2b][:, d0:d1, :],
                           w2src[:, d0:d1, :], writes=[b_w2b[w2b]])
                for t4 in range(4):
                    tt = tc*4 + t4
                    pb = t4 % 2
                    tsl = slice(t4*128, (t4+1)*128)
                    tr.mmgroup([(lambda e, dt=dt, pb=pb, tsl=tsl, w2b=w2b:
                                 e.matmul(p_y[pb][:], s_h2c[:, dt, tsl], s_w2b[w2b][:, dt, :],
                                          start=(dt == 0), stop=(dt == NDT-1)))
                                for dt in range(NDT)],
                               reads=[b_h2c, b_w2b[w2b]], writes=[b_py[pb]])
                    gb = t4 % 2
                    tr.op("act", lambda e, pb=pb, gb=gb: e.activation(s_gs1[gb][:], p_y[pb][:], AF.Silu, scale=s_s[:, 2:3]),
                          reads=[b_py[pb], b_s], writes=[b_gs1[gb]])
                    tr.dma("q_act", A["d_gs1"][tt, :, csl], s_gs1[gb][:], reads=[b_gs1[gb]])
    tr.barrier()

    def gtile(t):
        return t if t < 8 else (t + 8 if t < 24 else t - 16)
    def gchunk(c):
        return c if c < 2 else (c + 2 if c < 6 else c - 4)
    at_stack = contextlib.ExitStack()
    s_aT, b_aT = C.alloc("l1_aT", [128, 16, 4096], BF16, at_stack)
    b_qmy, b_kmy, b_vmy = Buf(), Buf(), Buf()
    tr.wait("pool", [b_agk.w, b_agv.w, b_agq.w])
    rv = nc.gpsimd.snap(rkreg, min_val=0, max_val=1)
    rvT = rv * 2048
    rvTb = rv * 2048 + 4096
    rv5 = rv * 512
    rv5b = rv * 512 + 1024
    tr.dma("q_pool", A["d_qmy"][:, 0:T], A["d_agq_out"][bass.ds(rvT, 2048), :],
           reads=[b_agq], writes=[b_qmy])
    tr.dma("q_pool", A["d_qmy"][:, T:2*T], A["d_agq_out"][bass.ds(rvTb, 2048), :],
           reads=[b_agq], writes=[b_qmy])
    tr.dma("q_pool", A["d_kmy"][:, 0:T], A["d_agk_out"][bass.ds(rv5, 512), :],
           reads=[b_agk], writes=[b_kmy])
    tr.dma("q_pool", A["d_kmy"][:, T:2*T], A["d_agk_out"][bass.ds(rv5b, 512), :],
           reads=[b_agk], writes=[b_kmy])
    tr.dma("q_pool", A["d_vmy"], A["d_agv_out"][:, bass.ds(rv5, 512)],
           reads=[b_agv], writes=[b_vmy])
    b_aga_q = [Buf() for _ in range(4)]
    b_amy = Buf()

    def spill_quarter(kq):
        tr.dma("q_act",
               A[f"d_aga_in{kq}"].rearrange("(h p) t -> p h t", p=128),
               s_aT[:, kq*4:(kq+1)*4, :], reads=[b_aT])
        toks = [(sname, c) for sname, c in tr.counts.items() if sname.startswith("q_act")]
        tr.wait("pool", toks)
        tr.collective("AllGather", ALU.bypass, A[f"d_aga_in{kq}"], A[f"d_aga_out{kq}"], b_aga_q[kq])
        tr.wait("pool", [b_aga_q[kq].w])
        tr.dma("q_pool", A["d_amy"][kq*1024:(kq+1)*1024, :],
               A[f"d_aga_out{kq}"][:, bass.ds(rvT, 2048)],
               reads=[b_aga_q[kq]], writes=[b_amy])
    with contextlib.ExitStack() as st:
        s_kth, b_kth = C.alloc("l1_kth", [128, 4096], BF16, st)
        s_vh, b_vh = C.alloc("l1_vh", [128, 32, 128], BF16, st)
        s_qh, b_qh = zip(*[C.alloc(f"l1_qh{i}", [128, 4096], BF16, st) for i in range(2)])
        s_msk, b_msk = C.alloc("l1_msk", [128, 4, 512], BF16, st)
        tr.dma("q_sync", s_msk[:], A["masks"].rearrange("m p f -> p m f"), writes=[b_msk])
        s_pT, b_pT = zip(*[C.alloc(f"l1_pT{i}", [128, 512], BF16, st) for i in range(3)])
        s_pTm, b_pTm = zip(*[C.alloc(f"l1_pTm{i}", [128, 512], BF16, st) for i in range(3)])
        s_rl, b_rl = C.alloc("l1_rl", [1, 512], F32, st)
        s_rlb, b_rlb = C.alloc("l1_rlb", [1, 512], BF16, st)
        s_rb, b_rb = C.alloc("l1_rb", [128, 512], F32, st)
        p_st, b_pst = zip(*[C.palloc(f"l1_pst{i}", [128, 512], st) for i in range(3)])
        p_pv, b_ppv = zip(*[C.palloc(f"l1_ppv{i}", [128, 512], st) for i in range(2)])
        p_l, b_pl = zip(*[C.palloc(f"l1_pl{i}", [1, 512], st) for i in range(2)])
        p_bc, b_pbc = C.palloc("l1_pbc", [128, 512], st)
        ucnt = [0]
        pend = [None]

        def emit_norm():
            if pend[0] is None:
                return
            hh, gsl, pvb = pend[0]
            pend[0] = None
            tr.op("dve", lambda e: e.reciprocal(s_rl[:], p_l[pvb][:]),
                  reads=[b_pl[pvb]], writes=[b_rl])
            tr.op("dve", lambda e: e.tensor_copy(s_rlb[:], s_rl[:]),
                  reads=[b_rl], writes=[b_rlb])
            tr.mmgroup([(lambda e: e.matmul(p_bc[:], C.s_ones_row[:], s_rlb[:],
                                            start=True, stop=True))],
                       reads=[C.b_ones_row, b_rlb], writes=[b_pbc])
            tr.op("act", lambda e: e.copy(s_rb[:], p_bc[:]), reads=[b_pbc], writes=[b_rb])
            tr.op("dve", lambda e: e.tensor_tensor(s_aT[:, hh, gsl], p_pv[pvb][:], s_rb[:], ALU.mult),
                  reads=[b_ppv[pvb], b_rb], writes=[b_aT])

        for kvl in range(4):
            if kvl >= 1:
                emit_norm()
                spill_quarter(kvl - 1)
            tr.dma("q_sync", s_kth[:], A["d_kmy"][kvl*128:(kvl+1)*128, :],
                   reads=[b_kmy], writes=[b_kth])
            tr.dma("q_sync", s_vh[:],
                   A["d_vmy"].rearrange("(kt p) v -> p kt v", p=128)[:, :, kvl*128:(kvl+1)*128],
                   reads=[b_vmy], writes=[b_vh])
            for r4 in range(4):
                hh = kvl * 4 + r4
                hb = r4 % 2
                tr.dma("q_sync", s_qh[hb][:], A["d_qmy"][hh*128:(hh+1)*128, :],
                       reads=[b_qmy], writes=[b_qh[hb]])
                for qc in range(8):
                    q0 = qc * 512
                    gsl = slice(gchunk(qc)*512, gchunk(qc)*512 + 512)
                    nkt = (q0 + 512) // 128
                    pvb = ucnt[0] % 2
                    prev = [None]

                    def flush_prev(kt, nkt=nkt, pvb=pvb):
                        if prev[0] is None:
                            return
                        src, b_src, pkt = prev[0]
                        prev[0] = None
                        tr.mmgroup([(lambda e: e.matmul(p_pv[pvb][:], s_vh[:, gtile(pkt), :], src[:],
                                                        start=(pkt == 0), stop=(pkt == nkt-1))),
                                    (lambda e: e.matmul(p_l[pvb][:], C.s_ones_col[:], src[:],
                                                        start=(pkt == 0), stop=(pkt == nkt-1)))],
                                   reads=[b_vh, b_src, C.b_ones_col],
                                   writes=[b_ppv[pvb], b_pl[pvb]])

                    for kt in range(nkt):
                        ktg = gtile(kt)
                        sb = kt % 3
                        tr.mmgroup([(lambda e, ktg=ktg, sb=sb, hb=hb, gsl=gsl:
                                     e.matmul(p_st[sb][:], s_kth[:, ktg*128:(ktg+1)*128], s_qh[hb][:, gsl],
                                              start=True, stop=True))],
                                   reads=[b_kth, b_qh[hb]], writes=[b_pst[sb]])
                        tr.op("act", lambda e, sb=sb: e.activation(s_pT[sb][:], p_st[sb][:], AF.Exp, scale=ISCALE),
                              reads=[b_pst[sb]], writes=[b_pT[sb]])
                        src, b_src = s_pT[sb], b_pT[sb]
                        di = kt - q0 // 128
                        if di >= 0:
                            tr.op("dve", lambda e, sb=sb, di=di: e.tensor_tensor(s_pTm[sb][:], s_pT[sb][:], s_msk[:, di, :], ALU.mult),
                                  reads=[b_pT[sb], b_msk], writes=[b_pTm[sb]])
                            src, b_src = s_pTm[sb], b_pTm[sb]
                        if kt == 2:
                            emit_norm()
                        flush_prev(kt)
                        prev[0] = (src, b_src, kt)
                    flush_prev(nkt)
                    if pend[0] is not None:
                        emit_norm()
                    pend[0] = (hh, gsl, pvb)
                    ucnt[0] += 1
        emit_norm()
        spill_quarter(3)
        if DBG:
            dbg_aT = nc.dram_tensor("d_aTdump", [128, 16, 4096], BF16, kind="ExternalOutput").ap()
            tr.dma("q_act", dbg_aT, s_aT[:], reads=[b_aT])
    tr.barrier()
    at_stack.close()

    with contextlib.ExitStack() as st:
        s_aT2, b_aT2 = C.alloc("l1_aT2", [128, NH, T], BF16, st)
        tr.dma("q_sync", s_aT2[:],
               A["d_amy"].rearrange("(h p) t -> p h t", p=128),
               reads=[b_amy], writes=[b_aT2])
        def amyot(h):
            hh, rk_ = h % 16, h // 16
            return (hh // 4) * 8 + rk_ * 4 + (hh % 4)
        s_wo2, b_wo2 = zip(*[C.alloc(f"l1_wo{i}", [128, NH, 512], BF16, st) for i in range(2)])
        s_xc, b_xc = C.alloc("l1_xc", [128, 512], F32, st)
        s_g1, b_g1 = C.alloc("l1_g1", [128, 512], BF16, st)
        s_c2, b_c2 = C.alloc("l1_c2", [128, 512], F32, st)
        s_x2, b_x2 = zip(*[C.alloc(f"l1_x2{i}", [128, 512], F32, st) for i in range(2)])
        p_s, b_ps = zip(*[C.palloc(f"l1_ps{i}", [128, 512], st) for i in range(2)])
        for cc in range(8):
            csl = slice(cc*512, (cc+1)*512)
            wob = cc % 2
            s_wo, b_wo = s_wo2[wob], b_wo2[wob]
            tr.dma("q_pool", s_wo[:],
                   A["wo1"].rearrange("(ot p) c -> p ot c", p=128)[:, :, csl],
                   writes=[b_wo])
            for tt in range(NT):
                pb = tt % 2
                tsl = slice(tt*128, (tt+1)*128)
                tr.mmgroup([(lambda e, ot=ot, pb=pb, tsl=tsl, s_wo=s_wo:
                             e.matmul(p_s[pb][:], s_aT2[:, amyot(ot), tsl], s_wo[:, ot, :],
                                      start=(ot == 0), stop=(ot == NH-1)))
                            for ot in range(NH)],
                           reads=[b_aT2, b_wo], writes=[b_ps[pb]])
                tr.dma("q_sync", s_xc[:], x1_ap[tsl, csl], writes=[b_xc])
                tr.dma("q_sync", s_g1[:], A["d_gs1"][tt, :, csl], writes=[b_g1])
                tr.op("dve", lambda e, pb=pb: e.tensor_mul(s_c2[:], p_s[pb][:], s_g1[:]),
                      reads=[b_ps[pb], b_g1], writes=[b_c2])
                xb = tt % 2
                tr.op("dve", lambda e, xb=xb: e.tensor_add(s_x2[xb][:], s_c2[:], s_xc[:]),
                      reads=[b_c2, b_xc], writes=[b_x2[xb]])
                tr.dma("q_act", out_ap[tsl, csl], s_x2[xb][:], reads=[b_x2[xb]])
    tr.barrier()
    lay.close()


DBG = False

def _scratch(nc, name, shape, dtype):
    kind = "ExternalOutput" if (DBG and name in (
        "d_h", "d_phiq", "d_gs0", "d_xnT", "d_zsg",
        "d_h1", "d_gs1", "d_w1bf", "d_qmy", "d_kmy", "d_vmy", "d_wdbf")) else "Internal"
    return nc.dram_tensor(name, list(shape), dtype, kind=kind).ap()


def _common_dram(nc, A, need_l0, need_l1):
    ext = []
    def inp(name, shape, dtype):
        A[name] = nc.dram_tensor(name, list(shape), dtype, kind="ExternalInput").ap()
        ext.append(name)
    inp("ident", (128, 128), BF16)
    if need_l0:
        inp("x", (T, D), F32)
        inp("wqT", (D, D), F32); inp("wkT", (D, 1024), F32)
        inp("wvT", (D, 1024), F32); inp("woT", (D, D), F32)
        inp("bq_c", (128, 32), F32); inp("bk_c", (128, 8), F32)
        inp("bv_c", (128, 8), F32); inp("bo", (1, D), F32)
        inp("qg", (NOI, D, 64), U8); inp("qu", (NOI, D, 64), U8)
        inp("sge", (NOI, 32, 64), F32); inp("sgo", (NOI, 32, 64), F32)
        inp("sue", (NOI, 32, 64), F32); inp("suo", (NOI, 32, 64), F32)
        inp("sgef", (32, DI // 2), F32); inp("sgof", (32, DI // 2), F32)
        inp("suef", (32, DI // 2), F32); inp("suof", (32, DI // 2), F32)
        inp("zg8", (32, DI // 2), U8); inp("zu8", (32, DI // 2), U8)
        inp("qd", (8, DI, 256), U8)
        inp("sd_b", (8, NOI, 512), F32)
        inp("sdef", (NOI, 2048), F32); inp("sdof", (NOI, 2048), F32)
        inp("zd8", (NOI, 2048), U8)
        inp("stair32", (128, 63), BF16); inp("stair86", (128, 171), BF16)
        A["d_h"] = _scratch(nc, "d_h", (NOI, 128, T), BF16)
        A["d_phiq"] = _scratch(nc, "d_phiq", (NDT, 128, T), BF16)
        A["d_gs0"] = _scratch(nc, "d_gs0", (NT, 128, D), BF16)
        A["d_zsg"] = _scratch(nc, "d_zsg", (32, DI), BF16)
        A["d_zsu"] = _scratch(nc, "d_zsu", (32, DI), BF16)
        A["d_wdbf"] = _scratch(nc, "d_wdbf", (8, DI, 512), BF16)
        A["d_cc0in"] = _scratch(nc, "d_cc0in", (129, 1024), F32)
        A["d_cc0out"] = _scratch(nc, "d_cc0out", (129, 1024), F32)
    if need_l1:
        inp("wq1", (D, D), F32); inp("wk1", (D, 1024), F32)
        inp("wv1", (D, 1024), F32); inp("wo1", (D, D), F32)
        inp("k0r", (NDT, NDT, 128, 128), U8)
        inp("k1r", (NDT, NDT, 128, 128), U8)
        inp("k2r", (NDT, 128, D), U8)
        inp("s012", (1, 4), F32)
        inp("cvec_c", (128, NDT), F32)
        inp("cos_c", (128, T), F32)
        inp("sin_c", (128, T), F32)
        inp("masks", (4, 128, 512), BF16)
        inp("rk", (1, 1), mybir.dt.uint32)
        A["d_agk_in"] = _scratch(nc, "d_agk_in", (1024, T), BF16)
        A["d_agk_out"] = _scratch(nc, "d_agk_out", (2048, T), BF16)
        A["d_agv_in"] = _scratch(nc, "d_agv_in", (T, 1024), BF16)
        A["d_agv_out"] = _scratch(nc, "d_agv_out", (2 * T, 1024), BF16)
        A["d_agq_in"] = _scratch(nc, "d_agq_in", (4096, T), BF16)
        A["d_agq_out"] = _scratch(nc, "d_agq_out", (8192, T), BF16)
        for k in range(4):
            A[f"d_aga_in{k}"] = _scratch(nc, f"d_aga_in{k}", (512, 2 * T), BF16)
            A[f"d_aga_out{k}"] = _scratch(nc, f"d_aga_out{k}", (1024, 2 * T), BF16)
        A["d_amy"] = _scratch(nc, "d_amy", (4096, T), BF16)
        A["d_qmy"] = _scratch(nc, "d_qmy", (2048, 2 * T), BF16)
        A["d_kmy"] = _scratch(nc, "d_kmy", (512, 2 * T), BF16)
        A["d_vmy"] = _scratch(nc, "d_vmy", (2 * T, 512), BF16)
        A["d_h1"] = _scratch(nc, "d_h1", (NDT, 128, T), BF16)
        A["d_w1bf"] = _scratch(nc, "d_w1bf", (NDT, NDT, 128, 128), BF16)
        A["d_w2bf"] = _scratch(nc, "d_w2bf", (NDT, 128, D), BF16)
        A["d_gs1"] = _scratch(nc, "d_gs1", (NT, 128, D), BF16)
    return ext


def build_l0():
    nc = bass.Bass("TRN2", target_bir_lowering=False)
    A = {}
    ext = _common_dram(nc, A, True, False)
    x1 = nc.dram_tensor("x1", [T, D], F32, kind="ExternalOutput").ap()
    C = Ctx(nc, A)
    emit_layer0(C, A["x"], x1)
    C.tr.barrier()
    C.stack.close()
    C.tr.ctx.close()
    return nc, ext


def build_fused():
    nc = bass.Bass("TRN2", target_bir_lowering=False)
    A = {}
    ext = _common_dram(nc, A, True, True)
    A["d_x1"] = _scratch(nc, "d_x1", (T, D), F32)
    out = nc.dram_tensor("xout", [T, D], F32, kind="ExternalOutput").ap()
    C = Ctx(nc, A)
    emit_layer0(C, A["x"], A["d_x1"], preconv_l1=True)
    emit_layer1(C, A["d_x1"], out)
    C.tr.barrier()
    C.stack.close()
    C.tr.ctx.close()
    return nc, ext


def build_l1():
    nc = bass.Bass("TRN2", target_bir_lowering=False)
    A = {}
    ext = _common_dram(nc, A, False, True)
    x1 = nc.dram_tensor("x1in", [T, D], F32, kind="ExternalInput").ap()
    ext.append("x1in")
    out = nc.dram_tensor("xout", [T, D], F32, kind="ExternalOutput").ap()
    C = Ctx(nc, A)
    emit_layer1(C, x1, out)
    C.tr.barrier()
    C.stack.close()
    C.tr.ctx.close()
    return nc, ext



def _host_shared(inputs):
    S = {}
    S["ident"] = np.eye(128, dtype=BF16NP)
    f32 = lambda a: np.ascontiguousarray(np.asarray(a), dtype=np.float32)
    S["wqT"] = f32(np.asarray(inputs["L0_attn_Wq"]).T)
    S["wkT"] = f32(np.asarray(inputs["L0_attn_Wk"]).T)
    S["wvT"] = f32(np.asarray(inputs["L0_attn_Wv"]).T)
    S["woT"] = f32(np.asarray(inputs["L0_attn_Wo"]).T)
    S["bq_c"] = f32(np.asarray(inputs["L0_attn_bq"]).reshape(32, 128).T)
    S["bk_c"] = f32(np.asarray(inputs["L0_attn_bk"]).reshape(8, 128).T)
    S["bv_c"] = f32(np.asarray(inputs["L0_attn_bv"]).reshape(8, 128).T)
    S["bo"] = f32(np.asarray(inputs["L0_attn_bo"]).reshape(1, D))
    for nm, qk, sk, zk in (("g", "L0_ffn_gate_qweight", "L0_ffn_gate_scales", "L0_ffn_gate_qzeros"),
                           ("u", "L0_ffn_up_qweight", "L0_ffn_up_scales", "L0_ffn_up_qzeros")):
        qb = np.ascontiguousarray(np.asarray(inputs[qk])).view(np.uint8)
        S["q" + nm] = np.ascontiguousarray(qb.reshape(D, NOI, 64).transpose(1, 0, 2))
        sc = f32(np.asarray(inputs[sk]))
        s4 = sc.reshape(32, NOI, 64, 2)
        S["s" + nm + "e"] = np.ascontiguousarray(s4[:, :, :, 0].transpose(1, 0, 2))
        S["s" + nm + "o"] = np.ascontiguousarray(s4[:, :, :, 1].transpose(1, 0, 2))
        S["s" + nm + "ef"] = np.ascontiguousarray(sc[:, 0::2])
        S["s" + nm + "of"] = np.ascontiguousarray(sc[:, 1::2])
        S["z" + nm + "8"] = np.ascontiguousarray(np.asarray(inputs[zk])).view(np.uint8)
    db = np.ascontiguousarray(np.asarray(inputs["L0_ffn_down_qweight"])).view(np.uint8)
    S["qd"] = np.ascontiguousarray(db.reshape(DI, 8, 256).transpose(1, 0, 2))
    ds = f32(np.asarray(inputs["L0_ffn_down_scales"]))
    S["sd_b"] = np.ascontiguousarray(ds.reshape(NOI, 8, 512).transpose(1, 0, 2))
    S["sdef"] = np.ascontiguousarray(ds[:, 0::2])
    S["sdof"] = np.ascontiguousarray(ds[:, 1::2])
    S["zd8"] = np.ascontiguousarray(np.asarray(inputs["L0_ffn_down_qzeros"])).view(np.uint8)
    st32 = np.zeros((128, 63), dtype=BF16NP); st32[:, 31] = 1
    S["stair32"] = st32
    st86 = np.zeros((128, 171), dtype=BF16NP); st86[:, 85] = 1
    S["stair86"] = st86
    S["wq1"] = f32(inputs["L1_attn_Wq"])
    S["wk1"] = f32(inputs["L1_attn_Wk"])
    S["wv1"] = f32(inputs["L1_attn_Wv"])
    S["wo1"] = f32(inputs["L1_attn_Wo"])
    for nm, key in (("k0r", "L1_ffn_k0"), ("k1r", "L1_ffn_k1")):
        ku = np.ascontiguousarray(np.asarray(inputs[key])).view(np.uint8)
        S[nm] = np.ascontiguousarray(ku.reshape(NDT, 128, NDT, 128).transpose(2, 0, 1, 3))
    k2u = np.ascontiguousarray(np.asarray(inputs["L1_ffn_k2"])).view(np.uint8)
    S["k2r"] = np.ascontiguousarray(k2u.reshape(NDT, 128, D))
    S["s012"] = np.array([[float(np.asarray(inputs["L1_ffn_s0"])),
                           float(np.asarray(inputs["L1_ffn_s1"])),
                           float(np.asarray(inputs["L1_ffn_s2"])), 0.0]], dtype=np.float32)
    S["cvec_c"] = np.ascontiguousarray(
        np.where(np.arange(D) < 3277, 2.0, -8.0).astype(np.float32).reshape(NDT, 128).T)
    masks = np.zeros((4, 128, 512), dtype=BF16NP)
    for i in range(4):
        pp = np.arange(128)[:, None]
        ff = np.arange(512)[None, :]
        masks[i] = (ff >= pp + i * 128).astype(BF16NP)
    S["masks"] = masks
    inv = (10000.0 ** (-np.arange(64) / 64.0)).astype(np.float64)
    for h in (0, 1):
        glob = np.concatenate([np.arange(s0, s1) for (s0, s1) in _tok_slices(h)])
        ang = glob[None, :] * inv[:, None]
        cos = np.cos(ang).astype(np.float32)
        sin = np.sin(ang).astype(np.float32)
        S[f"cos_c{h}"] = np.ascontiguousarray(np.concatenate([cos, cos], axis=0))
        S[f"sin_c{h}"] = np.ascontiguousarray(np.concatenate([-sin, sin], axis=0))
    return S


def _tok_slices(h):
    if h == 0:
        return [(0, 1024), (3072, 4096)]
    return [(1024, 3072)]


def _host_percore(inputs, core):
    b, h = core // 2, core % 2
    x = np.asarray(inputs["L0_attn_hidden_states"])
    parts = [np.asarray(x[b, s0:s1, :], dtype=np.float32) for (s0, s1) in _tok_slices(h)]
    return {"x": np.ascontiguousarray(np.concatenate(parts, axis=0))}


def kernel(**inputs):
    S = _host_shared(inputs)
    nc, ext = build_fused()
    in_maps = []
    for c in range(8):
        hh = c % 2
        m = {k: S[k] for k in ext if k in S}
        m.update(_host_percore(inputs, c))
        m["cos_c"] = S[f"cos_c{hh}"]
        m["sin_c"] = S[f"sin_c{hh}"]
        m["rk"] = np.array([[hh]], dtype=np.uint32)
        in_maps.append(m)
    res = run_bass_kernel_spmd(nc, in_maps, core_ids=list(range(8)))
    out = np.zeros((4, 4096, 4096), dtype=np.float32)
    for c in range(8):
        b, hh = c // 2, c % 2
        xo = np.asarray(res.results[c]["xout"])
        ofs = 0
        for (s0, s1) in _tok_slices(hh):
            n = s1 - s0
            out[b, s0:s1, :] = xo[ofs:ofs+n]
            ofs += n
    return out
